# version 05
# baseline (speedup 1.0000x reference)
import numpy as np
import concourse.bass as bass
import concourse.mybir as mybir
from concourse.bass_utils import run_bass_kernel_spmd
from contextlib import ExitStack

F32 = mybir.dt.float32
BF16 = mybir.dt.bfloat16
AF = mybir.ActivationFunctionType
ALU = mybir.AluOpType
AXX = mybir.AxisListType.X

D = 2048
FT = 16
H = 16
KD = 128
HQ, HKV, HD = 16, 4, 128
E, I = 8, 1408
IC = 11
C = 64
T_OWN = 1024
T_PRI = 1024
T_ALL = 2048
NT_OWN = 8
NT_ALL = 16
NCH = T_ALL // C
OWNC = T_OWN // C
ROPE_THETA = 1e6
QK_SCALE = float(KD) ** (-0.5)
ATT_SCALE = float(HD) ** (-0.5)

BUILD_PHASES = 6
DEBUG_TAPS = []
GLA_HEADS = H
GLA_CHUNKS = NCH
GLA_OWN_OPS = 5
WO_STAGE = 4

TAP_SPECS = {
    "xn": ([128, FT * T_ALL], BF16),
    "rows": ([1, T_ALL], BF16),
    "statsT": ([128, 2 * NT_OWN], F32),
    "y": ([128, H * T_OWN], BF16),
    "t1a": ([128, NT_OWN * D], BF16),
    "qhat": ([128, HQ * T_OWN], BF16),
    "khat": ([128, HKV * T_ALL], BF16),
    "vffn": ([128, NT_ALL * 512], BF16),
    "attnp": ([128, HQ * T_OWN], BF16),
    "h2": ([128, H * T_OWN], BF16),
    "cvals": ([128, NT_OWN * 8], F32),
    "negM": ([128, 2 * HQ], F32),
    "hfm": ([128, H * T_OWN], BF16),
}


class B:

    def __init__(self, nc, sems, tapout):
        self.nc = nc
        self.sems = sems
        self.tapout = tapout
        self.cur = None
        self.eng = None
        self.count = None

    def begin(self, section, eng):
        self.cur = section
        self.eng = eng
        self.count = {k: 0 for k in self.sems}
        self.dq = {
            "cast": {"sems": [f"cq{i}" for i in range(4)], "n": 4, "next": 0},
            "wld": {"sems": [f"wq{i}" for i in range(6)], "n": 6, "next": 0},
        }
        self.tap_tokens = []

    def val(self, sem):
        return self.count[sem]

    def _fin(self, instr, inc, n):
        if inc is not None:
            self.count[inc] += n
            if instr is not None:
                instr.then_inc(self.sems[inc], n)

    def wait(self, section, sem, v):
        if v <= 0:
            return
        if self.cur == section:
            self.eng.wait_ge(self.sems[sem], v)

    def dma(self, section, out, in_, inc, transpose=False):
        q = self.dq[inc]
        lane = q["next"] % q["n"]
        q["next"] += 1
        sem = q["sems"][lane]
        prev = self.count[sem]
        if prev > 0 and self.cur == section:
            self.eng.wait_ge(self.sems[sem], prev)
        instr = None
        if self.cur == section:
            instr = self.eng.dma_start(out=out, in_=in_, transpose=transpose)
        self.count[sem] += 16
        if instr is not None:
            instr.then_inc(self.sems[sem], 16)
        return (sem, self.count[sem])

    def wait_dma(self, section, tokens):
        if isinstance(tokens, tuple):
            tokens = [tokens]
        best = {}
        for sem, v in tokens:
            best[sem] = max(best.get(sem, 0), v)
        if self.cur == section:
            for sem, v in best.items():
                self.eng.wait_ge(self.sems[sem], v)

    def mm(self, out, lhsT, rhs, start=True, stop=True, inc=None):
        instr = None
        if self.cur == "tensor":
            instr = self.eng.matmul(out, lhsT, rhs, start=start, stop=stop)
        self._fin(instr, inc, 1)

    def tr(self, out, in_, ident, inc=None):
        instr = None
        if self.cur == "tensor":
            instr = self.eng.transpose(out, in_, ident)
        self._fin(instr, inc, 1)

    def act(self, out, in_, func, bias=0.0, scale=1.0, accum_out=None, inc=None):
        instr = None
        if self.cur == "scalar":
            instr = self.eng.activation(out, in_, func, bias=bias, scale=scale,
                                        accum_out=accum_out)
        self._fin(instr, inc, 1)

    def scopy(self, out, in_, inc=None):
        instr = None
        if self.cur == "scalar":
            instr = self.eng.copy(out, in_)
        self._fin(instr, inc, 1)

    def v(self, method, *args, inc=None, **kw):
        instr = None
        if self.cur == "vector":
            instr = getattr(self.eng, method)(*args, **kw)
        self._fin(instr, inc, 1)

    def gp(self, method, *args, inc=None, **kw):
        instr = None
        if self.cur == "gpsimd":
            instr = getattr(self.eng, method)(*args, **kw)
        self._fin(instr, inc, 1)

    def sem_inc(self, section, sem, n=1):
        if self.cur == section:
            self.eng.nop().then_inc(self.sems[sem], n)
        self.count[sem] += n

    def marker(self, section, sem, junk, ones):
        instr = None
        if self.cur == section:
            if section == "vector":
                instr = self.eng.engine_nop()
            elif section == "scalar":
                instr = self.eng.copy(junk[0:1, 0:1], ones[0:1, 0:1])
            elif section == "gpsimd":
                instr = self.eng.memset(junk[0:1, 4:5], 0.0)
        self.count[sem] += 1
        if instr is not None:
            instr.then_inc(self.sems[sem], 1)

    def barrier(self):
        self.wait_dma("sync", list(self.tap_tokens)) if self.tap_tokens else None
        L = self._L
        self.marker("vector", "vec", L["junk"], L["ones"])
        self.marker("scalar", "sc", L["junk"], L["ones"])
        self.marker("gpsimd", "gp", L["junk"], L["ones"])
        self.sem_inc("sync", "sy", 1)
        targets = {"vec": self.count["vec"], "sc": self.count["sc"],
                   "gp": self.count["gp"], "sy": self.count["sy"],
                   "pe": self.count["pe"]}
        own = {"sync": "sy", "gpsimd": "gp", "tensor": "pe",
               "vector": "vec", "scalar": "sc"}
        if self.cur in own:
            for sem, v in targets.items():
                if sem != own[self.cur]:
                    self.eng.wait_ge(self.sems[sem], v)

    def tap(self, name, ap, sem, v):
        if name not in DEBUG_TAPS:
            return
        self.wait("sync", sem, v)
        tok = self.dma("sync", self.tapout[name], ap, inc="wld")
        if not hasattr(self, "tap_tokens"):
            self.tap_tokens = []
        self.tap_tokens.append(tok)


ORG = 16512

MEM = [
    ("xn",     [128, FT * T_ALL], BF16, 0),
    ("big2",   [128, H * T_OWN],  BF16, 0),
    ("hfm",    [128, H * T_OWN],  BF16, 0),
    ("t1r",    [128, NT_OWN * D], BF16, 32768),
    ("yq",     [128, H * T_OWN],  BF16, 65536),
    ("gate8b", [8, D],            BF16, 98304),
    ("gf32",   [128, T_ALL],      F32,  98304),
    ("Af32",   [128, T_ALL],      F32,  106496),
    ("kraw",   [128, T_ALL],      BF16, 114688),
    ("Ktil",   [128, T_ALL],      BF16, 118784),
    ("qraw",   [128, T_OWN],      BF16, 122880),
    ("Qtil",   [128, T_OWN],      BF16, 124928),
    ("Khat",   [128, T_ALL],      BF16, 126976),
    ("t1a",    [128, NT_OWN * D], BF16, 98304),
    ("kv1",    [128, HKV * T_ALL], BF16, 98304),
    ("kv2",    [128, NT_ALL * 512], BF16, 114688),
    ("mb1a",   [128, 16384],      BF16, 98304),
    ("vgla",   [128, NT_ALL * 256], BF16, 131072),
    ("xtile",  [128, D],          F32,  131072),
    ("mb1b",   [128, 4096],       BF16, 131072),
    ("cosb",   [128, T_ALL],      BF16, 139264),
    ("sinb",   [128, T_ALL],      BF16, 143360),
    ("mb1c",   [128, 4096],       BF16, 139264),
    ("wst",    [128, 22528],      BF16, 147456),
    ("scrbf",  [128, 2048],       BF16, 192512),
    ("scrbf2", [128, 1024],       BF16, 196608),
    ("mb1d",   [128, 2048],       BF16, 192512),
    ("mb1e",   [128, 2048],       BF16, 198656),
    ("statsT", [128, 2 * NT_OWN], F32,  198656),
    ("cvals",  [128, NT_OWN * 8], F32,  198720),
    ("gwT",    [128, FT * 8],     BF16, 198976),
    ("r0",     [1, T_ALL],        BF16, 200704),
    ("r1",     [1, T_ALL],        BF16, 204800),
    ("mb1e",   [128, 2048],       BF16, 200704),
    ("Acols",  [128, 6 * NCH],    F32,  208896),
    ("KhatT",  [128, 128],        BF16, 209664),
    ("Qr",     [128, C],          BF16, 209920),
    ("Pt0",    [128, C],          BF16, 210048),
    ("Pt1",    [128, C],          BF16, 210176),
    ("vpad0",  [128, 128],        BF16, 210304),
    ("vpad1",  [128, 128],        BF16, 210560),
    ("Sfp",    [128, 128],        F32,  210816),
    ("denomrow", [1, 512],        F32,  208896),
    ("negM",   [128, 2 * HQ],     F32,  210944),
    ("snorm",  [1, 64],           BF16, 211072),
    ("grow",   [128, 8],          F32,  211200),
    ("gcol",   [128, 16],         F32,  211232),
    ("grow8",  [128, 64],         F32,  208896),
    ("masked8", [128, 64],        F32,  209152),
    ("gsc",    [128, 40],         F32,  209408),
    ("Sbf",    [128, 128],        BF16, 211392),
    ("ones",   [128, 128],        BF16, 211648),
    ("identb", [128, 128],        BF16, 211904),
    ("gammab", [128, FT],         F32,  212160),
    ("betab",  [128, FT],         F32,  212224),
    ("wanorm", [128, FT],         F32,  212288),
    ("wfnorm", [128, FT],         F32,  212352),
    ("wqnc",   [128, 1],          F32,  212416),
    ("wknc",   [128, 1],          F32,  212448),
    ("flagc",  [128, 1],          F32,  212480),
    ("pbiasc", [128, 1],          F32,  212512),
    ("swapm",  [128, 128],        BF16, 212544),
    ("junk",   [128, 8],          F32,  212800),
]
WST_QKG = [0, 6144]
WST_V = [12288, 16384]
WST_ZERO = 20480


def build_nc():
    nc = bass.Bass("TRN2", target_bir_lowering=False, debug=False,
                   detect_race_conditions=False)

    din = {}
    def dinp(name, shape, dtype=F32):
        din[name] = nc.dram_tensor(name, list(shape), dtype, kind="ExternalInput").ap()

    dinp("x_prior", [T_PRI, D]); dinp("x_own", [T_OWN, D])
    dinp("L0_attn_Wq", [D, D]); dinp("L0_attn_Wk", [D, D])
    dinp("L0_attn_Wv", [D, D]); dinp("L0_attn_Wg", [D, D])
    dinp("L0_attn_Wo", [D, D])
    dinp("gamma_t", [128, FT]); dinp("beta_t", [128, FT])
    dinp("w_anorm_t", [128, FT]); dinp("w_fnorm_t", [128, FT])
    dinp("w_qn_t", [128, 1]); dinp("w_kn_t", [128, 1])
    dinp("L0_ffn_q_w", [D, D]); dinp("L0_ffn_k_w", [HKV * HD, D])
    dinp("L0_ffn_v_w", [HKV * HD, D]); dinp("L0_ffn_o_w", [D, HQ * HD])
    dinp("L0_ffn_gate_w", [E, D])
    dinp("L0_ffn_wg", [E, I, D]); dinp("L0_ffn_wu", [E, I, D])
    dinp("L0_ffn_wd", [E, D, I])
    dinp("identf", [128, 128])
    dinp("cosA", [128, T_ALL]); dinp("sinA", [128, T_ALL])
    dinp("swapf", [128, 128])
    dinp("flag_col", [128, 1]); dinp("pbias_col", [128, 1])

    out_own = nc.dram_tensor("out_own", [T_OWN, D], F32, kind="ExternalOutput").ap()

    dint = {}
    def dmk(name, shape):
        dint[name] = nc.dram_tensor(name, list(shape), BF16, kind="Internal").ap()

    dmk("xbf", [T_ALL, D])
    dmk("wq_bf", [D, D]); dmk("wk_bf", [D, D]); dmk("wv_bf", [D, D])
    dmk("wg_bf", [D, D]); dmk("wo_bf", [D, D])
    dmk("fq_bf", [D, D]); dmk("fk_bf", [HKV * HD, D]); dmk("fv_bf", [HKV * HD, D])
    dmk("fo_bf", [D, HQ * HD])
    dmk("wgm_bf", [E, I, D]); dmk("wum_bf", [E, I, D]); dmk("wdm_bf", [E, D, I])
    dmk("t1_sp", [T_OWN, D])

    ctx = ExitStack()
    with ctx:
        sb = {}
        for name, shape, dtype, off in MEM:
            sb[name] = nc.alloc_sbuf_tensor_at(name, list(shape), dtype,
                                               offset=ORG + off)

        def psum(name, shape, dtype=F32):
            sb[name] = ctx.enter_context(nc.psum_tensor(name, list(shape), dtype))
        psum("W4", [128, 2048])
        psum("P0", [128, 512])
        psum("P1", [128, 512])
        psum("TB0", [128, 512], BF16)
        psum("TB1", [128, 512], BF16)

        cast_list = [
            ("ident", sb["identb"][:], din["identf"]),
            ("swap", sb["swapm"][:], din["swapf"]),
            ("gate", sb["gate8b"][:], din["L0_ffn_gate_w"]),
            ("cos", sb["cosb"][:], din["cosA"]),
            ("sin", sb["sinb"][:], din["sinA"]),
            ("x0", dint["xbf"][0:T_PRI, :], din["x_prior"]),
            ("x1", dint["xbf"][T_PRI:T_ALL, :], din["x_own"]),
            ("wq", dint["wq_bf"], din["L0_attn_Wq"]),
            ("wk", dint["wk_bf"], din["L0_attn_Wk"]),
            ("wg", dint["wg_bf"], din["L0_attn_Wg"]),
            ("wv", dint["wv_bf"], din["L0_attn_Wv"]),
            ("wo", dint["wo_bf"], din["L0_attn_Wo"]),
            ("fq", dint["fq_bf"], din["L0_ffn_q_w"]),
            ("fk", dint["fk_bf"], din["L0_ffn_k_w"]),
            ("fv", dint["fv_bf"], din["L0_ffn_v_w"]),
            ("fo", dint["fo_bf"], din["L0_ffn_o_w"]),
        ]
        for e in range(E):
            cast_list.append((f"wgm{e}", dint["wgm_bf"][e], din["L0_ffn_wg"][e]))
            cast_list.append((f"wum{e}", dint["wum_bf"][e], din["L0_ffn_wu"][e]))
            cast_list.append((f"wdm{e}", dint["wdm_bf"][e], din["L0_ffn_wd"][e]))
        cast_idx = {name: i for i, (name, _, _) in enumerate(cast_list)}

        sems = {}
        for s in ("pe", "vec", "sc", "gp", "sy",
                  "cq0", "cq1", "cq2", "cq3",
                  "wq0", "wq1", "wq2", "wq3", "wq4", "wq5"):
            sems[s] = ctx.enter_context(nc.semaphore(s))

        tapout = {}
        for name in DEBUG_TAPS:
            shape, dt = TAP_SPECS[name]
            tapout[name] = nc.dram_tensor("tap_" + name, list(shape), dt,
                                          kind="ExternalOutput").ap()

        b = B(nc, sems, tapout)
        b._L = sb
        L0 = dict(sb)
        L0.update(din=din, dint=dint, out_own=out_own,
                  cast_list=cast_list, cast_idx=cast_idx)

        counts = []

        def run_section(name, eng):
            b.begin(name, eng)
            with nc.allow_low_precision(reason="bf16 operand storage by design"):
                program(b, dict(L0))
            counts.append(dict(b.count))
            if len(counts) > 1:
                assert counts[-1] == counts[0], \
                    f"walk divergence in section {name}"

        with nc.Block() as block:
            @block.sync
            def _(eng):
                run_section("sync", eng)

            @block.gpsimd
            def _(eng):
                run_section("gpsimd", eng)

            @block.tensor
            def _(eng):
                run_section("tensor", eng)

            @block.vector
            def _(eng):
                run_section("vector", eng)

            @block.scalar
            def _(eng):
                run_section("scalar", eng)

    return nc


def ROW(L, r, n=T_ALL):
    return L[("r0", "r1")[r]][0:1, 0:n]



def program(b, L):
    din = L["din"]

    cast_tok = {}
    for name, out_ap, in_ap in L["cast_list"]:
        cast_tok[name] = b.dma("gpsimd", out_ap, in_ap, inc="cast")
    L["cd"] = lambda key: cast_tok[key]

    consts_ld = []
    for nm, key in (("gammab", "gamma_t"), ("betab", "beta_t"),
                    ("wanorm", "w_anorm_t"), ("wfnorm", "w_fnorm_t"),
                    ("wqnc", "w_qn_t"), ("wknc", "w_kn_t"),
                    ("flagc", "flag_col"), ("pbiasc", "pbias_col")):
        consts_ld.append(b.dma("sync", L[nm][:], din[key], inc="wld"))

    b.wait_dma("vector", consts_ld)
    b.v("memset", L["ones"][:], 1.0)
    b.v("memset", L["wst"][:, WST_ZERO:WST_ZERO + 2048], 0.0, inc="vec")
    L["consts_v"] = b.val("vec")

    phase1(b, L)
    b.barrier()
    if BUILD_PHASES >= 2:
        phase_gla(b, L)
        b.barrier()
    if BUILD_PHASES >= 3:
        phase_wo(b, L)
        b.barrier()
    if BUILD_PHASES >= 4:
        phase_ffnproj(b, L)
        b.barrier()
        phase_attn(b, L)
        b.barrier()
    if BUILD_PHASES >= 5:
        phase_hidden(b, L)
        b.barrier()
        gate(b, L)
        b.barrier()
    if BUILD_PHASES >= 6:
        phase_moe(b, L)
    finish(b, L)


def ln_stats_fm(b, L, src_getter, tlen, with_mean, eps):
    P0, P1 = L["P0"], L["P1"]
    sq_pe = [0, 0]
    for tb in range(tlen // 512):
        t0 = tb * 512
        for ft in range(FT):
            xt = src_getter(ft, t0)
            scr = [L["scrbf"], L["scrbf2"]][ft % 2]
            b.wait("scalar", "pe", sq_pe[-2])
            b.act(scr[:, 0:512], xt, AF.Square, inc="sc")
            b.wait("tensor", "sc", b.val("sc"))
            if with_mean:
                b.mm(P0[0:1, :], L["ones"][:, 0:1], xt,
                     start=(ft == 0), stop=(ft == FT - 1))
            b.mm(P1[0:1, :], L["ones"][:, 0:1], scr[:, 0:512],
                 start=(ft == 0), stop=(ft == FT - 1), inc="pe")
            sq_pe.append(b.val("pe"))
        b.wait("vector", "pe", b.val("pe"))
        r0s = L["r0"][0:1, t0:t0 + 512]
        r1s = L["r1"][0:1, t0:t0 + 512]
        if with_mean:
            b.v("tensor_scalar", r0s, P0[0:1, :], 1.0 / D, None, ALU.mult)
        b.v("tensor_scalar", r1s, P1[0:1, :], 1.0 / D, None, ALU.mult)
        if with_mean:
            b.v("tensor_mul", P1[0:1, :], r0s, r0s)
            b.v("tensor_sub", r1s, r1s, P1[0:1, :])
        b.v("tensor_scalar", r1s, r1s, eps, None, ALU.add)
        b.v("reciprocal", r1s, r1s, inc="vec")
        b.wait("tensor", "vec", b.val("vec"))
    b.wait("scalar", "vec", b.val("vec"))
    b.act(ROW(L, 1, tlen), ROW(L, 1, tlen), AF.Sqrt, inc="sc")
    b.wait("tensor", "sc", b.val("sc"))
    b.wait("vector", "sc", b.val("sc"))


def bcast_mm(b, L, psum_slice, row_ap):
    b.mm(psum_slice, L["ones"][0:1, :], row_ap, inc="pe")


def phase1(b, L):
    xn, W4, P0, P1 = L["xn"], L["W4"], L["P0"], L["P1"]
    xbf = L["dint"]["xbf"]

    b.wait_dma("tensor", [L["cd"]("ident"), L["cd"]("gate")])
    gtr = [0, 0]
    for dc in range(FT):
        TB = [L["TB0"], L["TB1"]][dc % 2]
        b.wait("tensor", "sc", gtr[-2])
        b.tr(TB[0:128, 0:8], L["gate8b"][0:8, dc * 128:(dc + 1) * 128],
             L["identb"][0:8, 0:8], inc="pe")
        b.wait("scalar", "pe", b.val("pe"))
        b.scopy(L["gwT"][:, dc * 8:(dc + 1) * 8], TB[0:128, 0:8], inc="sc")
        gtr.append(b.val("sc"))

    b.wait_dma("sync", [L["cd"]("x0"), L["cd"]("x1")])
    xld = []
    for ft in range(FT):
        xld.append(b.dma("sync", xn[:, ft * T_ALL:(ft + 1) * T_ALL],
                         xbf[:, ft * 128:(ft + 1) * 128], inc="wld",
                         transpose=True))
    b.wait_dma("scalar", xld)
    b.wait_dma("tensor", xld)
    b.wait("tensor", "vec", L["consts_v"])
    ln_stats_fm(b, L, lambda ft, t0: xn[:, ft * T_ALL + t0: ft * T_ALL + t0 + 512],
                T_ALL, True, 1e-5)

    nrm_v = [0, 0]
    for tb in range(4):
        t0 = tb * 512
        mb = W4[:, (tb % 2) * 1024:(tb % 2) * 1024 + 512]
        rb = W4[:, (tb % 2) * 1024 + 512:(tb % 2) * 1024 + 1024]
        b.wait("tensor", "vec", nrm_v[-2])
        bcast_mm(b, L, mb, L["r0"][0:1, t0:t0 + 512])
        bcast_mm(b, L, rb, L["r1"][0:1, t0:t0 + 512])
        b.wait("vector", "pe", b.val("pe"))
        for ft in range(FT):
            xt = xn[:, ft * T_ALL + t0: ft * T_ALL + t0 + 512]
            b.v("tensor_sub", xt, xt, mb)
            b.v("tensor_mul", xt, xt, rb, inc="vec")
        nrm_v.append(b.val("vec"))
    L["xn_v"] = b.val("vec")

    b.wait_dma("tensor", L["cd"]("ident"))
    b.wait("tensor", "vec", L["xn_v"])
    prev = [0, 0]
    for tt in range(NT_OWN):
        t0 = T_PRI + tt * 128
        TB = [L["TB0"], L["TB1"]][tt % 2]
        b.wait("tensor", "sc", prev[-2])
        b.tr(TB[0:128, 0:1], L["r0"][0:1, t0:t0 + 128],
             L["identb"][0:1, 0:1], inc="pe")
        b.tr(TB[0:128, 2:3], L["r1"][0:1, t0:t0 + 128],
             L["identb"][0:1, 0:1], inc="pe")
        b.wait("scalar", "pe", b.val("pe"))
        b.scopy(L["statsT"][:, 2 * tt:2 * tt + 1], TB[0:128, 0:1])
        b.scopy(L["statsT"][:, 2 * tt + 1:2 * tt + 2], TB[0:128, 2:3], inc="sc")
        prev.append(b.val("sc"))
    L["statsT_sc"] = b.val("sc")

    b.tap("xn", xn[:], "vec", L["xn_v"])
    b.tap("rows", L["r0"][:], "vec", L["xn_v"])
    b.tap("statsT", L["statsT"][:], "sc", L["statsT_sc"])


def stream_wT(b, L, dram_bf, ckey, f0, f1, col0):
    b.wait_dma("sync", L["cd"](ckey))
    w = f1 - f0
    toks = []
    for dc in range(FT):
        toks.append(b.dma("sync", L["wst"][:, col0 + dc * w: col0 + (dc + 1) * w],
                          dram_bf[f0:f1, dc * 128:(dc + 1) * 128], inc="wld",
                          transpose=True))
    return toks


def phase_gla(b, L):
    xn, Ac, wst = L["xn"], L["Acols"], L["wst"]
    W4, P0, P1 = L["W4"], L["P0"], L["P1"]
    dint = L["dint"]

    b.wait("tensor", "vec", L["xn_v"])
    b.wait_dma("tensor", L["cd"]("ident"))
    b.v("memset", L["vpad0"][:], 0.0)
    b.v("memset", L["vpad1"][:], 0.0)
    b.v("memset", L["Pt0"][:], 0.0)
    b.v("memset", L["Pt1"][:], 0.0, inc="vec")

    trD, evD, dsD, chD = {}, {}, {}, {}
    ptS, ptG, ymD, yeS = [0, 0], [0, 0], [0, 0], [0, 0]
    for h in range(GLA_HEADS):
        f0, f1 = h * 128, (h + 1) * 128
        par = h % 2
        hh2 = h % 2
        qkg0 = WST_QKG[par]

        if h >= 2:
            b.wait("sync", "pe", L[f"gla_wused_{h - 2}"])
        ldq = stream_wT(b, L, dint["wq_bf"], "wq", f0, f1, qkg0 + 0)
        ldk = stream_wT(b, L, dint["wk_bf"], "wk", f0, f1, qkg0 + 2048)
        ldg = stream_wT(b, L, dint["wg_bf"], "wg", f0, f1, qkg0 + 4096)

        if h % 2 == 0:
            grp = h // 2
            v0 = WST_V[par]
            if h >= 2:
                b.wait("sync", "pe", L[f"gla_vused_{h - 2}"])
            ldv = stream_wT(b, L, dint["wv_bf"], "wv", grp * 256, (grp + 1) * 256,
                            v0)
            b.wait_dma("tensor", ldv)
            if h > 0:
                b.wait("tensor", "vec", L[f"gla_head_done_{h - 1}"])
                b.wait("tensor", "sc", L[f"gla_ysc_{h - 1}"])
            vev = [0, 0]
            for tt in range(NT_ALL):
                pcol = (tt % 2) * 1024
                b.wait("tensor", "vec", vev[-2])
                for dc in range(FT):
                    b.mm(W4[:, pcol:pcol + 256],
                         xn[:, dc * T_ALL + tt * 128: dc * T_ALL + (tt + 1) * 128],
                         wst[:, v0 + dc * 256: v0 + (dc + 1) * 256],
                         start=(dc == 0), stop=(dc == FT - 1),
                         inc=("pe" if dc == FT - 1 else None))
                b.wait("vector", "pe", b.val("pe"))
                b.v("tensor_copy", L["vgla"][:, tt * 256:(tt + 1) * 256],
                    W4[:, pcol:pcol + 256], inc="vec")
                vev.append(b.val("vec"))
            L[f"gla_vused_{h}"] = b.val("pe")
            L[f"gla_vused_{h + 1}"] = b.val("pe")
            L["gla_vgrp_v"] = b.val("vec")

        b.wait_dma("tensor", ldq)
        if h >= 1:
            b.wait("tensor", "vec", L[f"gla_head_done_{h - 1}"])
            b.wait("scalar", "vec", L[f"gla_prep_v_{h - 1}"])
        if h % 2 == 0:
            b.wait("tensor", "vec", L["gla_vgrp_v"])
        for half in range(2):
            ps = [P0, P1][half]
            for dc in range(FT):
                b.mm(ps[:], wst[:, qkg0 + dc * 128: qkg0 + (dc + 1) * 128],
                     xn[:, dc * T_ALL + T_PRI + half * 512:
                        dc * T_ALL + T_PRI + (half + 1) * 512],
                     start=(dc == 0), stop=(dc == FT - 1),
                     inc=("pe" if dc == FT - 1 and half == 1 else None))
        b.wait("scalar", "pe", b.val("pe"))
        b.act(L["qraw"][:, 0:512], P0[:], AF.Copy, scale=QK_SCALE)
        b.act(L["qraw"][:, 512:1024], P1[:], AF.Copy, scale=QK_SCALE, inc="sc")
        qevac = b.val("sc")
        b.wait_dma("tensor", ldk)
        kevacs = []
        for half in range(4):
            ps = [P0, P1][half % 2]
            b.wait("tensor", "sc", qevac if half < 2 else kevacs[half - 2])
            for dc in range(FT):
                b.mm(ps[:], wst[:, qkg0 + 2048 + dc * 128: qkg0 + 2048 + (dc + 1) * 128],
                     xn[:, dc * T_ALL + half * 512: dc * T_ALL + (half + 1) * 512],
                     start=(dc == 0), stop=(dc == FT - 1),
                     inc=("pe" if dc == FT - 1 else None))
            b.wait("scalar", "pe", b.val("pe"))
            b.act(L["kraw"][:, half * 512:(half + 1) * 512], ps[:], AF.Copy, inc="sc")
            kevacs.append(b.val("sc"))
        b.wait_dma("tensor", ldg)
        gevacs = []
        for half in range(4):
            ps = [P0, P1][half % 2]
            b.wait("tensor", "sc", kevacs[2 + half % 2] if half < 2 else gevacs[half - 2])
            for dc in range(FT):
                b.mm(ps[:], wst[:, qkg0 + 4096 + dc * 128: qkg0 + 4096 + (dc + 1) * 128],
                     xn[:, dc * T_ALL + half * 512: dc * T_ALL + (half + 1) * 512],
                     start=(dc == 0), stop=(dc == FT - 1),
                     inc=("pe" if dc == FT - 1 else None))
            b.wait("scalar", "pe", b.val("pe"))
            b.act(L["gf32"][:, half * 512:(half + 1) * 512], ps[:], AF.Sigmoid, inc="sc")
            gevacs.append(b.val("sc"))
        b.act(L["gf32"][:], L["gf32"][:], AF.Ln, inc="sc")
        lsig_sc = b.val("sc")
        L[f"gla_wused_{h}"] = b.val("pe")

        b.wait("vector", "sc", lsig_sc)
        b.v("tensor_tensor_scan", L["Af32"][:], L["gf32"][:],
            wst[:, WST_ZERO:WST_ZERO + 2048], 0.0, ALU.add, ALU.add, inc="vec")
        for c in range(NCH):
            b.v("tensor_copy", Ac[:, 0 * NCH + c:0 * NCH + c + 1],
                L["Af32"][:, c * C + C // 2 - 1:c * C + C // 2])
            b.v("tensor_copy", Ac[:, 1 * NCH + c:1 * NCH + c + 1],
                L["Af32"][:, c * C + C - 1:c * C + C])
        b.v("memset", Ac[:, 2 * NCH:2 * NCH + 1], 0.0)
        b.v("tensor_copy", Ac[:, 2 * NCH + 1:3 * NCH], Ac[:, 1 * NCH:2 * NCH - 1])
        b.v("tensor_sub", Ac[:, 3 * NCH:4 * NCH], Ac[:, 1 * NCH:2 * NCH],
            Ac[:, 2 * NCH:3 * NCH])
        b.v("tensor_sub", Ac[:, 4 * NCH:5 * NCH], Ac[:, 1 * NCH:2 * NCH], Ac[:, 0:NCH])
        b.v("tensor_sub", Ac[:, 5 * NCH:6 * NCH], Ac[:, 0:NCH], Ac[:, 2 * NCH:3 * NCH])
        b.v("tensor_scalar", Ac[:, 2 * NCH:3 * NCH], Ac[:, 0:NCH], -1.0, None,
            ALU.mult, inc="vec")
        cols_v = b.val("vec")
        b.wait("scalar", "vec", cols_v)
        b.act(Ac[:, 3 * NCH:4 * NCH], Ac[:, 3 * NCH:4 * NCH], AF.Exp)
        b.act(Ac[:, 4 * NCH:5 * NCH], Ac[:, 4 * NCH:5 * NCH], AF.Exp)
        b.act(Ac[:, 5 * NCH:6 * NCH], Ac[:, 5 * NCH:6 * NCH], AF.Exp, inc="sc")
        for c in range(NCH):
            b.act(L["scrbf"][:, c * C:(c + 1) * C], L["Af32"][:, c * C:(c + 1) * C],
                  AF.Exp, scale=-1.0, bias=Ac[:, 0 * NCH + c:0 * NCH + c + 1],
                  inc=("sc" if c == NCH - 1 else None))
        ek_sc = b.val("sc")
        for co in range(OWNC):
            c = OWNC + co
            b.act(L["scrbf2"][:, co * C:(co + 1) * C], L["Af32"][:, c * C:(c + 1) * C],
                  AF.Exp, scale=1.0, bias=Ac[:, 2 * NCH + c:2 * NCH + c + 1],
                  inc=("sc" if co == OWNC - 1 else None))
        eq_sc = b.val("sc")
        b.wait("vector", "sc", ek_sc)
        b.v("tensor_mul", L["Ktil"][:], L["kraw"][:], L["scrbf"][:])
        b.wait("vector", "sc", eq_sc)
        b.v("tensor_mul", L["Qtil"][:], L["qraw"][:], L["scrbf2"][:, 0:T_OWN])
        for c in range(NCH):
            b.v("tensor_scalar", L["Khat"][:, c * C:(c + 1) * C],
                L["Ktil"][:, c * C:(c + 1) * C],
                Ac[:, 4 * NCH + c:4 * NCH + c + 1], None, ALU.mult)
        b.v("memset", L["Sfp"][:], 0.0, inc="vec")
        prep_v = b.val("vec")
        L[f"gla_prep_v_{h}"] = prep_v

        b.wait("tensor", "vec", prep_v)
        b.wait("gpsimd", "vec", prep_v)
        for c in range(GLA_CHUNKS):
            cc = h * NCH + c
            sl = cc % 2
            bp = (c % 2) * 64
            tt = c // 2
            TB = [L["TB0"], L["TB1"]][sl]
            PD = [P0, P1][sl]
            vsl = L["vgla"][bp:bp + 64, tt * 256 + hh2 * 128: tt * 256 + (hh2 + 1) * 128]
            b.wait("tensor", "sc", evD.get(cc - 2, 0))
            b.tr(TB[bp:bp + 64, 0:128], L["Khat"][:, c * C:(c + 1) * C],
                 L["identb"][:], inc="pe")
            trD[cc] = b.val("pe")
            b.wait("scalar", "pe", trD[cc])
            b.wait("scalar", "pe", dsD.get(cc - 2, 0))
            b.scopy(L["KhatT"][bp:bp + 64, 0:128], TB[bp:bp + 64, 0:128], inc="sc")
            evD[cc] = b.val("sc")
            b.wait("tensor", "sc", evD[cc])
            b.wait("tensor", "vec", chD.get(cc - 2, 0))
            b.mm(PD[0:128, 0:128], L["KhatT"][bp:bp + 64, 0:128], vsl, inc="pe")
            dsD[cc] = b.val("pe")
            if c >= OWNC:
                co = c - OWNC
                Ptp = [L["Pt0"], L["Pt1"]][c % 2]
                vpad = [L["vpad0"], L["vpad1"]][c % 2]
                b.wait("vector", "pe", ymD[-1])
                b.v("tensor_scalar", L["Qr"][:, 0:C],
                    L["Qtil"][:, co * C:(co + 1) * C],
                    Ac[:, 5 * NCH + c:5 * NCH + c + 1], None, ALU.mult)
                b.v("tensor_copy", L["Sbf"][:], L["Sfp"][:])
                b.v("tensor_copy", vpad[bp:bp + 64, :], vsl, inc="vec")
                qr_v = b.val("vec")
                if GLA_OWN_OPS >= 2:
                    b.wait("tensor", "vec", qr_v)
                    b.wait("tensor", "sc", ptS[-1])
                    b.mm(W4[bp:bp + 64, 1024:1088], L["Ktil"][:, c * C:(c + 1) * C],
                         L["Qtil"][:, co * C:(co + 1) * C], inc="pe")
                    b.wait("scalar", "pe", b.val("pe"))
                    b.wait("scalar", "gp", ptG[-2])
                    b.scopy(Ptp[bp:bp + 64, 0:C], W4[bp:bp + 64, 1024:1088],
                            inc="sc")
                    ptS.append(b.val("sc"))
                if GLA_OWN_OPS >= 3:
                    b.wait("gpsimd", "sc", ptS[-1])
                    b.gp("affine_select", Ptp[bp:bp + 64, 0:C],
                         Ptp[bp:bp + 64, 0:C], [[1, C]], ALU.is_ge, 0.0,
                         base=0, channel_multiplier=-1, inc="gp")
                    ptG.append(b.val("gp"))
                if GLA_OWN_OPS >= 4:
                    b.wait("tensor", "gp", ptG[-1])
                    b.wait("tensor", "sc", yeS[-1])
                    b.mm(W4[0:128, 1536:1600], L["Sbf"][:], L["Qr"][:, 0:C],
                         start=True, stop=False)
                    b.mm(W4[0:128, 1536:1600], vpad[:], Ptp[:, 0:C],
                         start=False, stop=True, inc="pe")
                    ymD.append(b.val("pe"))
                if GLA_OWN_OPS >= 5:
                    b.wait("scalar", "pe", ymD[-1])
                    b.scopy(L["yq"][:, h * T_OWN + co * C: h * T_OWN + (co + 1) * C],
                            W4[0:128, 1536:1600], inc="sc")
                    yeS.append(b.val("sc"))
            b.wait("vector", "pe", dsD[cc])
            if c == OWNC:
                b.v("tensor_scalar", L["Sfp"][:], L["Sfp"][:], L["flagc"][:, 0:1],
                    None, ALU.mult)
            b.v("scalar_tensor_tensor", L["Sfp"][:], L["Sfp"][:],
                Ac[:, 3 * NCH + c:3 * NCH + c + 1], PD[0:128, 0:128],
                ALU.mult, ALU.add, inc="vec")
            chD[cc] = b.val("vec")
        L[f"gla_head_done_{h}"] = b.val("vec")
        L[f"gla_ysc_{h}"] = b.val("sc")
    L["y_sc"] = b.val("sc")
    b.tap("y", L["yq"][:], "sc", L["y_sc"])


def phase_wo(b, L):
    yq, wst, W4 = L["yq"], L["wst"], L["W4"]
    dint = L["dint"]

    b.wait("scalar", "sc", L["y_sc"])
    b.wait("tensor", "sc", L["y_sc"])
    ln_stats_fm(b, L, lambda ft, t0: yq[:, ft * T_OWN + t0: ft * T_OWN + t0 + 512],
                T_OWN, True, 1e-5)
    if WO_STAGE < 2:
        L["t1sp_ld"] = []
        for tt in range(NT_OWN):
            L[f"wo_grp_pe_{tt}"] = 0
        L["t1_gla_v"] = b.val("vec")
        return
    nrm_v = [0, 0]
    for tb in range(2):
        t0 = tb * 512
        mb = W4[:, (tb % 2) * 1024:(tb % 2) * 1024 + 512]
        rb = W4[:, (tb % 2) * 1024 + 512:(tb % 2) * 1024 + 1024]
        b.wait("tensor", "vec", nrm_v[-2])
        bcast_mm(b, L, mb, L["r0"][0:1, t0:t0 + 512])
        bcast_mm(b, L, rb, L["r1"][0:1, t0:t0 + 512])
        b.wait("vector", "pe", b.val("pe"))
        for hh in range(H):
            yt = yq[:, hh * T_OWN + t0: hh * T_OWN + t0 + 512]
            b.v("tensor_sub", yt, yt, mb)
            b.v("tensor_mul", yt, yt, rb)
            b.v("tensor_scalar", yt, yt, L["gammab"][:, hh:hh + 1],
                L["betab"][:, hh:hh + 1], ALU.mult, ALU.add, inc="vec")
        nrm_v.append(b.val("vec"))
    yn_v = b.val("vec")
    if WO_STAGE < 3:
        L["t1sp_ld"] = []
        L["t1_gla_v"] = b.val("vec")
        return

    xl = []
    b.wait_dma("sync", L["cd"]("x1"))
    for tt in range(NT_OWN):
        xl.append(b.dma("sync", L["t1a"][:, tt * D:(tt + 1) * D],
                        dint["xbf"][T_PRI + tt * 128: T_PRI + (tt + 1) * 128, :],
                        inc="wld"))
    acc_v = [0, 0, 0, 0]
    for grp in range(4):
        par = grp % 2
        if grp >= 2:
            b.wait("sync", "pe", L[f"wo_grp_pe_{grp - 2}"])
        ld = stream_wT(b, L, dint["wo_bf"], "wo", grp * 512, (grp + 1) * 512,
                       par * 8192)
        b.wait_dma("tensor", ld)
        b.wait("tensor", "vec", yn_v)
        for tt in range(NT_OWN):
            bk = tt % 4
            b.wait("tensor", "vec", acc_v[-4])
            if grp == 0:
                b.wait_dma("vector", xl[tt])
            for fc in range(FT):
                b.mm(W4[:, bk * 512:(bk + 1) * 512],
                     yq[:, fc * T_OWN + tt * 128: fc * T_OWN + (tt + 1) * 128],
                     wst[:, par * 8192 + fc * 512: par * 8192 + (fc + 1) * 512],
                     start=(fc == 0), stop=(fc == FT - 1),
                     inc=("pe" if fc == FT - 1 else None))
            b.wait("vector", "pe", b.val("pe"))
            sl1 = L["t1a"][:, tt * D + grp * 512: tt * D + grp * 512 + 512]
            b.v("tensor_add", sl1, sl1, W4[:, bk * 512:(bk + 1) * 512], inc="vec")
            acc_v.append(b.val("vec"))
        L[f"wo_grp_pe_{grp}"] = b.val("pe")
    t1_v = b.val("vec")
    if WO_STAGE >= 4:
        b.wait("sync", "vec", t1_v)
        sp = []
        for tt in range(NT_OWN):
            sp.append(b.dma("sync", dint["t1_sp"][tt * 128:(tt + 1) * 128, :],
                            L["t1a"][:, tt * D:(tt + 1) * D], inc="wld"))
        L["t1sp_ld"] = sp
    else:
        L["t1sp_ld"] = []
    b.tap("t1a", L["t1a"][:], "vec", t1_v)


def phase_ffnproj(b, L):
    xn, wst, W4, P0, P1 = L["xn"], L["wst"], L["W4"], L["P0"], L["P1"]
    dint = L["dint"]

    b.wait("scalar", "vec", L["xn_v"])
    ln_stats_fm(b, L, lambda ft, t0: xn[:, ft * T_ALL + t0: ft * T_ALL + t0 + 512],
                T_ALL, False, 1e-6)
    h_v = [0, 0]
    for tb in range(4):
        t0 = tb * 512
        rb = W4[:, (tb % 2) * 512:(tb % 2) * 512 + 512]
        b.wait("tensor", "vec", h_v[-2])
        bcast_mm(b, L, rb, L["r1"][0:1, t0:t0 + 512])
        b.wait("vector", "pe", b.val("pe"))
        for ft in range(FT):
            xt = xn[:, ft * T_ALL + t0: ft * T_ALL + t0 + 512]
            b.v("tensor_scalar", xt, xt, L["wanorm"][:, ft:ft + 1], None, ALU.mult)
            b.v("tensor_mul", xt, xt, rb, inc="vec")
        h_v.append(b.val("vec"))
    L["h_v"] = b.val("vec")

    ldk = stream_wT(b, L, dint["fk_bf"], "fk", 0, 512, 0)
    b.wait("tensor", "vec", L["h_v"])
    b.wait_dma("tensor", ldk)
    b.wait_dma("tensor", L["t1sp_ld"])
    for kh in range(HKV):
        b.wait("scalar", "vec", b.val("vec"))
        kev = []
        for half in range(4):
            ps = [P0, P1][half % 2]
            if half >= 2:
                b.wait("tensor", "sc", kev[half - 2])
            for dc in range(FT):
                b.mm(ps[:], wst[:, dc * 512 + kh * 128: dc * 512 + (kh + 1) * 128],
                     xn[:, dc * T_ALL + half * 512: dc * T_ALL + (half + 1) * 512],
                     start=(dc == 0), stop=(dc == FT - 1),
                     inc=("pe" if dc == FT - 1 else None))
            b.wait("scalar", "pe", b.val("pe"))
            b.act(L["scrbf"][:, half * 512:(half + 1) * 512], ps[:], AF.Copy, inc="sc")
            kev.append(b.val("sc"))
        rope_norm(b, L, src=L["scrbf"], dst=L["kv1"], dst_off=kh * T_ALL,
                  tlen=T_ALL, wcol=L["wknc"], src_sc=kev[-1], tpos_off=0)
    L["khat_v"] = b.val("vec")
    L["kproj_pe"] = b.val("pe")

    ldv = stream_wT(b, L, dint["fv_bf"], "fv", 0, 512, 8192)
    b.wait_dma("tensor", ldv)
    b.wait("tensor", "vec", L["khat_v"])
    vev = [0, 0]
    for tt in range(NT_ALL):
        pcol = 1024 + (tt % 2) * 512
        b.wait("tensor", "vec", vev[-2])
        for dc in range(FT):
            b.mm(W4[:, pcol:pcol + 512],
                 xn[:, dc * T_ALL + tt * 128: dc * T_ALL + (tt + 1) * 128],
                 wst[:, 8192 + dc * 512: 8192 + (dc + 1) * 512],
                 start=(dc == 0), stop=(dc == FT - 1),
                 inc=("pe" if dc == FT - 1 else None))
        b.wait("vector", "pe", b.val("pe"))
        b.v("tensor_copy", L["kv2"][:, tt * 512:(tt + 1) * 512],
            W4[:, pcol:pcol + 512], inc="vec")
        vev.append(b.val("vec"))
    L["vffn_v"] = b.val("vec")
    L["vproj_pe"] = b.val("pe")

    for qg in range(4):
        par = qg % 2
        if qg >= 2:
            b.wait("sync", "pe", L[f"fp_qg_pe_{qg - 2}"])
        elif qg == 0:
            b.wait("sync", "pe", L["kproj_pe"])
        else:
            b.wait("sync", "pe", L["vproj_pe"])
        ld = stream_wT(b, L, dint["fq_bf"], "fq", qg * 512, (qg + 1) * 512,
                       par * 8192)
        b.wait_dma("tensor", ld)
        for hi in range(4):
            qh = qg * 4 + hi
            b.wait("tensor", "vec", b.val("vec"))
            b.wait("scalar", "vec", b.val("vec"))
            for half in range(2):
                ps = [P0, P1][half]
                for dc in range(FT):
                    b.mm(ps[:], wst[:, par * 8192 + dc * 512 + hi * 128:
                                    par * 8192 + dc * 512 + (hi + 1) * 128],
                         xn[:, dc * T_ALL + T_PRI + half * 512:
                            dc * T_ALL + T_PRI + (half + 1) * 512],
                         start=(dc == 0), stop=(dc == FT - 1),
                         inc=("pe" if dc == FT - 1 and half == 1 else None))
            b.wait("scalar", "pe", b.val("pe"))
            b.act(L["scrbf"][:, 0:512], P0[:], AF.Copy)
            b.act(L["scrbf"][:, 512:1024], P1[:], AF.Copy, inc="sc")
            rope_norm(b, L, src=L["scrbf"], dst=L["yq"], dst_off=qh * T_OWN,
                      tlen=T_OWN, wcol=L["wqnc"], src_sc=b.val("sc"),
                      tpos_off=T_PRI)
        L[f"fp_qg_pe_{qg}"] = b.val("pe")
    L["qhat_v"] = b.val("vec")

    bounds(b, L)
    b.tap("qhat", L["yq"][:], "vec", L["qhat_v"])
    b.tap("khat", L["kv1"][:], "vec", L["khat_v"])
    b.tap("vffn", L["kv2"][:], "vec", L["vffn_v"])
    b.tap("negM", L["negM"][:], "vec", L["negM_v"])


def rope_norm(b, L, src, dst, dst_off, tlen, wcol, src_sc, tpos_off):
    P0, W4 = L["P0"], L["W4"]
    nb = tlen // 512
    b.wait("scalar", "sc", src_sc)
    mm_pe = [0, 0]
    for i in range(nb):
        t0 = i * 512
        b.wait("scalar", "pe", mm_pe[-2])
        b.act(L["scrbf2"][:, 512:1024], src[:, t0:t0 + 512], AF.Square, inc="sc")
        b.wait("tensor", "sc", b.val("sc"))
        b.mm(P0[0:1, :], L["ones"][:, 0:1], L["scrbf2"][:, 512:1024], inc="pe")
        mm_pe.append(b.val("pe"))
        b.wait("vector", "pe", b.val("pe"))
        r1s = L["r1"][0:1, t0:t0 + 512]
        b.v("tensor_scalar", r1s, P0[0:1, :], 1.0 / HD, None, ALU.mult)
        b.v("tensor_scalar", r1s, r1s, 1e-6, None, ALU.add)
        b.v("reciprocal", r1s, r1s, inc="vec")
        b.wait("scalar", "vec", b.val("vec"))
    b.act(ROW(L, 1, tlen), ROW(L, 1, tlen), AF.Sqrt, inc="sc")
    b.wait("tensor", "sc", b.val("sc"))
    rp = [0, 0]
    for i in range(nb):
        t0 = i * 512
        sl = i % 2
        rb = W4[:, sl * 512: sl * 512 + 512]
        sw = W4[:, 1024 + sl * 512: 1024 + sl * 512 + 512]
        b.wait("tensor", "vec", rp[-2])
        bcast_mm(b, L, rb, L["r1"][0:1, t0:t0 + 512])
        b.wait("vector", "pe", b.val("pe"))
        kn = L["scrbf2"][:, 0:512]
        tmp = L["scrbf2"][:, 512:1024]
        b.v("tensor_scalar", kn, src[:, t0:t0 + 512], wcol[:, 0:1], None, ALU.mult)
        b.v("tensor_mul", kn, kn, rb, inc="vec")
        kn_v = b.val("vec")
        b.wait("tensor", "vec", kn_v)
        b.mm(sw, L["swapm"][:], kn, inc="pe")
        b.wait("vector", "pe", b.val("pe"))
        cs = L["cosb"][:, tpos_off + t0: tpos_off + t0 + 512]
        sn = L["sinb"][:, tpos_off + t0: tpos_off + t0 + 512]
        do = dst[:, dst_off + t0: dst_off + t0 + 512]
        b.v("tensor_mul", do, kn, cs)
        b.v("tensor_mul", tmp, sw, sn)
        b.v("tensor_add", do, do, tmp, inc="vec")
        rp.append(b.val("vec"))


def bounds(b, L):
    P0 = L["P0"]
    b.wait("scalar", "vec", max(L["qhat_v"], L["khat_v"]))
    b.wait("tensor", "vec", max(L["qhat_v"], L["khat_v"]))
    mm_pe = [0, 0]

    def max_norm2(src, src_off, tlen, out_cell):
        for i in range(tlen // 512):
            t0 = i * 512
            sl = i % 2
            b.wait("scalar", "pe", mm_pe[-2])
            b.act(L["scrbf2"][:, sl * 512:(sl + 1) * 512],
                  src[:, src_off + t0: src_off + t0 + 512], AF.Square, inc="sc")
            b.wait("tensor", "sc", b.val("sc"))
            b.mm(P0[0:1, :], L["ones"][:, 0:1],
                 L["scrbf2"][:, sl * 512:(sl + 1) * 512], inc="pe")
            mm_pe.append(b.val("pe"))
            b.wait("vector", "pe", b.val("pe"))
            b.v("tensor_copy", L["r1"][0:1, t0:t0 + 512], P0[0:1, :], inc="vec")
            b.wait("tensor", "vec", b.val("vec"))
        b.v("reduce_max", out_cell, ROW(L, 1, tlen), axis=AXX, inc="vec")

    for kh in range(HKV):
        max_norm2(L["kv1"], kh * T_ALL, T_ALL, L["snorm"][0:1, 32 + kh:33 + kh])
    for qh in range(HQ):
        max_norm2(L["yq"], qh * T_OWN, T_OWN, L["snorm"][0:1, qh:qh + 1])
    for qh in range(HQ):
        kh = qh // (HQ // HKV)
        b.v("tensor_mul", L["snorm"][0:1, 48 + qh:49 + qh],
            L["snorm"][0:1, qh:qh + 1], L["snorm"][0:1, 32 + kh:33 + kh], inc="vec")
    b.wait("scalar", "vec", b.val("vec"))
    b.act(L["snorm"][0:1, 48:48 + HQ], L["snorm"][0:1, 48:48 + HQ], AF.Sqrt, inc="sc")
    b.wait("tensor", "sc", b.val("sc"))
    b.mm(L["W4"][:, 0:16], L["ones"][0:1, :], L["snorm"][0:1, 48:48 + HQ], inc="pe")
    b.wait("vector", "pe", b.val("pe"))
    b.v("tensor_scalar", L["negM"][:, 0:HQ], L["W4"][:, 0:16], -ATT_SCALE, None,
        ALU.mult)
    b.v("tensor_scalar", L["negM"][:, HQ:2 * HQ], L["negM"][:, 0:HQ],
        L["pbiasc"][:, 0:1], None, ALU.add, inc="vec")
    b.wait("tensor", "vec", b.val("vec"))
    L["negM_v"] = b.val("vec")


def phase_attn(b, L):
    P0, P1, W4 = L["P0"], L["P1"], L["W4"]
    attn = L["big2"]
    b.wait("tensor", "vec", L["negM_v"])
    b.wait("scalar", "vec", L["negM_v"])
    st_sc, pv_pe, msk_gp = {}, {}, {}
    i = 0
    for qh in range(HQ):
        kh = qh // (HQ // HKV)
        for blk in range(2):
            q0 = blk * 512
            nkt = 8 + (blk + 1) * 4
            diag0 = 8 + blk * 4
            for kt in range(nkt):
                sl = i % 2
                ps = [P0, P1][sl]
                b.wait("tensor", "sc", st_sc.get(i - 2, 0))
                b.mm(ps[:], L["kv1"][:, kh * T_ALL + kt * 128:
                                     kh * T_ALL + (kt + 1) * 128],
                     L["yq"][:, qh * T_OWN + q0: qh * T_OWN + q0 + 512], inc="pe")
                b.wait("scalar", "pe", b.val("pe"))
                b.wait("scalar", "pe", pv_pe.get(i - 2, 0))
                b.wait("scalar", "gp", msk_gp.get(i - 2, 0))
                bias = L["negM"][:, HQ + qh:HQ + qh + 1] if kt < 8 else \
                    L["negM"][:, qh:qh + 1]
                b.act(L["scrbf"][:, sl * 512:(sl + 1) * 512], ps[:], AF.Exp,
                      bias=bias, scale=ATT_SCALE, inc="sc")
                st_sc[i] = b.val("sc")
                if kt >= diag0:
                    b.wait("gpsimd", "sc", st_sc[i])
                    base = (T_PRI + q0) - kt * 128
                    b.gp("affine_select", L["scrbf"][:, sl * 512:(sl + 1) * 512],
                         L["scrbf"][:, sl * 512:(sl + 1) * 512], [[1, 512]],
                         ALU.is_ge, 0.0, base=base, channel_multiplier=-1,
                         inc="gp")
                    msk_gp[i] = b.val("gp")
                    b.wait("tensor", "gp", msk_gp[i])
                else:
                    b.wait("tensor", "sc", st_sc[i])
                b.mm(W4[:, 0:512],
                     L["kv2"][:, kt * 512 + kh * 128: kt * 512 + (kh + 1) * 128],
                     L["scrbf"][:, sl * 512:(sl + 1) * 512],
                     start=(kt == 0), stop=(kt == nkt - 1))
                b.mm(W4[0:1, 512:1024], L["ones"][:, 0:1],
                     L["scrbf"][:, sl * 512:(sl + 1) * 512],
                     start=(kt == 0), stop=(kt == nkt - 1), inc="pe")
                pv_pe[i] = b.val("pe")
                i += 1
            b.wait("scalar", "pe", b.val("pe"))
            b.act(L["denomrow"][0:1, 0:512], W4[0:1, 512:1024], AF.Ln)
            b.act(L["r1"][0:1, 0:512], L["denomrow"][0:1, 0:512], AF.Exp,
                  scale=-1.0, inc="sc")
            b.wait("tensor", "sc", b.val("sc"))
            b.mm(W4[:, 1024:1536], L["ones"][0:1, :], L["r1"][0:1, 0:512],
                 inc="pe")
            b.wait("scalar", "pe", b.val("pe"))
            b.scopy(L["scrbf2"][:, 0:512], W4[:, 1024:1536], inc="sc")
            b.wait("vector", "sc", b.val("sc"))
            b.v("tensor_mul", attn[:, qh * T_OWN + q0: qh * T_OWN + q0 + 512],
                W4[:, 0:512], L["scrbf2"][:, 0:512], inc="vec")
            b.wait("tensor", "vec", b.val("vec"))
    L["attn_v"] = b.val("vec")
    b.tap("attnp", attn[:], "vec", L["attn_v"])


def phase_hidden(b, L):
    wst, W4 = L["wst"], L["W4"]
    attn = L["big2"]
    dint = L["dint"]

    def hid_tile(tt):
        buf = L["kv1"] if tt < 4 else L["kv2"]
        off = (tt % 4) * D
        return buf[:, off:off + D]

    rl = []
    b.wait_dma("sync", L["t1sp_ld"])
    for tt in range(NT_OWN):
        rl.append(b.dma("sync", L["t1r"][:, tt * D:(tt + 1) * D],
                        dint["t1_sp"][tt * 128:(tt + 1) * 128, :], inc="wld"))
    b.wait_dma("vector", rl)
    b.wait("tensor", "vec", L["attn_v"])

    acc_v = [0, 0, 0, 0]
    xn_v = {}
    for grp in range(4):
        par = grp % 2
        if grp >= 2:
            b.wait("sync", "pe", L[f"hid_grp_pe_{grp - 2}"])
        ld = stream_wT(b, L, dint["fo_bf"], "fo", grp * 512, (grp + 1) * 512,
                       par * 8192)
        b.wait_dma("tensor", ld)
        for tt in range(NT_OWN):
            bk = tt % 4
            if grp == 0:
                if tt >= 1:
                    b.wait("sync", "vec", xn_v[tt - 1])
                xt_tok = b.dma("sync", L["xtile"][:],
                               L["din"]["x_own"][tt * 128:(tt + 1) * 128, :],
                               inc="wld")
                b.wait_dma("vector", xt_tok)
                b.wait("vector", "sc", L["statsT_sc"])
                b.v("tensor_scalar", hid_tile(tt), L["xtile"][:],
                    L["statsT"][:, 2 * tt:2 * tt + 1],
                    L["statsT"][:, 2 * tt + 1:2 * tt + 2],
                    ALU.subtract, ALU.mult)
                b.v("tensor_add", L["t1r"][:, tt * D:(tt + 1) * D],
                    L["t1r"][:, tt * D:(tt + 1) * D], hid_tile(tt), inc="vec")
                xn_v[tt] = b.val("vec")
            b.wait("tensor", "vec", acc_v[-4])
            if grp == 0:
                b.wait("tensor", "vec", xn_v[tt])
            for fc in range(FT):
                b.mm(W4[:, bk * 512:(bk + 1) * 512],
                     attn[:, fc * T_OWN + tt * 128: fc * T_OWN + (tt + 1) * 128],
                     wst[:, par * 8192 + fc * 512: par * 8192 + (fc + 1) * 512],
                     start=(fc == 0), stop=(fc == FT - 1),
                     inc=("pe" if fc == FT - 1 else None))
            b.wait("vector", "pe", b.val("pe"))
            ps = W4[:, bk * 512:(bk + 1) * 512]
            sl1 = L["t1r"][:, tt * D + grp * 512: tt * D + grp * 512 + 512]
            b.v("tensor_add", sl1, sl1, ps)
            hs = hid_tile(tt)[:, grp * 512:(grp + 1) * 512]
            b.v("tensor_add", hs, hs, ps, inc="vec")
            acc_v.append(b.val("vec"))
        L[f"hid_grp_pe_{grp}"] = b.val("pe")
    L["t1_attn_v"] = b.val("vec")

    b.wait("tensor", "vec", L["t1_attn_v"])
    trs = [0, 0]
    for tt in range(NT_OWN):
        for fc in range(FT):
            sl = (tt * FT + fc) % 2
            TB = [L["TB0"], L["TB1"]][sl]
            b.wait("tensor", "sc", trs[-2])
            b.tr(TB[:, 0:128], hid_tile(tt)[:, fc * 128:(fc + 1) * 128],
                 L["identb"][:], inc="pe")
            b.wait("scalar", "pe", b.val("pe"))
            b.scopy(L["hfm"][:, fc * T_OWN + tt * 128: fc * T_OWN + (tt + 1) * 128],
                    TB[:, 0:128], inc="sc")
            trs.append(b.val("sc"))
    hfm_sc = b.val("sc")
    L["hfm_sc"] = hfm_sc
    b.tap("hfm", L["hfm"][:], "sc", hfm_sc)

    b.wait("scalar", "sc", hfm_sc)
    b.wait("tensor", "sc", hfm_sc)
    ln_stats_fm(b, L, lambda ft, t0: L["hfm"][:, ft * T_OWN + t0:
                                              ft * T_OWN + t0 + 512],
                T_OWN, False, 1e-6)
    h2v = [0, 0]
    for tb in range(2):
        t0 = tb * 512
        rb = W4[:, (tb % 2) * 512:(tb % 2) * 512 + 512]
        b.wait("tensor", "vec", h2v[-2])
        bcast_mm(b, L, rb, L["r1"][0:1, t0:t0 + 512])
        b.wait("vector", "pe", b.val("pe"))
        for ft in range(FT):
            src = L["hfm"][:, ft * T_OWN + t0: ft * T_OWN + t0 + 512]
            dst = L["yq"][:, ft * T_OWN + t0: ft * T_OWN + t0 + 512]
            b.v("tensor_scalar", dst, src, L["wfnorm"][:, ft:ft + 1], None, ALU.mult)
            b.v("tensor_mul", dst, dst, rb, inc="vec")
        h2v.append(b.val("vec"))
    L["h2_v"] = b.val("vec")
    b.tap("h2", L["yq"][:], "vec", L["h2_v"])


def gate(b, L):
    P0 = L["P0"]
    h2 = L["yq"]
    G8, M8, GS = L["grow8"], L["masked8"], L["gsc"]
    b.wait("tensor", "vec", L["h2_v"])
    for tt in range(NT_OWN):
        for dc in range(FT):
            b.mm(P0[0:128, tt * 8:(tt + 1) * 8],
                 h2[:, dc * T_OWN + tt * 128: dc * T_OWN + (tt + 1) * 128],
                 L["gwT"][:, dc * 8:(dc + 1) * 8],
                 start=(dc == 0), stop=(dc == FT - 1),
                 inc=("pe" if dc == FT - 1 else None))
    b.wait("vector", "pe", b.val("pe"))
    for tt in range(NT_OWN):
        b.v("reduce_max", GS[:, 24 + tt:25 + tt], P0[0:128, tt * 8:(tt + 1) * 8],
            axis=AXX, negate=True, inc=("vec" if tt == NT_OWN - 1 else None))
    b.wait("scalar", "vec", b.val("vec"))
    for tt in range(NT_OWN):
        b.act(G8[:, tt * 8:(tt + 1) * 8], P0[0:128, tt * 8:(tt + 1) * 8], AF.Exp,
              bias=GS[:, 24 + tt:25 + tt], accum_out=GS[:, 16 + tt:17 + tt],
              inc=("sc" if tt == NT_OWN - 1 else None))
    b.wait("vector", "sc", b.val("sc"))
    for tt in range(NT_OWN):
        b.v("reduce_max", GS[:, tt:tt + 1], G8[:, tt * 8:(tt + 1) * 8], axis=AXX)
    for tt in range(NT_OWN):
        b.v("tensor_scalar", M8[:, tt * 8:(tt + 1) * 8], G8[:, tt * 8:(tt + 1) * 8],
            GS[:, tt:tt + 1], None, ALU.is_ge)
    for tt in range(NT_OWN):
        b.v("scalar_tensor_tensor", M8[:, tt * 8:(tt + 1) * 8],
            M8[:, tt * 8:(tt + 1) * 8], -1e30, G8[:, tt * 8:(tt + 1) * 8],
            ALU.mult, ALU.add)
    for tt in range(NT_OWN):
        b.v("reduce_max", GS[:, 8 + tt:9 + tt], M8[:, tt * 8:(tt + 1) * 8],
            axis=AXX)
    for tt in range(NT_OWN):
        b.v("tensor_add", GS[:, 24 + tt:25 + tt], GS[:, tt:tt + 1],
            GS[:, 8 + tt:9 + tt])
    for tt in range(NT_OWN):
        b.v("scalar_tensor_tensor", GS[:, 24 + tt:25 + tt],
            GS[:, 16 + tt:17 + tt], 1e-9, GS[:, 24 + tt:25 + tt],
            ALU.mult, ALU.add)
    for tt in range(NT_OWN):
        b.v("reciprocal", GS[:, 32 + tt:33 + tt], GS[:, 24 + tt:25 + tt])
    for tt in range(NT_OWN):
        b.v("scalar_tensor_tensor", L["cvals"][:, tt * 8:(tt + 1) * 8],
            G8[:, tt * 8:(tt + 1) * 8], GS[:, 8 + tt:9 + tt],
            G8[:, tt * 8:(tt + 1) * 8], ALU.is_ge, ALU.mult)
    for tt in range(NT_OWN):
        b.v("tensor_scalar", L["cvals"][:, tt * 8:(tt + 1) * 8],
            L["cvals"][:, tt * 8:(tt + 1) * 8], GS[:, 32 + tt:33 + tt], None,
            ALU.mult, inc=("vec" if tt == NT_OWN - 1 else None))
    L["cvals_v"] = b.val("vec")
    b.tap("cvals", L["cvals"][:], "vec", L["cvals_v"])


def phase_moe(b, L):
    W4, P0, P1 = L["W4"], L["P0"], L["P1"]
    h2 = L["yq"]
    act = L["hfm"]
    groups = [(0, 3), (3, 3), (6, 3), (9, 2)]

    def wtile(buf, dc):
        if buf == 0:
            return L["wst"][:, dc * 1408:(dc + 1) * 1408]
        if dc < 11:
            return L["mb1a"][:, dc * 1408:(dc + 1) * 1408]
        if dc < 13:
            return L["mb1b"][:, (dc - 11) * 1408:(dc - 10) * 1408]
        if dc < 15:
            return L["mb1c"][:, (dc - 13) * 1408:(dc - 12) * 1408]
        return L["mb1d"][:, 0:1408]

    def wdtile(buf, ic):
        if buf == 0:
            return L["wst"][:, ic * 2048:(ic + 1) * 2048]
        if ic < 8:
            return L["mb1a"][:, ic * 2048:(ic + 1) * 2048]
        if ic < 10:
            return L["mb1b"][:, (ic - 8) * 2048:(ic - 7) * 2048]
        return L["mb1c"][:, 0:2048]

    b.wait("vector", "vec", L["cvals_v"])
    b.wait("tensor", "vec", L["cvals_v"])

    pass_pe = {-2: 0, -1: 0}
    ld_done = {}
    prev_act_sc = 0
    prev_eu_v = 0
    out_v = [0] * NT_OWN

    def emit_loads(p):
        e, wi = p // 3, p % 3
        buf = p % 2
        ck = (f"wgm{e}", f"wum{e}", f"wdm{e}")[wi]
        dram = (L["dint"]["wgm_bf"], L["dint"]["wum_bf"], L["dint"]["wdm_bf"])[wi][e]
        b.wait_dma("sync", L["cd"](ck))
        b.wait("sync", "pe", pass_pe[p - 2])
        toks = []
        if wi < 2:
            for dc in range(FT):
                toks.append(b.dma("sync", wtile(buf, dc),
                                  dram[:, dc * 128:(dc + 1) * 128],
                                  inc="wld", transpose=True))
        else:
            for ic in range(IC):
                toks.append(b.dma("sync", wdtile(buf, ic),
                                  dram[:, ic * 128:(ic + 1) * 128],
                                  inc="wld", transpose=True))
        ld_done[p] = toks

    def emit_compute(p):
        nonlocal prev_act_sc, prev_eu_v
        e, wi = p // 3, p % 3
        buf = p % 2
        b.wait_dma("tensor", ld_done[p])
        if wi == 0 and e > 0:
            b.wait("tensor", "vec", out_v[NT_OWN - 1])
        if wi < 2:
            for g0, gn in groups:
                b.wait("tensor", "sc", prev_act_sc)
                if wi == 1:
                    b.wait("tensor", "vec", prev_eu_v)
                for dc in range(FT):
                    for gi in range(gn):
                        ic = g0 + gi
                        lhs = wtile(buf, dc)[:, ic * 128:(ic + 1) * 128]
                        for th in range(2):
                            if gi < 2:
                                ps = W4[:, gi * 1024 + th * 512:
                                        gi * 1024 + th * 512 + 512]
                            else:
                                ps = [P0, P1][th][:]
                            b.mm(ps, lhs,
                                 h2[:, dc * T_OWN + th * 512:
                                    dc * T_OWN + (th + 1) * 512],
                                 start=(dc == 0), stop=(dc == FT - 1),
                                 inc=("pe" if (dc == FT - 1 and gi == gn - 1
                                               and th == 1) else None))
                grp_pe = b.val("pe")
                for gi in range(gn):
                    ic = g0 + gi
                    for th in range(2):
                        ps = (W4[:, gi * 1024 + th * 512: gi * 1024 + th * 512 + 512]
                              if gi < 2 else [P0, P1][th][:])
                        sl = act[:, ic * T_OWN + th * 512: ic * T_OWN + (th + 1) * 512]
                        if wi == 0:
                            b.wait("scalar", "pe", grp_pe)
                            b.act(sl, ps, AF.Silu, inc="sc")
                        else:
                            b.wait("vector", "pe", grp_pe)
                            b.v("tensor_tensor", sl, sl, ps, ALU.mult, inc="vec")
                if wi == 0:
                    prev_act_sc = b.val("sc")
                else:
                    prev_eu_v = b.val("vec")
        else:
            b.wait("tensor", "vec", prev_eu_v)
            b.wait("tensor", "sc", prev_act_sc)
            for tt in range(NT_OWN):
                b.wait("tensor", "vec", out_v[tt - 1] if tt > 0 else
                       out_v[NT_OWN - 1])
                for ic in range(IC):
                    for nbk in range(4):
                        b.mm(W4[:, nbk * 512:(nbk + 1) * 512],
                             act[:, ic * T_OWN + tt * 128:
                                 ic * T_OWN + (tt + 1) * 128],
                             wdtile(buf, ic)[:, nbk * 512:(nbk + 1) * 512],
                             start=(ic == 0), stop=(ic == IC - 1),
                             inc=("pe" if ic == IC - 1 and nbk == 3 else None))
                b.wait("vector", "pe", b.val("pe"))
                b.v("scalar_tensor_tensor", L["t1r"][:, tt * D:(tt + 1) * D],
                    W4[:], L["cvals"][:, tt * 8 + e: tt * 8 + e + 1],
                    L["t1r"][:, tt * D:(tt + 1) * D], ALU.mult, ALU.add, inc="vec")
                out_v[tt] = b.val("vec")
        pass_pe[p] = b.val("pe")

    NP = 3 * E
    emit_loads(0)
    for p in range(1, NP):
        emit_loads(p)
        emit_compute(p - 1)
    emit_compute(NP - 1)
    for tt in range(NT_OWN):
        L[f"out_tile_v_{tt}"] = out_v[tt]


def finish(b, L):
    toks = list(getattr(b, "tap_tokens", []))
    if BUILD_PHASES >= 6:
        for tt in range(NT_OWN):
            b.wait("gpsimd", "vec", L[f"out_tile_v_{tt}"])
            toks.append(b.dma("gpsimd", L["out_own"][tt * 128:(tt + 1) * 128, :],
                              L["t1r"][:, tt * D:(tt + 1) * D], inc="cast"))
    else:
        b.barrier()
        src_t = L["yq"] if (BUILD_PHASES < 3 or WO_STAGE < 3) else (
            L["t1a"] if BUILD_PHASES < 4 else L["t1r"])
        for tt in range(NT_OWN):
            toks.append(b.dma("gpsimd", L["out_own"][tt * 128:(tt + 1) * 128, :],
                              src_t[:, tt * D:(tt + 1) * D], inc="cast"))
    b.wait_dma("gpsimd", toks)
    b.wait_dma("sync", toks)



_NC_CACHE = {}

def kernel(**inputs):
    key = (BUILD_PHASES, tuple(DEBUG_TAPS))
    if key not in _NC_CACHE:
        _NC_CACHE[key] = build_nc()
    nc = _NC_CACHE[key]

    x = np.ascontiguousarray(np.asarray(inputs["L0_attn_x"]), dtype=np.float32)
    freqs = 1.0 / ROPE_THETA ** (np.arange(0, HD, 2, dtype=np.float32) / HD)
    ident = np.eye(128, dtype=np.float32)

    def tcols(w):
        return np.ascontiguousarray(np.asarray(w, np.float32).reshape(FT, 128).T)

    base = {
        "gamma_t": tcols(inputs["L0_attn_gamma"]),
        "beta_t": tcols(inputs["L0_attn_beta"]),
        "w_anorm_t": tcols(inputs["L0_ffn_attn_norm_w"]),
        "w_fnorm_t": tcols(inputs["L0_ffn_ffn_norm_w"]),
        "w_qn_t": np.ascontiguousarray(
            np.asarray(inputs["L0_ffn_q_norm_w"], np.float32).reshape(128, 1)),
        "w_kn_t": np.ascontiguousarray(
            np.asarray(inputs["L0_ffn_k_norm_w"], np.float32).reshape(128, 1)),
        "identf": ident,
        "swapf": np.ascontiguousarray(
            np.eye(128, dtype=np.float32)[:, list(range(64, 128)) + list(range(64))]),
    }
    for k in ("L0_attn_Wq", "L0_attn_Wk", "L0_attn_Wv", "L0_attn_Wg", "L0_attn_Wo",
              "L0_ffn_q_w", "L0_ffn_k_w", "L0_ffn_v_w", "L0_ffn_o_w",
              "L0_ffn_gate_w", "L0_ffn_wg", "L0_ffn_wu", "L0_ffn_wd"):
        base[k] = np.ascontiguousarray(np.asarray(inputs[k]), dtype=np.float32)

    in_maps = []
    for core in range(8):
        bb, half = core // 2, core % 2
        pos = np.concatenate([np.arange(0, T_PRI, dtype=np.float32),
                              np.arange(half * T_OWN, half * T_OWN + T_OWN,
                                        dtype=np.float32)])
        f = np.outer(pos, freqs)
        m = dict(base)
        m["x_prior"] = np.ascontiguousarray(x[bb, 0:T_PRI])
        m["x_own"] = np.ascontiguousarray(x[bb, half * T_OWN:(half + 1) * T_OWN])
        cosT = np.cos(f).T.astype(np.float32)
        sinT = np.sin(f).T.astype(np.float32)
        m["cosA"] = np.ascontiguousarray(np.concatenate([cosT, cosT], 0))
        m["sinA"] = np.ascontiguousarray(np.concatenate([-sinT, sinT], 0))
        m["flag_col"] = np.full((128, 1), 1.0 if half == 1 else 0.0, np.float32)
        m["pbias_col"] = np.full((128, 1), 0.0 if half == 1 else -1e30, np.float32)
        in_maps.append(m)

    res = run_bass_kernel_spmd(nc, in_maps, core_ids=list(range(8)))
    out = np.empty((4, T_ALL, D), dtype=np.float32)
    for core in range(8):
        bb, half = core // 2, core % 2
        out[bb, half * T_OWN:(half + 1) * T_OWN] = res.results[core]["out_own"]
    kernel._last_results = res
    return out


# version 06
# speedup vs baseline: 1.0379x; 1.0379x over previous
import numpy as np
import concourse.bass as bass
import concourse.mybir as mybir
from concourse.bass_utils import run_bass_kernel_spmd
from contextlib import ExitStack

F32 = mybir.dt.float32
BF16 = mybir.dt.bfloat16
AF = mybir.ActivationFunctionType
ALU = mybir.AluOpType
AXX = mybir.AxisListType.X

D = 2048
FT = 16
H = 16
KD = 128
HQ, HKV, HD = 16, 4, 128
E, I = 8, 1408
IC = 11
C = 64
T_OWN = 1024
T_PRI = 1024
T_ALL = 2048
NT_OWN = 8
NT_ALL = 16
NCH = T_ALL // C
OWNC = T_OWN // C
ROPE_THETA = 1e6
QK_SCALE = float(KD) ** (-0.5)
ATT_SCALE = float(HD) ** (-0.5)

BUILD_PHASES = 6
DEBUG_TAPS = []
GLA_HEADS = H
GLA_CHUNKS = NCH
GLA_OWN_OPS = 5
WO_STAGE = 4

TAP_SPECS = {
    "xn": ([128, FT * T_ALL], BF16),
    "rows": ([1, T_ALL], BF16),
    "statsT": ([128, 2 * NT_OWN], F32),
    "y": ([128, H * T_OWN], BF16),
    "t1a": ([128, NT_OWN * D], BF16),
    "qhat": ([128, HQ * T_OWN], BF16),
    "khat": ([128, HKV * T_ALL], BF16),
    "vffn": ([128, NT_ALL * 512], BF16),
    "attnp": ([128, HQ * T_OWN], BF16),
    "h2": ([128, H * T_OWN], BF16),
    "cvals": ([128, NT_OWN * 8], F32),
    "negM": ([128, 2 * HQ], F32),
    "hfm": ([128, H * T_OWN], BF16),
}


class B:

    def __init__(self, nc, sems, tapout):
        self.nc = nc
        self.sems = sems
        self.tapout = tapout
        self.cur = None
        self.eng = None
        self.count = None

    def begin(self, section, eng):
        self.cur = section
        self.eng = eng
        self.count = {k: 0 for k in self.sems}
        self.dq = {
            "cast": {"sems": [f"cq{i}" for i in range(4)], "n": 4, "next": 0},
            "wld": {"sems": [f"wq{i}" for i in range(6)], "n": 6, "next": 0},
        }
        self.tap_tokens = []

    def val(self, sem):
        return self.count[sem]

    def _fin(self, instr, inc, n):
        if inc is not None:
            self.count[inc] += n
            if instr is not None:
                instr.then_inc(self.sems[inc], n)

    def wait(self, section, sem, v):
        if v <= 0:
            return
        if self.cur == section:
            self.eng.wait_ge(self.sems[sem], v)

    def dma(self, section, out, in_, inc, transpose=False):
        q = self.dq[inc]
        lane = q["next"] % q["n"]
        q["next"] += 1
        sem = q["sems"][lane]
        prev = self.count[sem]
        if prev > 0 and self.cur == section:
            self.eng.wait_ge(self.sems[sem], prev)
        instr = None
        if self.cur == section:
            instr = self.eng.dma_start(out=out, in_=in_, transpose=transpose)
        self.count[sem] += 16
        if instr is not None:
            instr.then_inc(self.sems[sem], 16)
        return (sem, self.count[sem])

    def wait_dma(self, section, tokens):
        if isinstance(tokens, tuple):
            tokens = [tokens]
        best = {}
        for sem, v in tokens:
            best[sem] = max(best.get(sem, 0), v)
        if self.cur == section:
            for sem, v in best.items():
                self.eng.wait_ge(self.sems[sem], v)

    def mm(self, out, lhsT, rhs, start=True, stop=True, inc=None):
        instr = None
        if self.cur == "tensor":
            instr = self.eng.matmul(out, lhsT, rhs, start=start, stop=stop)
        self._fin(instr, inc, 1)

    def tr(self, out, in_, ident, inc=None):
        instr = None
        if self.cur == "tensor":
            instr = self.eng.transpose(out, in_, ident)
        self._fin(instr, inc, 1)

    def act(self, out, in_, func, bias=0.0, scale=1.0, accum_out=None, inc=None):
        instr = None
        if self.cur == "scalar":
            instr = self.eng.activation(out, in_, func, bias=bias, scale=scale,
                                        accum_out=accum_out)
        self._fin(instr, inc, 1)

    def scopy(self, out, in_, inc=None):
        instr = None
        if self.cur == "scalar":
            instr = self.eng.copy(out, in_)
        self._fin(instr, inc, 1)

    def v(self, method, *args, inc=None, **kw):
        instr = None
        if self.cur == "vector":
            instr = getattr(self.eng, method)(*args, **kw)
        self._fin(instr, inc, 1)

    def gp(self, method, *args, inc=None, **kw):
        instr = None
        if self.cur == "gpsimd":
            instr = getattr(self.eng, method)(*args, **kw)
        self._fin(instr, inc, 1)

    def sem_inc(self, section, sem, n=1):
        if self.cur == section:
            self.eng.nop().then_inc(self.sems[sem], n)
        self.count[sem] += n

    def marker(self, section, sem, junk, ones):
        instr = None
        if self.cur == section:
            if section == "vector":
                instr = self.eng.engine_nop()
            elif section == "scalar":
                instr = self.eng.copy(junk[0:1, 0:1], ones[0:1, 0:1])
            elif section == "gpsimd":
                instr = self.eng.memset(junk[0:1, 4:5], 0.0)
        self.count[sem] += 1
        if instr is not None:
            instr.then_inc(self.sems[sem], 1)

    def barrier(self):
        self.wait_dma("sync", list(self.tap_tokens)) if self.tap_tokens else None
        L = self._L
        self.marker("vector", "vec", L["junk"], L["ones"])
        self.marker("scalar", "sc", L["junk"], L["ones"])
        self.marker("gpsimd", "gp", L["junk"], L["ones"])
        self.sem_inc("sync", "sy", 1)
        targets = {"vec": self.count["vec"], "sc": self.count["sc"],
                   "gp": self.count["gp"], "sy": self.count["sy"],
                   "pe": self.count["pe"]}
        own = {"sync": "sy", "gpsimd": "gp", "tensor": "pe",
               "vector": "vec", "scalar": "sc"}
        if self.cur in own:
            for sem, v in targets.items():
                if sem != own[self.cur]:
                    self.eng.wait_ge(self.sems[sem], v)

    def tap(self, name, ap, sem, v):
        if name not in DEBUG_TAPS:
            return
        self.wait("sync", sem, v)
        tok = self.dma("sync", self.tapout[name], ap, inc="wld")
        if not hasattr(self, "tap_tokens"):
            self.tap_tokens = []
        self.tap_tokens.append(tok)


ORG = 16512

MEM = [
    ("xn",     [128, FT * T_ALL], BF16, 0),
    ("big2",   [128, H * T_OWN],  BF16, 0),
    ("hfm",    [128, H * T_OWN],  BF16, 0),
    ("t1r",    [128, NT_OWN * D], BF16, 32768),
    ("yq",     [128, H * T_OWN],  BF16, 65536),
    ("gate8b", [8, D],            BF16, 98304),
    ("gf32",   [128, T_ALL],      F32,  98304),
    ("Af32",   [128, T_ALL],      F32,  106496),
    ("kraw",   [128, T_ALL],      BF16, 114688),
    ("Ktil",   [128, T_ALL],      BF16, 118784),
    ("qraw",   [128, T_OWN],      BF16, 122880),
    ("Qtil",   [128, T_OWN],      BF16, 124928),
    ("Khat",   [128, T_ALL],      BF16, 126976),
    ("t1a",    [128, NT_OWN * D], BF16, 98304),
    ("kv1",    [128, HKV * T_ALL], BF16, 98304),
    ("kv2",    [128, NT_ALL * 512], BF16, 114688),
    ("mb1a",   [128, 16384],      BF16, 98304),
    ("vgla",   [128, NT_ALL * 256], BF16, 131072),
    ("xtile",  [128, D],          F32,  131072),
    ("mb1b",   [128, 4096],       BF16, 131072),
    ("cosb",   [128, T_ALL],      BF16, 139264),
    ("sinb",   [128, T_ALL],      BF16, 143360),
    ("mb1c",   [128, 4096],       BF16, 139264),
    ("wst",    [128, 22528],      BF16, 147456),
    ("scrbf",  [128, 2048],       BF16, 192512),
    ("scrbf2", [128, 1024],       BF16, 196608),
    ("mb1d",   [128, 2048],       BF16, 192512),
    ("mb1e",   [128, 2048],       BF16, 198656),
    ("statsT", [128, 2 * NT_OWN], F32,  198656),
    ("cvals",  [128, NT_OWN * 8], F32,  198720),
    ("gwT",    [128, FT * 8],     BF16, 198976),
    ("r0",     [1, T_ALL],        BF16, 200704),
    ("r1",     [1, T_ALL],        BF16, 204800),
    ("mb1e",   [128, 2048],       BF16, 200704),
    ("Acols",  [128, 6 * NCH],    F32,  208896),
    ("KhatT",  [128, 128],        BF16, 209664),
    ("Qr",     [128, C],          BF16, 209920),
    ("Pt0",    [128, C],          BF16, 210048),
    ("Pt1",    [128, C],          BF16, 210176),
    ("vpad0",  [128, 128],        BF16, 210304),
    ("vpad1",  [128, 128],        BF16, 210560),
    ("Sfp",    [128, 128],        F32,  210816),
    ("denomrow", [1, 512],        F32,  208896),
    ("negM",   [128, 2 * HQ],     F32,  210944),
    ("snorm",  [1, 64],           BF16, 211072),
    ("grow",   [128, 8],          F32,  211200),
    ("gcol",   [128, 16],         F32,  211232),
    ("grow8",  [128, 64],         F32,  208896),
    ("masked8", [128, 64],        F32,  209152),
    ("gsc",    [128, 40],         F32,  209408),
    ("Sbf",    [128, 128],        BF16, 211392),
    ("ones",   [128, 128],        BF16, 211648),
    ("identb", [128, 128],        BF16, 211904),
    ("gammab", [128, FT],         F32,  212160),
    ("betab",  [128, FT],         F32,  212224),
    ("wanorm", [128, FT],         F32,  212288),
    ("wfnorm", [128, FT],         F32,  212352),
    ("wqnc",   [128, 1],          F32,  212416),
    ("wknc",   [128, 1],          F32,  212448),
    ("flagc",  [128, 1],          F32,  212480),
    ("pbiasc", [128, 1],          F32,  212512),
    ("swapm",  [128, 128],        BF16, 212544),
    ("junk",   [128, 8],          F32,  212800),
]
WST_QKG = [0, 6144]
WST_V = [12288, 16384]
WST_ZERO = 20480


def build_nc():
    nc = bass.Bass("TRN2", target_bir_lowering=False, debug=False,
                   detect_race_conditions=False)

    din = {}
    def dinp(name, shape, dtype=F32):
        din[name] = nc.dram_tensor(name, list(shape), dtype, kind="ExternalInput").ap()

    dinp("xT_all", [D, T_ALL]); dinp("x_own", [T_OWN, D])
    dinp("L0_attn_Wq", [D, D]); dinp("L0_attn_Wk", [D, D])
    dinp("L0_attn_Wv", [D, D]); dinp("L0_attn_Wg", [D, D])
    dinp("L0_attn_Wo", [D, D])
    dinp("gamma_t", [128, FT]); dinp("beta_t", [128, FT])
    dinp("w_anorm_t", [128, FT]); dinp("w_fnorm_t", [128, FT])
    dinp("w_qn_t", [128, 1]); dinp("w_kn_t", [128, 1])
    dinp("L0_ffn_q_w", [D, D]); dinp("L0_ffn_k_w", [D, HKV * HD])
    dinp("L0_ffn_v_w", [D, HKV * HD]); dinp("L0_ffn_o_w", [HQ * HD, D])
    dinp("L0_ffn_gate_w", [E, D])
    dinp("L0_ffn_wg", [E, D, I]); dinp("L0_ffn_wu", [E, D, I])
    dinp("L0_ffn_wd", [E, I, D])
    dinp("identf", [128, 128])
    dinp("cosA", [128, T_ALL]); dinp("sinA", [128, T_ALL])
    dinp("swapf", [128, 128])
    dinp("flag_col", [128, 1]); dinp("pbias_col", [128, 1])

    out_own = nc.dram_tensor("out_own", [T_OWN, D], F32, kind="ExternalOutput").ap()

    dint = {}
    def dmk(name, shape):
        dint[name] = nc.dram_tensor(name, list(shape), BF16, kind="Internal").ap()

    dmk("xbf", [T_OWN, D]); dmk("xbfT", [D, T_ALL])
    dmk("wq_bf", [D, D]); dmk("wk_bf", [D, D]); dmk("wv_bf", [D, D])
    dmk("wg_bf", [D, D]); dmk("wo_bf", [D, D])
    dmk("fq_bf", [D, D]); dmk("fk_bf", [D, HKV * HD]); dmk("fv_bf", [D, HKV * HD])
    dmk("fo_bf", [HQ * HD, D])
    dmk("wgm_bf", [E, D, I]); dmk("wum_bf", [E, D, I]); dmk("wdm_bf", [E, I, D])
    dmk("t1_sp", [T_OWN, D])

    ctx = ExitStack()
    with ctx:
        sb = {}
        for name, shape, dtype, off in MEM:
            sb[name] = nc.alloc_sbuf_tensor_at(name, list(shape), dtype,
                                               offset=ORG + off)

        def psum(name, shape, dtype=F32):
            sb[name] = ctx.enter_context(nc.psum_tensor(name, list(shape), dtype))
        psum("W4", [128, 2048])
        psum("P0", [128, 512])
        psum("P1", [128, 512])
        psum("TB0", [128, 512], BF16)
        psum("TB1", [128, 512], BF16)

        cast_list = [
            ("ident", sb["identb"][:], din["identf"]),
            ("swap", sb["swapm"][:], din["swapf"]),
            ("gate", sb["gate8b"][:], din["L0_ffn_gate_w"]),
            ("cos", sb["cosb"][:], din["cosA"]),
            ("sin", sb["sinb"][:], din["sinA"]),
            ("x0", dint["xbfT"], din["xT_all"]),
            ("x1", dint["xbf"], din["x_own"]),
            ("wq", dint["wq_bf"], din["L0_attn_Wq"]),
            ("wk", dint["wk_bf"], din["L0_attn_Wk"]),
            ("wg", dint["wg_bf"], din["L0_attn_Wg"]),
            ("wv", dint["wv_bf"], din["L0_attn_Wv"]),
            ("wo", dint["wo_bf"], din["L0_attn_Wo"]),
            ("fq", dint["fq_bf"], din["L0_ffn_q_w"]),
            ("fk", dint["fk_bf"], din["L0_ffn_k_w"]),
            ("fv", dint["fv_bf"], din["L0_ffn_v_w"]),
            ("fo", dint["fo_bf"], din["L0_ffn_o_w"]),
        ]
        for e in range(E):
            cast_list.append((f"wgm{e}", dint["wgm_bf"][e], din["L0_ffn_wg"][e]))
            cast_list.append((f"wum{e}", dint["wum_bf"][e], din["L0_ffn_wu"][e]))
            cast_list.append((f"wdm{e}", dint["wdm_bf"][e], din["L0_ffn_wd"][e]))
        cast_idx = {name: i for i, (name, _, _) in enumerate(cast_list)}

        sems = {}
        for s in ("pe", "vec", "sc", "gp", "sy",
                  "cq0", "cq1", "cq2", "cq3",
                  "wq0", "wq1", "wq2", "wq3", "wq4", "wq5"):
            sems[s] = ctx.enter_context(nc.semaphore(s))

        tapout = {}
        for name in DEBUG_TAPS:
            shape, dt = TAP_SPECS[name]
            tapout[name] = nc.dram_tensor("tap_" + name, list(shape), dt,
                                          kind="ExternalOutput").ap()

        b = B(nc, sems, tapout)
        b._L = sb
        L0 = dict(sb)
        L0.update(din=din, dint=dint, out_own=out_own,
                  cast_list=cast_list, cast_idx=cast_idx)

        counts = []

        def run_section(name, eng):
            b.begin(name, eng)
            with nc.allow_low_precision(reason="bf16 operand storage by design"):
                program(b, dict(L0))
            counts.append(dict(b.count))
            if len(counts) > 1:
                assert counts[-1] == counts[0], \
                    f"walk divergence in section {name}"

        with nc.Block() as block:
            @block.sync
            def _(eng):
                run_section("sync", eng)

            @block.gpsimd
            def _(eng):
                run_section("gpsimd", eng)

            @block.tensor
            def _(eng):
                run_section("tensor", eng)

            @block.vector
            def _(eng):
                run_section("vector", eng)

            @block.scalar
            def _(eng):
                run_section("scalar", eng)

    return nc


def ROW(L, r, n=T_ALL):
    return L[("r0", "r1")[r]][0:1, 0:n]



def program(b, L):
    din = L["din"]

    cast_tok = {}
    for name, out_ap, in_ap in L["cast_list"]:
        cast_tok[name] = b.dma("gpsimd", out_ap, in_ap, inc="cast")
    L["cd"] = lambda key: cast_tok[key]

    consts_ld = []
    for nm, key in (("gammab", "gamma_t"), ("betab", "beta_t"),
                    ("wanorm", "w_anorm_t"), ("wfnorm", "w_fnorm_t"),
                    ("wqnc", "w_qn_t"), ("wknc", "w_kn_t"),
                    ("flagc", "flag_col"), ("pbiasc", "pbias_col")):
        consts_ld.append(b.dma("sync", L[nm][:], din[key], inc="wld"))

    b.wait_dma("vector", consts_ld)
    b.v("memset", L["ones"][:], 1.0)
    b.v("memset", L["wst"][:, WST_ZERO:WST_ZERO + 2048], 0.0, inc="vec")
    L["consts_v"] = b.val("vec")

    phase1(b, L)
    b.barrier()
    if BUILD_PHASES >= 2:
        phase_gla(b, L)
        b.barrier()
    if BUILD_PHASES >= 3:
        phase_wo(b, L)
        b.barrier()
    if BUILD_PHASES >= 4:
        phase_ffnproj(b, L)
        b.barrier()
        phase_attn(b, L)
        b.barrier()
    if BUILD_PHASES >= 5:
        phase_hidden(b, L)
        b.barrier()
        gate(b, L)
        b.barrier()
    if BUILD_PHASES >= 6:
        phase_moe(b, L)
    finish(b, L)


def ln_stats_fm(b, L, src_getter, tlen, with_mean, eps):
    P0, P1 = L["P0"], L["P1"]
    sq_pe = [0, 0]
    for tb in range(tlen // 512):
        t0 = tb * 512
        for ft in range(FT):
            xt = src_getter(ft, t0)
            scr = [L["scrbf"], L["scrbf2"]][ft % 2]
            b.wait("scalar", "pe", sq_pe[-2])
            b.act(scr[:, 0:512], xt, AF.Square, inc="sc")
            b.wait("tensor", "sc", b.val("sc"))
            if with_mean:
                b.mm(P0[0:1, :], L["ones"][:, 0:1], xt,
                     start=(ft == 0), stop=(ft == FT - 1))
            b.mm(P1[0:1, :], L["ones"][:, 0:1], scr[:, 0:512],
                 start=(ft == 0), stop=(ft == FT - 1), inc="pe")
            sq_pe.append(b.val("pe"))
        b.wait("vector", "pe", b.val("pe"))
        r0s = L["r0"][0:1, t0:t0 + 512]
        r1s = L["r1"][0:1, t0:t0 + 512]
        if with_mean:
            b.v("tensor_scalar", r0s, P0[0:1, :], 1.0 / D, None, ALU.mult)
        b.v("tensor_scalar", r1s, P1[0:1, :], 1.0 / D, None, ALU.mult)
        if with_mean:
            b.v("tensor_mul", P1[0:1, :], r0s, r0s)
            b.v("tensor_sub", r1s, r1s, P1[0:1, :])
        b.v("tensor_scalar", r1s, r1s, eps, None, ALU.add)
        b.v("reciprocal", r1s, r1s, inc="vec")
        b.wait("tensor", "vec", b.val("vec"))
    b.wait("scalar", "vec", b.val("vec"))
    b.act(ROW(L, 1, tlen), ROW(L, 1, tlen), AF.Sqrt, inc="sc")
    b.wait("tensor", "sc", b.val("sc"))
    b.wait("vector", "sc", b.val("sc"))


def bcast_mm(b, L, psum_slice, row_ap):
    b.mm(psum_slice, L["ones"][0:1, :], row_ap, inc="pe")


def phase1(b, L):
    xn, W4, P0, P1 = L["xn"], L["W4"], L["P0"], L["P1"]
    xbf = L["dint"]["xbf"]

    b.wait_dma("tensor", [L["cd"]("ident"), L["cd"]("gate")])
    gtr = [0, 0]
    for dc in range(FT):
        TB = [L["TB0"], L["TB1"]][dc % 2]
        b.wait("tensor", "sc", gtr[-2])
        b.tr(TB[0:128, 0:8], L["gate8b"][0:8, dc * 128:(dc + 1) * 128],
             L["identb"][0:8, 0:8], inc="pe")
        b.wait("scalar", "pe", b.val("pe"))
        b.scopy(L["gwT"][:, dc * 8:(dc + 1) * 8], TB[0:128, 0:8], inc="sc")
        gtr.append(b.val("sc"))

    b.wait_dma("sync", [L["cd"]("x0")])
    xbfT = L["dint"]["xbfT"]
    xld = []
    for ft in range(FT):
        xld.append(b.dma("sync", xn[:, ft * T_ALL:(ft + 1) * T_ALL],
                         xbfT[ft * 128:(ft + 1) * 128, :], inc="wld"))
    b.wait_dma("scalar", xld)
    b.wait_dma("tensor", xld)
    b.wait("tensor", "vec", L["consts_v"])
    ln_stats_fm(b, L, lambda ft, t0: xn[:, ft * T_ALL + t0: ft * T_ALL + t0 + 512],
                T_ALL, True, 1e-5)

    nrm_v = [0, 0]
    for tb in range(4):
        t0 = tb * 512
        mb = W4[:, (tb % 2) * 1024:(tb % 2) * 1024 + 512]
        rb = W4[:, (tb % 2) * 1024 + 512:(tb % 2) * 1024 + 1024]
        b.wait("tensor", "vec", nrm_v[-2])
        bcast_mm(b, L, mb, L["r0"][0:1, t0:t0 + 512])
        bcast_mm(b, L, rb, L["r1"][0:1, t0:t0 + 512])
        b.wait("vector", "pe", b.val("pe"))
        for ft in range(FT):
            xt = xn[:, ft * T_ALL + t0: ft * T_ALL + t0 + 512]
            b.v("tensor_sub", xt, xt, mb)
            b.v("tensor_mul", xt, xt, rb, inc="vec")
        nrm_v.append(b.val("vec"))
    L["xn_v"] = b.val("vec")

    b.wait_dma("tensor", L["cd"]("ident"))
    b.wait("tensor", "vec", L["xn_v"])
    prev = [0, 0]
    for tt in range(NT_OWN):
        t0 = T_PRI + tt * 128
        TB = [L["TB0"], L["TB1"]][tt % 2]
        b.wait("tensor", "sc", prev[-2])
        b.tr(TB[0:128, 0:1], L["r0"][0:1, t0:t0 + 128],
             L["identb"][0:1, 0:1], inc="pe")
        b.tr(TB[0:128, 2:3], L["r1"][0:1, t0:t0 + 128],
             L["identb"][0:1, 0:1], inc="pe")
        b.wait("scalar", "pe", b.val("pe"))
        b.scopy(L["statsT"][:, 2 * tt:2 * tt + 1], TB[0:128, 0:1])
        b.scopy(L["statsT"][:, 2 * tt + 1:2 * tt + 2], TB[0:128, 2:3], inc="sc")
        prev.append(b.val("sc"))
    L["statsT_sc"] = b.val("sc")

    b.tap("xn", xn[:], "vec", L["xn_v"])
    b.tap("rows", L["r0"][:], "vec", L["xn_v"])
    b.tap("statsT", L["statsT"][:], "sc", L["statsT_sc"])


def stream_wT(b, L, dram_bf, ckey, f0, f1, col0):
    b.wait_dma("sync", L["cd"](ckey))
    w = f1 - f0
    toks = []
    for dc in range(FT):
        toks.append(b.dma("sync", L["wst"][:, col0 + dc * w: col0 + (dc + 1) * w],
                          dram_bf[dc * 128:(dc + 1) * 128, f0:f1], inc="wld"))
    return toks


def phase_gla(b, L):
    xn, Ac, wst = L["xn"], L["Acols"], L["wst"]
    W4, P0, P1 = L["W4"], L["P0"], L["P1"]
    dint = L["dint"]

    b.wait("tensor", "vec", L["xn_v"])
    b.wait_dma("tensor", L["cd"]("ident"))
    b.v("memset", L["vpad0"][:], 0.0)
    b.v("memset", L["vpad1"][:], 0.0)
    b.v("memset", L["Pt0"][:], 0.0)
    b.v("memset", L["Pt1"][:], 0.0, inc="vec")

    trD, evD, dsD, chD = {}, {}, {}, {}
    ptS, ptG, ymD, yeS = [0, 0], [0, 0], [0, 0], [0, 0]
    for h in range(GLA_HEADS):
        f0, f1 = h * 128, (h + 1) * 128
        par = h % 2
        hh2 = h % 2
        qkg0 = WST_QKG[par]

        if h >= 2:
            b.wait("sync", "pe", L[f"gla_wused_{h - 2}"])
        ldq = stream_wT(b, L, dint["wq_bf"], "wq", f0, f1, qkg0 + 0)
        ldk = stream_wT(b, L, dint["wk_bf"], "wk", f0, f1, qkg0 + 2048)
        ldg = stream_wT(b, L, dint["wg_bf"], "wg", f0, f1, qkg0 + 4096)

        if h % 2 == 0:
            grp = h // 2
            v0 = WST_V[par]
            if h >= 2:
                b.wait("sync", "pe", L[f"gla_vused_{h - 2}"])
            ldv = stream_wT(b, L, dint["wv_bf"], "wv", grp * 256, (grp + 1) * 256,
                            v0)
            b.wait_dma("tensor", ldv)
            if h > 0:
                b.wait("tensor", "vec", L[f"gla_head_done_{h - 1}"])
                b.wait("tensor", "sc", L[f"gla_ysc_{h - 1}"])
            vev = [0, 0]
            for tt in range(NT_ALL):
                pcol = (tt % 2) * 1024
                b.wait("tensor", "vec", vev[-2])
                for dc in range(FT):
                    b.mm(W4[:, pcol:pcol + 256],
                         xn[:, dc * T_ALL + tt * 128: dc * T_ALL + (tt + 1) * 128],
                         wst[:, v0 + dc * 256: v0 + (dc + 1) * 256],
                         start=(dc == 0), stop=(dc == FT - 1),
                         inc=("pe" if dc == FT - 1 else None))
                b.wait("vector", "pe", b.val("pe"))
                b.v("tensor_copy", L["vgla"][:, tt * 256:(tt + 1) * 256],
                    W4[:, pcol:pcol + 256], inc="vec")
                vev.append(b.val("vec"))
            L[f"gla_vused_{h}"] = b.val("pe")
            L[f"gla_vused_{h + 1}"] = b.val("pe")
            L["gla_vgrp_v"] = b.val("vec")

        b.wait_dma("tensor", ldq)
        if h >= 1:
            b.wait("tensor", "vec", L[f"gla_head_done_{h - 1}"])
            b.wait("scalar", "vec", L[f"gla_prep_v_{h - 1}"])
        if h % 2 == 0:
            b.wait("tensor", "vec", L["gla_vgrp_v"])
        for half in range(2):
            ps = [P0, P1][half]
            for dc in range(FT):
                b.mm(ps[:], wst[:, qkg0 + dc * 128: qkg0 + (dc + 1) * 128],
                     xn[:, dc * T_ALL + T_PRI + half * 512:
                        dc * T_ALL + T_PRI + (half + 1) * 512],
                     start=(dc == 0), stop=(dc == FT - 1),
                     inc=("pe" if dc == FT - 1 and half == 1 else None))
        b.wait("scalar", "pe", b.val("pe"))
        b.act(L["qraw"][:, 0:512], P0[:], AF.Copy, scale=QK_SCALE)
        b.act(L["qraw"][:, 512:1024], P1[:], AF.Copy, scale=QK_SCALE, inc="sc")
        qevac = b.val("sc")
        b.wait_dma("tensor", ldk)
        kevacs = []
        for half in range(4):
            ps = [P0, P1][half % 2]
            b.wait("tensor", "sc", qevac if half < 2 else kevacs[half - 2])
            for dc in range(FT):
                b.mm(ps[:], wst[:, qkg0 + 2048 + dc * 128: qkg0 + 2048 + (dc + 1) * 128],
                     xn[:, dc * T_ALL + half * 512: dc * T_ALL + (half + 1) * 512],
                     start=(dc == 0), stop=(dc == FT - 1),
                     inc=("pe" if dc == FT - 1 else None))
            b.wait("scalar", "pe", b.val("pe"))
            b.act(L["kraw"][:, half * 512:(half + 1) * 512], ps[:], AF.Copy, inc="sc")
            kevacs.append(b.val("sc"))
        b.wait_dma("tensor", ldg)
        gevacs = []
        for half in range(4):
            ps = [P0, P1][half % 2]
            b.wait("tensor", "sc", kevacs[2 + half % 2] if half < 2 else gevacs[half - 2])
            for dc in range(FT):
                b.mm(ps[:], wst[:, qkg0 + 4096 + dc * 128: qkg0 + 4096 + (dc + 1) * 128],
                     xn[:, dc * T_ALL + half * 512: dc * T_ALL + (half + 1) * 512],
                     start=(dc == 0), stop=(dc == FT - 1),
                     inc=("pe" if dc == FT - 1 else None))
            b.wait("scalar", "pe", b.val("pe"))
            b.act(L["gf32"][:, half * 512:(half + 1) * 512], ps[:], AF.Sigmoid, inc="sc")
            gevacs.append(b.val("sc"))
        b.act(L["gf32"][:], L["gf32"][:], AF.Ln, inc="sc")
        lsig_sc = b.val("sc")
        L[f"gla_wused_{h}"] = b.val("pe")

        b.wait("vector", "sc", lsig_sc)
        b.v("tensor_tensor_scan", L["Af32"][:], L["gf32"][:],
            wst[:, WST_ZERO:WST_ZERO + 2048], 0.0, ALU.add, ALU.add, inc="vec")
        for c in range(NCH):
            b.v("tensor_copy", Ac[:, 0 * NCH + c:0 * NCH + c + 1],
                L["Af32"][:, c * C + C // 2 - 1:c * C + C // 2])
            b.v("tensor_copy", Ac[:, 1 * NCH + c:1 * NCH + c + 1],
                L["Af32"][:, c * C + C - 1:c * C + C])
        b.v("memset", Ac[:, 2 * NCH:2 * NCH + 1], 0.0)
        b.v("tensor_copy", Ac[:, 2 * NCH + 1:3 * NCH], Ac[:, 1 * NCH:2 * NCH - 1])
        b.v("tensor_sub", Ac[:, 3 * NCH:4 * NCH], Ac[:, 1 * NCH:2 * NCH],
            Ac[:, 2 * NCH:3 * NCH])
        b.v("tensor_sub", Ac[:, 4 * NCH:5 * NCH], Ac[:, 1 * NCH:2 * NCH], Ac[:, 0:NCH])
        b.v("tensor_sub", Ac[:, 5 * NCH:6 * NCH], Ac[:, 0:NCH], Ac[:, 2 * NCH:3 * NCH])
        b.v("tensor_scalar", Ac[:, 2 * NCH:3 * NCH], Ac[:, 0:NCH], -1.0, None,
            ALU.mult, inc="vec")
        cols_v = b.val("vec")
        b.wait("scalar", "vec", cols_v)
        b.act(Ac[:, 3 * NCH:4 * NCH], Ac[:, 3 * NCH:4 * NCH], AF.Exp)
        b.act(Ac[:, 4 * NCH:5 * NCH], Ac[:, 4 * NCH:5 * NCH], AF.Exp)
        b.act(Ac[:, 5 * NCH:6 * NCH], Ac[:, 5 * NCH:6 * NCH], AF.Exp, inc="sc")
        for c in range(NCH):
            b.act(L["scrbf"][:, c * C:(c + 1) * C], L["Af32"][:, c * C:(c + 1) * C],
                  AF.Exp, scale=-1.0, bias=Ac[:, 0 * NCH + c:0 * NCH + c + 1],
                  inc=("sc" if c == NCH - 1 else None))
        ek_sc = b.val("sc")
        for co in range(OWNC):
            c = OWNC + co
            b.act(L["scrbf2"][:, co * C:(co + 1) * C], L["Af32"][:, c * C:(c + 1) * C],
                  AF.Exp, scale=1.0, bias=Ac[:, 2 * NCH + c:2 * NCH + c + 1],
                  inc=("sc" if co == OWNC - 1 else None))
        eq_sc = b.val("sc")
        b.wait("vector", "sc", ek_sc)
        b.v("tensor_mul", L["Ktil"][:], L["kraw"][:], L["scrbf"][:])
        b.wait("vector", "sc", eq_sc)
        b.v("tensor_mul", L["Qtil"][:], L["qraw"][:], L["scrbf2"][:, 0:T_OWN])
        for c in range(NCH):
            b.v("tensor_scalar", L["Khat"][:, c * C:(c + 1) * C],
                L["Ktil"][:, c * C:(c + 1) * C],
                Ac[:, 4 * NCH + c:4 * NCH + c + 1], None, ALU.mult)
        b.v("memset", L["Sfp"][:], 0.0, inc="vec")
        prep_v = b.val("vec")
        L[f"gla_prep_v_{h}"] = prep_v

        b.wait("tensor", "vec", prep_v)
        b.wait("gpsimd", "vec", prep_v)
        for c in range(GLA_CHUNKS):
            cc = h * NCH + c
            sl = cc % 2
            bp = (c % 2) * 64
            tt = c // 2
            TB = [L["TB0"], L["TB1"]][sl]
            PD = [P0, P1][sl]
            vsl = L["vgla"][bp:bp + 64, tt * 256 + hh2 * 128: tt * 256 + (hh2 + 1) * 128]
            b.wait("tensor", "sc", evD.get(cc - 2, 0))
            b.tr(TB[bp:bp + 64, 0:128], L["Khat"][:, c * C:(c + 1) * C],
                 L["identb"][:], inc="pe")
            trD[cc] = b.val("pe")
            b.wait("scalar", "pe", trD[cc])
            b.wait("scalar", "pe", dsD.get(cc - 2, 0))
            b.scopy(L["KhatT"][bp:bp + 64, 0:128], TB[bp:bp + 64, 0:128], inc="sc")
            evD[cc] = b.val("sc")
            b.wait("tensor", "sc", evD[cc])
            b.wait("tensor", "vec", chD.get(cc - 2, 0))
            b.mm(PD[0:128, 0:128], L["KhatT"][bp:bp + 64, 0:128], vsl, inc="pe")
            dsD[cc] = b.val("pe")
            if c >= OWNC:
                co = c - OWNC
                Ptp = [L["Pt0"], L["Pt1"]][c % 2]
                vpad = [L["vpad0"], L["vpad1"]][c % 2]
                b.wait("vector", "pe", ymD[-1])
                b.v("tensor_scalar", L["Qr"][:, 0:C],
                    L["Qtil"][:, co * C:(co + 1) * C],
                    Ac[:, 5 * NCH + c:5 * NCH + c + 1], None, ALU.mult)
                b.v("tensor_copy", L["Sbf"][:], L["Sfp"][:])
                b.v("tensor_copy", vpad[bp:bp + 64, :], vsl, inc="vec")
                qr_v = b.val("vec")
                if GLA_OWN_OPS >= 2:
                    b.wait("tensor", "vec", qr_v)
                    b.wait("tensor", "sc", ptS[-1])
                    b.mm(W4[bp:bp + 64, 1024:1088], L["Ktil"][:, c * C:(c + 1) * C],
                         L["Qtil"][:, co * C:(co + 1) * C], inc="pe")
                    b.wait("scalar", "pe", b.val("pe"))
                    b.wait("scalar", "gp", ptG[-2])
                    b.scopy(Ptp[bp:bp + 64, 0:C], W4[bp:bp + 64, 1024:1088],
                            inc="sc")
                    ptS.append(b.val("sc"))
                if GLA_OWN_OPS >= 3:
                    b.wait("gpsimd", "sc", ptS[-1])
                    b.gp("affine_select", Ptp[bp:bp + 64, 0:C],
                         Ptp[bp:bp + 64, 0:C], [[1, C]], ALU.is_ge, 0.0,
                         base=0, channel_multiplier=-1, inc="gp")
                    ptG.append(b.val("gp"))
                if GLA_OWN_OPS >= 4:
                    b.wait("tensor", "gp", ptG[-1])
                    b.wait("tensor", "sc", yeS[-1])
                    b.mm(W4[0:128, 1536:1600], L["Sbf"][:], L["Qr"][:, 0:C],
                         start=True, stop=False)
                    b.mm(W4[0:128, 1536:1600], vpad[:], Ptp[:, 0:C],
                         start=False, stop=True, inc="pe")
                    ymD.append(b.val("pe"))
                if GLA_OWN_OPS >= 5:
                    b.wait("scalar", "pe", ymD[-1])
                    b.scopy(L["yq"][:, h * T_OWN + co * C: h * T_OWN + (co + 1) * C],
                            W4[0:128, 1536:1600], inc="sc")
                    yeS.append(b.val("sc"))
            b.wait("vector", "pe", dsD[cc])
            if c == OWNC:
                b.v("tensor_scalar", L["Sfp"][:], L["Sfp"][:], L["flagc"][:, 0:1],
                    None, ALU.mult)
            b.v("scalar_tensor_tensor", L["Sfp"][:], L["Sfp"][:],
                Ac[:, 3 * NCH + c:3 * NCH + c + 1], PD[0:128, 0:128],
                ALU.mult, ALU.add, inc="vec")
            chD[cc] = b.val("vec")
        L[f"gla_head_done_{h}"] = b.val("vec")
        L[f"gla_ysc_{h}"] = b.val("sc")
    L["y_sc"] = b.val("sc")
    b.tap("y", L["yq"][:], "sc", L["y_sc"])


def phase_wo(b, L):
    yq, wst, W4 = L["yq"], L["wst"], L["W4"]
    dint = L["dint"]

    b.wait("scalar", "sc", L["y_sc"])
    b.wait("tensor", "sc", L["y_sc"])
    ln_stats_fm(b, L, lambda ft, t0: yq[:, ft * T_OWN + t0: ft * T_OWN + t0 + 512],
                T_OWN, True, 1e-5)
    if WO_STAGE < 2:
        L["t1sp_ld"] = []
        for tt in range(NT_OWN):
            L[f"wo_grp_pe_{tt}"] = 0
        L["t1_gla_v"] = b.val("vec")
        return
    nrm_v = [0, 0]
    for tb in range(2):
        t0 = tb * 512
        mb = W4[:, (tb % 2) * 1024:(tb % 2) * 1024 + 512]
        rb = W4[:, (tb % 2) * 1024 + 512:(tb % 2) * 1024 + 1024]
        b.wait("tensor", "vec", nrm_v[-2])
        bcast_mm(b, L, mb, L["r0"][0:1, t0:t0 + 512])
        bcast_mm(b, L, rb, L["r1"][0:1, t0:t0 + 512])
        b.wait("vector", "pe", b.val("pe"))
        for hh in range(H):
            yt = yq[:, hh * T_OWN + t0: hh * T_OWN + t0 + 512]
            b.v("tensor_sub", yt, yt, mb)
            b.v("tensor_mul", yt, yt, rb)
            b.v("tensor_scalar", yt, yt, L["gammab"][:, hh:hh + 1],
                L["betab"][:, hh:hh + 1], ALU.mult, ALU.add, inc="vec")
        nrm_v.append(b.val("vec"))
    yn_v = b.val("vec")
    if WO_STAGE < 3:
        L["t1sp_ld"] = []
        L["t1_gla_v"] = b.val("vec")
        return

    xl = []
    b.wait_dma("sync", L["cd"]("x1"))
    for tt in range(NT_OWN):
        xl.append(b.dma("sync", L["t1a"][:, tt * D:(tt + 1) * D],
                        dint["xbf"][tt * 128:(tt + 1) * 128, :],
                        inc="wld"))
    acc_v = [0, 0, 0, 0]
    for grp in range(4):
        par = grp % 2
        if grp >= 2:
            b.wait("sync", "pe", L[f"wo_grp_pe_{grp - 2}"])
        ld = stream_wT(b, L, dint["wo_bf"], "wo", grp * 512, (grp + 1) * 512,
                       par * 8192)
        b.wait_dma("tensor", ld)
        b.wait("tensor", "vec", yn_v)
        for tt in range(NT_OWN):
            bk = tt % 4
            b.wait("tensor", "vec", acc_v[-4])
            if grp == 0:
                b.wait_dma("vector", xl[tt])
            for fc in range(FT):
                b.mm(W4[:, bk * 512:(bk + 1) * 512],
                     yq[:, fc * T_OWN + tt * 128: fc * T_OWN + (tt + 1) * 128],
                     wst[:, par * 8192 + fc * 512: par * 8192 + (fc + 1) * 512],
                     start=(fc == 0), stop=(fc == FT - 1),
                     inc=("pe" if fc == FT - 1 else None))
            b.wait("vector", "pe", b.val("pe"))
            sl1 = L["t1a"][:, tt * D + grp * 512: tt * D + grp * 512 + 512]
            b.v("tensor_add", sl1, sl1, W4[:, bk * 512:(bk + 1) * 512], inc="vec")
            acc_v.append(b.val("vec"))
        L[f"wo_grp_pe_{grp}"] = b.val("pe")
    t1_v = b.val("vec")
    if WO_STAGE >= 4:
        b.wait("sync", "vec", t1_v)
        sp = []
        for tt in range(NT_OWN):
            sp.append(b.dma("sync", dint["t1_sp"][tt * 128:(tt + 1) * 128, :],
                            L["t1a"][:, tt * D:(tt + 1) * D], inc="wld"))
        L["t1sp_ld"] = sp
    else:
        L["t1sp_ld"] = []
    b.tap("t1a", L["t1a"][:], "vec", t1_v)


def phase_ffnproj(b, L):
    xn, wst, W4, P0, P1 = L["xn"], L["wst"], L["W4"], L["P0"], L["P1"]
    dint = L["dint"]

    b.wait("scalar", "vec", L["xn_v"])
    ln_stats_fm(b, L, lambda ft, t0: xn[:, ft * T_ALL + t0: ft * T_ALL + t0 + 512],
                T_ALL, False, 1e-6)
    h_v = [0, 0]
    for tb in range(4):
        t0 = tb * 512
        rb = W4[:, (tb % 2) * 512:(tb % 2) * 512 + 512]
        b.wait("tensor", "vec", h_v[-2])
        bcast_mm(b, L, rb, L["r1"][0:1, t0:t0 + 512])
        b.wait("vector", "pe", b.val("pe"))
        for ft in range(FT):
            xt = xn[:, ft * T_ALL + t0: ft * T_ALL + t0 + 512]
            b.v("tensor_scalar", xt, xt, L["wanorm"][:, ft:ft + 1], None, ALU.mult)
            b.v("tensor_mul", xt, xt, rb, inc="vec")
        h_v.append(b.val("vec"))
    L["h_v"] = b.val("vec")

    ldk = stream_wT(b, L, dint["fk_bf"], "fk", 0, 512, 0)
    b.wait("tensor", "vec", L["h_v"])
    b.wait_dma("tensor", ldk)
    b.wait_dma("tensor", L["t1sp_ld"])
    for kh in range(HKV):
        b.wait("scalar", "vec", b.val("vec"))
        kev = []
        for half in range(4):
            ps = [P0, P1][half % 2]
            if half >= 2:
                b.wait("tensor", "sc", kev[half - 2])
            for dc in range(FT):
                b.mm(ps[:], wst[:, dc * 512 + kh * 128: dc * 512 + (kh + 1) * 128],
                     xn[:, dc * T_ALL + half * 512: dc * T_ALL + (half + 1) * 512],
                     start=(dc == 0), stop=(dc == FT - 1),
                     inc=("pe" if dc == FT - 1 else None))
            b.wait("scalar", "pe", b.val("pe"))
            b.act(L["scrbf"][:, half * 512:(half + 1) * 512], ps[:], AF.Copy, inc="sc")
            kev.append(b.val("sc"))
        rope_norm(b, L, src=L["scrbf"], dst=L["kv1"], dst_off=kh * T_ALL,
                  tlen=T_ALL, wcol=L["wknc"], src_sc=kev[-1], tpos_off=0)
    L["khat_v"] = b.val("vec")
    L["kproj_pe"] = b.val("pe")

    ldv = stream_wT(b, L, dint["fv_bf"], "fv", 0, 512, 8192)
    b.wait_dma("tensor", ldv)
    b.wait("tensor", "vec", L["khat_v"])
    vev = [0, 0]
    for tt in range(NT_ALL):
        pcol = 1024 + (tt % 2) * 512
        b.wait("tensor", "vec", vev[-2])
        for dc in range(FT):
            b.mm(W4[:, pcol:pcol + 512],
                 xn[:, dc * T_ALL + tt * 128: dc * T_ALL + (tt + 1) * 128],
                 wst[:, 8192 + dc * 512: 8192 + (dc + 1) * 512],
                 start=(dc == 0), stop=(dc == FT - 1),
                 inc=("pe" if dc == FT - 1 else None))
        b.wait("vector", "pe", b.val("pe"))
        b.v("tensor_copy", L["kv2"][:, tt * 512:(tt + 1) * 512],
            W4[:, pcol:pcol + 512], inc="vec")
        vev.append(b.val("vec"))
    L["vffn_v"] = b.val("vec")
    L["vproj_pe"] = b.val("pe")

    for qg in range(4):
        par = qg % 2
        if qg >= 2:
            b.wait("sync", "pe", L[f"fp_qg_pe_{qg - 2}"])
        elif qg == 0:
            b.wait("sync", "pe", L["kproj_pe"])
        else:
            b.wait("sync", "pe", L["vproj_pe"])
        ld = stream_wT(b, L, dint["fq_bf"], "fq", qg * 512, (qg + 1) * 512,
                       par * 8192)
        b.wait_dma("tensor", ld)
        for hi in range(4):
            qh = qg * 4 + hi
            b.wait("tensor", "vec", b.val("vec"))
            b.wait("scalar", "vec", b.val("vec"))
            for half in range(2):
                ps = [P0, P1][half]
                for dc in range(FT):
                    b.mm(ps[:], wst[:, par * 8192 + dc * 512 + hi * 128:
                                    par * 8192 + dc * 512 + (hi + 1) * 128],
                         xn[:, dc * T_ALL + T_PRI + half * 512:
                            dc * T_ALL + T_PRI + (half + 1) * 512],
                         start=(dc == 0), stop=(dc == FT - 1),
                         inc=("pe" if dc == FT - 1 and half == 1 else None))
            b.wait("scalar", "pe", b.val("pe"))
            b.act(L["scrbf"][:, 0:512], P0[:], AF.Copy)
            b.act(L["scrbf"][:, 512:1024], P1[:], AF.Copy, inc="sc")
            rope_norm(b, L, src=L["scrbf"], dst=L["yq"], dst_off=qh * T_OWN,
                      tlen=T_OWN, wcol=L["wqnc"], src_sc=b.val("sc"),
                      tpos_off=T_PRI)
        L[f"fp_qg_pe_{qg}"] = b.val("pe")
    L["qhat_v"] = b.val("vec")

    bounds(b, L)
    b.tap("qhat", L["yq"][:], "vec", L["qhat_v"])
    b.tap("khat", L["kv1"][:], "vec", L["khat_v"])
    b.tap("vffn", L["kv2"][:], "vec", L["vffn_v"])
    b.tap("negM", L["negM"][:], "vec", L["negM_v"])


def rope_norm(b, L, src, dst, dst_off, tlen, wcol, src_sc, tpos_off):
    P0, W4 = L["P0"], L["W4"]
    nb = tlen // 512
    b.wait("scalar", "sc", src_sc)
    mm_pe = [0, 0]
    for i in range(nb):
        t0 = i * 512
        b.wait("scalar", "pe", mm_pe[-2])
        b.act(L["scrbf2"][:, 512:1024], src[:, t0:t0 + 512], AF.Square, inc="sc")
        b.wait("tensor", "sc", b.val("sc"))
        b.mm(P0[0:1, :], L["ones"][:, 0:1], L["scrbf2"][:, 512:1024], inc="pe")
        mm_pe.append(b.val("pe"))
        b.wait("vector", "pe", b.val("pe"))
        r1s = L["r1"][0:1, t0:t0 + 512]
        b.v("tensor_scalar", r1s, P0[0:1, :], 1.0 / HD, None, ALU.mult)
        b.v("tensor_scalar", r1s, r1s, 1e-6, None, ALU.add)
        b.v("reciprocal", r1s, r1s, inc="vec")
        b.wait("scalar", "vec", b.val("vec"))
    b.act(ROW(L, 1, tlen), ROW(L, 1, tlen), AF.Sqrt, inc="sc")
    b.wait("tensor", "sc", b.val("sc"))
    rp = [0, 0]
    for i in range(nb):
        t0 = i * 512
        sl = i % 2
        rb = W4[:, sl * 512: sl * 512 + 512]
        sw = W4[:, 1024 + sl * 512: 1024 + sl * 512 + 512]
        b.wait("tensor", "vec", rp[-2])
        bcast_mm(b, L, rb, L["r1"][0:1, t0:t0 + 512])
        b.wait("vector", "pe", b.val("pe"))
        kn = L["scrbf2"][:, 0:512]
        tmp = L["scrbf2"][:, 512:1024]
        b.v("tensor_scalar", kn, src[:, t0:t0 + 512], wcol[:, 0:1], None, ALU.mult)
        b.v("tensor_mul", kn, kn, rb, inc="vec")
        kn_v = b.val("vec")
        b.wait("tensor", "vec", kn_v)
        b.mm(sw, L["swapm"][:], kn, inc="pe")
        b.wait("vector", "pe", b.val("pe"))
        cs = L["cosb"][:, tpos_off + t0: tpos_off + t0 + 512]
        sn = L["sinb"][:, tpos_off + t0: tpos_off + t0 + 512]
        do = dst[:, dst_off + t0: dst_off + t0 + 512]
        b.v("tensor_mul", do, kn, cs)
        b.v("tensor_mul", tmp, sw, sn)
        b.v("tensor_add", do, do, tmp, inc="vec")
        rp.append(b.val("vec"))


def bounds(b, L):
    P0 = L["P0"]
    b.wait("scalar", "vec", max(L["qhat_v"], L["khat_v"]))
    b.wait("tensor", "vec", max(L["qhat_v"], L["khat_v"]))
    mm_pe = [0, 0]

    def max_norm2(src, src_off, tlen, out_cell):
        for i in range(tlen // 512):
            t0 = i * 512
            sl = i % 2
            b.wait("scalar", "pe", mm_pe[-2])
            b.act(L["scrbf2"][:, sl * 512:(sl + 1) * 512],
                  src[:, src_off + t0: src_off + t0 + 512], AF.Square, inc="sc")
            b.wait("tensor", "sc", b.val("sc"))
            b.mm(P0[0:1, :], L["ones"][:, 0:1],
                 L["scrbf2"][:, sl * 512:(sl + 1) * 512], inc="pe")
            mm_pe.append(b.val("pe"))
            b.wait("vector", "pe", b.val("pe"))
            b.v("tensor_copy", L["r1"][0:1, t0:t0 + 512], P0[0:1, :], inc="vec")
            b.wait("tensor", "vec", b.val("vec"))
        b.v("reduce_max", out_cell, ROW(L, 1, tlen), axis=AXX, inc="vec")

    for kh in range(HKV):
        max_norm2(L["kv1"], kh * T_ALL, T_ALL, L["snorm"][0:1, 32 + kh:33 + kh])
    for qh in range(HQ):
        max_norm2(L["yq"], qh * T_OWN, T_OWN, L["snorm"][0:1, qh:qh + 1])
    for qh in range(HQ):
        kh = qh // (HQ // HKV)
        b.v("tensor_mul", L["snorm"][0:1, 48 + qh:49 + qh],
            L["snorm"][0:1, qh:qh + 1], L["snorm"][0:1, 32 + kh:33 + kh], inc="vec")
    b.wait("scalar", "vec", b.val("vec"))
    b.act(L["snorm"][0:1, 48:48 + HQ], L["snorm"][0:1, 48:48 + HQ], AF.Sqrt, inc="sc")
    b.wait("tensor", "sc", b.val("sc"))
    b.mm(L["W4"][:, 0:16], L["ones"][0:1, :], L["snorm"][0:1, 48:48 + HQ], inc="pe")
    b.wait("vector", "pe", b.val("pe"))
    b.v("tensor_scalar", L["negM"][:, 0:HQ], L["W4"][:, 0:16], -ATT_SCALE, None,
        ALU.mult)
    b.v("tensor_scalar", L["negM"][:, HQ:2 * HQ], L["negM"][:, 0:HQ],
        L["pbiasc"][:, 0:1], None, ALU.add, inc="vec")
    b.wait("tensor", "vec", b.val("vec"))
    L["negM_v"] = b.val("vec")


def phase_attn(b, L):
    P0, P1, W4 = L["P0"], L["P1"], L["W4"]
    attn = L["big2"]
    b.wait("tensor", "vec", L["negM_v"])
    b.wait("scalar", "vec", L["negM_v"])
    st_sc, pv_pe, msk_gp = {}, {}, {}
    i = 0
    for qh in range(HQ):
        kh = qh // (HQ // HKV)
        for blk in range(2):
            q0 = blk * 512
            nkt = 8 + (blk + 1) * 4
            diag0 = 8 + blk * 4
            for kt in range(nkt):
                sl = i % 2
                ps = [P0, P1][sl]
                b.wait("tensor", "sc", st_sc.get(i - 2, 0))
                b.mm(ps[:], L["kv1"][:, kh * T_ALL + kt * 128:
                                     kh * T_ALL + (kt + 1) * 128],
                     L["yq"][:, qh * T_OWN + q0: qh * T_OWN + q0 + 512], inc="pe")
                b.wait("scalar", "pe", b.val("pe"))
                b.wait("scalar", "pe", pv_pe.get(i - 2, 0))
                b.wait("scalar", "gp", msk_gp.get(i - 2, 0))
                bias = L["negM"][:, HQ + qh:HQ + qh + 1] if kt < 8 else \
                    L["negM"][:, qh:qh + 1]
                b.act(L["scrbf"][:, sl * 512:(sl + 1) * 512], ps[:], AF.Exp,
                      bias=bias, scale=ATT_SCALE, inc="sc")
                st_sc[i] = b.val("sc")
                if kt >= diag0:
                    b.wait("gpsimd", "sc", st_sc[i])
                    base = (T_PRI + q0) - kt * 128
                    b.gp("affine_select", L["scrbf"][:, sl * 512:(sl + 1) * 512],
                         L["scrbf"][:, sl * 512:(sl + 1) * 512], [[1, 512]],
                         ALU.is_ge, 0.0, base=base, channel_multiplier=-1,
                         inc="gp")
                    msk_gp[i] = b.val("gp")
                    b.wait("tensor", "gp", msk_gp[i])
                else:
                    b.wait("tensor", "sc", st_sc[i])
                b.mm(W4[:, 0:512],
                     L["kv2"][:, kt * 512 + kh * 128: kt * 512 + (kh + 1) * 128],
                     L["scrbf"][:, sl * 512:(sl + 1) * 512],
                     start=(kt == 0), stop=(kt == nkt - 1))
                b.mm(W4[0:1, 512:1024], L["ones"][:, 0:1],
                     L["scrbf"][:, sl * 512:(sl + 1) * 512],
                     start=(kt == 0), stop=(kt == nkt - 1), inc="pe")
                pv_pe[i] = b.val("pe")
                i += 1
            b.wait("scalar", "pe", b.val("pe"))
            b.act(L["denomrow"][0:1, 0:512], W4[0:1, 512:1024], AF.Ln)
            b.act(L["r1"][0:1, 0:512], L["denomrow"][0:1, 0:512], AF.Exp,
                  scale=-1.0, inc="sc")
            b.wait("tensor", "sc", b.val("sc"))
            b.mm(W4[:, 1024:1536], L["ones"][0:1, :], L["r1"][0:1, 0:512],
                 inc="pe")
            b.wait("scalar", "pe", b.val("pe"))
            b.scopy(L["scrbf2"][:, 0:512], W4[:, 1024:1536], inc="sc")
            b.wait("vector", "sc", b.val("sc"))
            b.v("tensor_mul", attn[:, qh * T_OWN + q0: qh * T_OWN + q0 + 512],
                W4[:, 0:512], L["scrbf2"][:, 0:512], inc="vec")
            b.wait("tensor", "vec", b.val("vec"))
    L["attn_v"] = b.val("vec")
    b.tap("attnp", attn[:], "vec", L["attn_v"])


def phase_hidden(b, L):
    wst, W4 = L["wst"], L["W4"]
    attn = L["big2"]
    dint = L["dint"]

    def hid_tile(tt):
        buf = L["kv1"] if tt < 4 else L["kv2"]
        off = (tt % 4) * D
        return buf[:, off:off + D]

    rl = []
    b.wait_dma("sync", L["t1sp_ld"])
    for tt in range(NT_OWN):
        rl.append(b.dma("sync", L["t1r"][:, tt * D:(tt + 1) * D],
                        dint["t1_sp"][tt * 128:(tt + 1) * 128, :], inc="wld"))
    b.wait_dma("vector", rl)
    b.wait("tensor", "vec", L["attn_v"])

    acc_v = [0, 0, 0, 0]
    xn_v = {}
    for grp in range(4):
        par = grp % 2
        if grp >= 2:
            b.wait("sync", "pe", L[f"hid_grp_pe_{grp - 2}"])
        ld = stream_wT(b, L, dint["fo_bf"], "fo", grp * 512, (grp + 1) * 512,
                       par * 8192)
        b.wait_dma("tensor", ld)
        for tt in range(NT_OWN):
            bk = tt % 4
            if grp == 0:
                if tt >= 1:
                    b.wait("sync", "vec", xn_v[tt - 1])
                xt_tok = b.dma("sync", L["xtile"][:],
                               L["din"]["x_own"][tt * 128:(tt + 1) * 128, :],
                               inc="wld")
                b.wait_dma("vector", xt_tok)
                b.wait("vector", "sc", L["statsT_sc"])
                b.v("tensor_scalar", hid_tile(tt), L["xtile"][:],
                    L["statsT"][:, 2 * tt:2 * tt + 1],
                    L["statsT"][:, 2 * tt + 1:2 * tt + 2],
                    ALU.subtract, ALU.mult)
                b.v("tensor_add", L["t1r"][:, tt * D:(tt + 1) * D],
                    L["t1r"][:, tt * D:(tt + 1) * D], hid_tile(tt), inc="vec")
                xn_v[tt] = b.val("vec")
            b.wait("tensor", "vec", acc_v[-4])
            if grp == 0:
                b.wait("tensor", "vec", xn_v[tt])
            for fc in range(FT):
                b.mm(W4[:, bk * 512:(bk + 1) * 512],
                     attn[:, fc * T_OWN + tt * 128: fc * T_OWN + (tt + 1) * 128],
                     wst[:, par * 8192 + fc * 512: par * 8192 + (fc + 1) * 512],
                     start=(fc == 0), stop=(fc == FT - 1),
                     inc=("pe" if fc == FT - 1 else None))
            b.wait("vector", "pe", b.val("pe"))
            ps = W4[:, bk * 512:(bk + 1) * 512]
            sl1 = L["t1r"][:, tt * D + grp * 512: tt * D + grp * 512 + 512]
            b.v("tensor_add", sl1, sl1, ps)
            hs = hid_tile(tt)[:, grp * 512:(grp + 1) * 512]
            b.v("tensor_add", hs, hs, ps, inc="vec")
            acc_v.append(b.val("vec"))
        L[f"hid_grp_pe_{grp}"] = b.val("pe")
    L["t1_attn_v"] = b.val("vec")

    b.wait("tensor", "vec", L["t1_attn_v"])
    trs = [0, 0]
    for tt in range(NT_OWN):
        for fc in range(FT):
            sl = (tt * FT + fc) % 2
            TB = [L["TB0"], L["TB1"]][sl]
            b.wait("tensor", "sc", trs[-2])
            b.tr(TB[:, 0:128], hid_tile(tt)[:, fc * 128:(fc + 1) * 128],
                 L["identb"][:], inc="pe")
            b.wait("scalar", "pe", b.val("pe"))
            b.scopy(L["hfm"][:, fc * T_OWN + tt * 128: fc * T_OWN + (tt + 1) * 128],
                    TB[:, 0:128], inc="sc")
            trs.append(b.val("sc"))
    hfm_sc = b.val("sc")
    L["hfm_sc"] = hfm_sc
    b.tap("hfm", L["hfm"][:], "sc", hfm_sc)

    b.wait("scalar", "sc", hfm_sc)
    b.wait("tensor", "sc", hfm_sc)
    ln_stats_fm(b, L, lambda ft, t0: L["hfm"][:, ft * T_OWN + t0:
                                              ft * T_OWN + t0 + 512],
                T_OWN, False, 1e-6)
    h2v = [0, 0]
    for tb in range(2):
        t0 = tb * 512
        rb = W4[:, (tb % 2) * 512:(tb % 2) * 512 + 512]
        b.wait("tensor", "vec", h2v[-2])
        bcast_mm(b, L, rb, L["r1"][0:1, t0:t0 + 512])
        b.wait("vector", "pe", b.val("pe"))
        for ft in range(FT):
            src = L["hfm"][:, ft * T_OWN + t0: ft * T_OWN + t0 + 512]
            dst = L["yq"][:, ft * T_OWN + t0: ft * T_OWN + t0 + 512]
            b.v("tensor_scalar", dst, src, L["wfnorm"][:, ft:ft + 1], None, ALU.mult)
            b.v("tensor_mul", dst, dst, rb, inc="vec")
        h2v.append(b.val("vec"))
    L["h2_v"] = b.val("vec")
    b.tap("h2", L["yq"][:], "vec", L["h2_v"])


def gate(b, L):
    P0 = L["P0"]
    h2 = L["yq"]
    G8, M8, GS = L["grow8"], L["masked8"], L["gsc"]
    b.wait("tensor", "vec", L["h2_v"])
    for tt in range(NT_OWN):
        for dc in range(FT):
            b.mm(P0[0:128, tt * 8:(tt + 1) * 8],
                 h2[:, dc * T_OWN + tt * 128: dc * T_OWN + (tt + 1) * 128],
                 L["gwT"][:, dc * 8:(dc + 1) * 8],
                 start=(dc == 0), stop=(dc == FT - 1),
                 inc=("pe" if dc == FT - 1 else None))
    b.wait("vector", "pe", b.val("pe"))
    for tt in range(NT_OWN):
        b.v("reduce_max", GS[:, 24 + tt:25 + tt], P0[0:128, tt * 8:(tt + 1) * 8],
            axis=AXX, negate=True, inc=("vec" if tt == NT_OWN - 1 else None))
    b.wait("scalar", "vec", b.val("vec"))
    for tt in range(NT_OWN):
        b.act(G8[:, tt * 8:(tt + 1) * 8], P0[0:128, tt * 8:(tt + 1) * 8], AF.Exp,
              bias=GS[:, 24 + tt:25 + tt], accum_out=GS[:, 16 + tt:17 + tt],
              inc=("sc" if tt == NT_OWN - 1 else None))
    b.wait("vector", "sc", b.val("sc"))
    for tt in range(NT_OWN):
        b.v("reduce_max", GS[:, tt:tt + 1], G8[:, tt * 8:(tt + 1) * 8], axis=AXX)
    for tt in range(NT_OWN):
        b.v("tensor_scalar", M8[:, tt * 8:(tt + 1) * 8], G8[:, tt * 8:(tt + 1) * 8],
            GS[:, tt:tt + 1], None, ALU.is_ge)
    for tt in range(NT_OWN):
        b.v("scalar_tensor_tensor", M8[:, tt * 8:(tt + 1) * 8],
            M8[:, tt * 8:(tt + 1) * 8], -1e30, G8[:, tt * 8:(tt + 1) * 8],
            ALU.mult, ALU.add)
    for tt in range(NT_OWN):
        b.v("reduce_max", GS[:, 8 + tt:9 + tt], M8[:, tt * 8:(tt + 1) * 8],
            axis=AXX)
    for tt in range(NT_OWN):
        b.v("tensor_add", GS[:, 24 + tt:25 + tt], GS[:, tt:tt + 1],
            GS[:, 8 + tt:9 + tt])
    for tt in range(NT_OWN):
        b.v("scalar_tensor_tensor", GS[:, 24 + tt:25 + tt],
            GS[:, 16 + tt:17 + tt], 1e-9, GS[:, 24 + tt:25 + tt],
            ALU.mult, ALU.add)
    for tt in range(NT_OWN):
        b.v("reciprocal", GS[:, 32 + tt:33 + tt], GS[:, 24 + tt:25 + tt])
    for tt in range(NT_OWN):
        b.v("scalar_tensor_tensor", L["cvals"][:, tt * 8:(tt + 1) * 8],
            G8[:, tt * 8:(tt + 1) * 8], GS[:, 8 + tt:9 + tt],
            G8[:, tt * 8:(tt + 1) * 8], ALU.is_ge, ALU.mult)
    for tt in range(NT_OWN):
        b.v("tensor_scalar", L["cvals"][:, tt * 8:(tt + 1) * 8],
            L["cvals"][:, tt * 8:(tt + 1) * 8], GS[:, 32 + tt:33 + tt], None,
            ALU.mult, inc=("vec" if tt == NT_OWN - 1 else None))
    L["cvals_v"] = b.val("vec")
    b.tap("cvals", L["cvals"][:], "vec", L["cvals_v"])


def phase_moe(b, L):
    W4, P0, P1 = L["W4"], L["P0"], L["P1"]
    h2 = L["yq"]
    act = L["hfm"]
    groups = [(0, 3), (3, 3), (6, 3), (9, 2)]

    def wtile(buf, dc):
        if buf == 0:
            return L["wst"][:, dc * 1408:(dc + 1) * 1408]
        if dc < 11:
            return L["mb1a"][:, dc * 1408:(dc + 1) * 1408]
        if dc < 13:
            return L["mb1b"][:, (dc - 11) * 1408:(dc - 10) * 1408]
        if dc < 15:
            return L["mb1c"][:, (dc - 13) * 1408:(dc - 12) * 1408]
        return L["mb1d"][:, 0:1408]

    def wdtile(buf, ic):
        if buf == 0:
            return L["wst"][:, ic * 2048:(ic + 1) * 2048]
        if ic < 8:
            return L["mb1a"][:, ic * 2048:(ic + 1) * 2048]
        if ic < 10:
            return L["mb1b"][:, (ic - 8) * 2048:(ic - 7) * 2048]
        return L["mb1c"][:, 0:2048]

    b.wait("vector", "vec", L["cvals_v"])
    b.wait("tensor", "vec", L["cvals_v"])

    pass_pe = {-2: 0, -1: 0}
    ld_done = {}
    prev_act_sc = 0
    prev_eu_v = 0
    out_v = [0] * NT_OWN

    def emit_loads(p):
        e, wi = p // 3, p % 3
        buf = p % 2
        ck = (f"wgm{e}", f"wum{e}", f"wdm{e}")[wi]
        dram = (L["dint"]["wgm_bf"], L["dint"]["wum_bf"], L["dint"]["wdm_bf"])[wi][e]
        b.wait_dma("sync", L["cd"](ck))
        b.wait("sync", "pe", pass_pe[p - 2])
        toks = []
        if wi < 2:
            for dc in range(FT):
                toks.append(b.dma("sync", wtile(buf, dc),
                                  dram[dc * 128:(dc + 1) * 128, :],
                                  inc="wld"))
        else:
            for ic in range(IC):
                toks.append(b.dma("sync", wdtile(buf, ic),
                                  dram[ic * 128:(ic + 1) * 128, :],
                                  inc="wld"))
        ld_done[p] = toks

    def emit_compute(p):
        nonlocal prev_act_sc, prev_eu_v
        e, wi = p // 3, p % 3
        buf = p % 2
        b.wait_dma("tensor", ld_done[p])
        if wi == 0 and e > 0:
            b.wait("tensor", "vec", out_v[NT_OWN - 1])
        if wi < 2:
            for g0, gn in groups:
                b.wait("tensor", "sc", prev_act_sc)
                if wi == 1:
                    b.wait("tensor", "vec", prev_eu_v)
                for dc in range(FT):
                    for gi in range(gn):
                        ic = g0 + gi
                        lhs = wtile(buf, dc)[:, ic * 128:(ic + 1) * 128]
                        for th in range(2):
                            if gi < 2:
                                ps = W4[:, gi * 1024 + th * 512:
                                        gi * 1024 + th * 512 + 512]
                            else:
                                ps = [P0, P1][th][:]
                            b.mm(ps, lhs,
                                 h2[:, dc * T_OWN + th * 512:
                                    dc * T_OWN + (th + 1) * 512],
                                 start=(dc == 0), stop=(dc == FT - 1),
                                 inc=("pe" if (dc == FT - 1 and gi == gn - 1
                                               and th == 1) else None))
                grp_pe = b.val("pe")
                for gi in range(gn):
                    ic = g0 + gi
                    for th in range(2):
                        ps = (W4[:, gi * 1024 + th * 512: gi * 1024 + th * 512 + 512]
                              if gi < 2 else [P0, P1][th][:])
                        sl = act[:, ic * T_OWN + th * 512: ic * T_OWN + (th + 1) * 512]
                        if wi == 0:
                            b.wait("scalar", "pe", grp_pe)
                            b.act(sl, ps, AF.Silu, inc="sc")
                        else:
                            b.wait("vector", "pe", grp_pe)
                            b.v("tensor_tensor", sl, sl, ps, ALU.mult, inc="vec")
                if wi == 0:
                    prev_act_sc = b.val("sc")
                else:
                    prev_eu_v = b.val("vec")
        else:
            b.wait("tensor", "vec", prev_eu_v)
            b.wait("tensor", "sc", prev_act_sc)
            for tt in range(NT_OWN):
                b.wait("tensor", "vec", out_v[tt - 1] if tt > 0 else
                       out_v[NT_OWN - 1])
                for ic in range(IC):
                    for nbk in range(4):
                        b.mm(W4[:, nbk * 512:(nbk + 1) * 512],
                             act[:, ic * T_OWN + tt * 128:
                                 ic * T_OWN + (tt + 1) * 128],
                             wdtile(buf, ic)[:, nbk * 512:(nbk + 1) * 512],
                             start=(ic == 0), stop=(ic == IC - 1),
                             inc=("pe" if ic == IC - 1 and nbk == 3 else None))
                b.wait("vector", "pe", b.val("pe"))
                b.v("scalar_tensor_tensor", L["t1r"][:, tt * D:(tt + 1) * D],
                    W4[:], L["cvals"][:, tt * 8 + e: tt * 8 + e + 1],
                    L["t1r"][:, tt * D:(tt + 1) * D], ALU.mult, ALU.add, inc="vec")
                out_v[tt] = b.val("vec")
        pass_pe[p] = b.val("pe")

    NP = 3 * E
    emit_loads(0)
    for p in range(1, NP):
        emit_loads(p)
        emit_compute(p - 1)
    emit_compute(NP - 1)
    for tt in range(NT_OWN):
        L[f"out_tile_v_{tt}"] = out_v[tt]


def finish(b, L):
    toks = list(getattr(b, "tap_tokens", []))
    if BUILD_PHASES >= 6:
        for tt in range(NT_OWN):
            b.wait("gpsimd", "vec", L[f"out_tile_v_{tt}"])
            toks.append(b.dma("gpsimd", L["out_own"][tt * 128:(tt + 1) * 128, :],
                              L["t1r"][:, tt * D:(tt + 1) * D], inc="cast"))
    else:
        b.barrier()
        src_t = L["yq"] if (BUILD_PHASES < 3 or WO_STAGE < 3) else (
            L["t1a"] if BUILD_PHASES < 4 else L["t1r"])
        for tt in range(NT_OWN):
            toks.append(b.dma("gpsimd", L["out_own"][tt * 128:(tt + 1) * 128, :],
                              src_t[:, tt * D:(tt + 1) * D], inc="cast"))
    b.wait_dma("gpsimd", toks)
    b.wait_dma("sync", toks)



_NC_CACHE = {}

def kernel(**inputs):
    key = (BUILD_PHASES, tuple(DEBUG_TAPS))
    if key not in _NC_CACHE:
        _NC_CACHE[key] = build_nc()
    nc = _NC_CACHE[key]

    x = np.ascontiguousarray(np.asarray(inputs["L0_attn_x"]), dtype=np.float32)
    freqs = 1.0 / ROPE_THETA ** (np.arange(0, HD, 2, dtype=np.float32) / HD)
    ident = np.eye(128, dtype=np.float32)

    def tcols(w):
        return np.ascontiguousarray(np.asarray(w, np.float32).reshape(FT, 128).T)

    base = {
        "gamma_t": tcols(inputs["L0_attn_gamma"]),
        "beta_t": tcols(inputs["L0_attn_beta"]),
        "w_anorm_t": tcols(inputs["L0_ffn_attn_norm_w"]),
        "w_fnorm_t": tcols(inputs["L0_ffn_ffn_norm_w"]),
        "w_qn_t": np.ascontiguousarray(
            np.asarray(inputs["L0_ffn_q_norm_w"], np.float32).reshape(128, 1)),
        "w_kn_t": np.ascontiguousarray(
            np.asarray(inputs["L0_ffn_k_norm_w"], np.float32).reshape(128, 1)),
        "identf": ident,
        "swapf": np.ascontiguousarray(
            np.eye(128, dtype=np.float32)[:, list(range(64, 128)) + list(range(64))]),
    }
    for k in ("L0_attn_Wq", "L0_attn_Wk", "L0_attn_Wv", "L0_attn_Wg", "L0_attn_Wo",
              "L0_ffn_q_w", "L0_ffn_k_w", "L0_ffn_v_w", "L0_ffn_o_w"):
        base[k] = np.ascontiguousarray(np.asarray(inputs[k], np.float32).T)
    base["L0_ffn_gate_w"] = np.ascontiguousarray(
        np.asarray(inputs["L0_ffn_gate_w"], np.float32))
    for k in ("L0_ffn_wg", "L0_ffn_wu", "L0_ffn_wd"):
        base[k] = np.ascontiguousarray(
            np.asarray(inputs[k], np.float32).transpose(0, 2, 1))

    in_maps = []
    for core in range(8):
        bb, half = core // 2, core % 2
        pos = np.concatenate([np.arange(0, T_PRI, dtype=np.float32),
                              np.arange(half * T_OWN, half * T_OWN + T_OWN,
                                        dtype=np.float32)])
        f = np.outer(pos, freqs)
        m = dict(base)
        xa = np.concatenate([x[bb, 0:T_PRI],
                             x[bb, half * T_OWN:(half + 1) * T_OWN]], 0)
        m["xT_all"] = np.ascontiguousarray(xa.T)
        m["x_own"] = np.ascontiguousarray(x[bb, half * T_OWN:(half + 1) * T_OWN])
        cosT = np.cos(f).T.astype(np.float32)
        sinT = np.sin(f).T.astype(np.float32)
        m["cosA"] = np.ascontiguousarray(np.concatenate([cosT, cosT], 0))
        m["sinA"] = np.ascontiguousarray(np.concatenate([-sinT, sinT], 0))
        m["flag_col"] = np.full((128, 1), 1.0 if half == 1 else 0.0, np.float32)
        m["pbias_col"] = np.full((128, 1), 0.0 if half == 1 else -1e30, np.float32)
        in_maps.append(m)

    res = run_bass_kernel_spmd(nc, in_maps, core_ids=list(range(8)))
    out = np.empty((4, T_ALL, D), dtype=np.float32)
    for core in range(8):
        bb, half = core // 2, core % 2
        out[bb, half * T_OWN:(half + 1) * T_OWN] = res.results[core]["out_own"]
    kernel._last_results = res
    return out


# version 07
# speedup vs baseline: 1.0644x; 1.0255x over previous
import numpy as np
import concourse.bass as bass
import concourse.mybir as mybir
from concourse.bass_utils import run_bass_kernel_spmd
from contextlib import ExitStack

F32 = mybir.dt.float32
BF16 = mybir.dt.bfloat16
AF = mybir.ActivationFunctionType
ALU = mybir.AluOpType
AXX = mybir.AxisListType.X

D = 2048
FT = 16
H = 16
KD = 128
HQ, HKV, HD = 16, 4, 128
E, I = 8, 1408
IC = 11
C = 64
T_OWN = 1024
T_PRI = 1024
T_ALL = 2048
NT_OWN = 8
NT_ALL = 16
NCH = T_ALL // C
OWNC = T_OWN // C
ROPE_THETA = 1e6
QK_SCALE = float(KD) ** (-0.5)
ATT_SCALE = float(HD) ** (-0.5)

BUILD_PHASES = 6
DEBUG_TAPS = []
GLA_HEADS = H
GLA_CHUNKS = NCH
GLA_OWN_OPS = 5
WO_STAGE = 4

TAP_SPECS = {
    "xn": ([128, FT * T_ALL], BF16),
    "rows": ([1, T_ALL], BF16),
    "statsT": ([128, 2 * NT_OWN], F32),
    "y": ([128, H * T_OWN], BF16),
    "t1a": ([128, NT_OWN * D], BF16),
    "qhat": ([128, HQ * T_OWN], BF16),
    "khat": ([128, HKV * T_ALL], BF16),
    "vffn": ([128, NT_ALL * 512], BF16),
    "attnp": ([128, HQ * T_OWN], BF16),
    "h2": ([128, H * T_OWN], BF16),
    "cvals": ([128, NT_OWN * 8], F32),
    "negM": ([128, 2 * HQ], F32),
    "hfm": ([128, H * T_OWN], BF16),
}


class B:

    def __init__(self, nc, sems, tapout):
        self.nc = nc
        self.sems = sems
        self.tapout = tapout
        self.cur = None
        self.eng = None
        self.count = None

    def begin(self, section, eng):
        self.cur = section
        self.eng = eng
        self.count = {k: 0 for k in self.sems}
        self.dq = {
            "cast": {"sems": [f"cq{i}" for i in range(4)], "n": 4, "next": 0},
            "wld": {"sems": [f"wq{i}" for i in range(6)], "n": 6, "next": 0},
        }
        self.tap_tokens = []

    def val(self, sem):
        return self.count[sem]

    def _fin(self, instr, inc, n):
        if inc is not None:
            self.count[inc] += n
            if instr is not None:
                instr.then_inc(self.sems[inc], n)

    def wait(self, section, sem, v):
        if v <= 0:
            return
        if self.cur == section:
            self.eng.wait_ge(self.sems[sem], v)

    def dma(self, section, out, in_, inc, transpose=False):
        q = self.dq[inc]
        lane = q["next"] % q["n"]
        q["next"] += 1
        sem = q["sems"][lane]
        prev = self.count[sem]
        if prev > 0 and self.cur == section:
            self.eng.wait_ge(self.sems[sem], prev)
        instr = None
        if self.cur == section:
            instr = self.eng.dma_start(out=out, in_=in_, transpose=transpose)
        self.count[sem] += 16
        if instr is not None:
            instr.then_inc(self.sems[sem], 16)
        return (sem, self.count[sem])

    def wait_dma(self, section, tokens):
        if isinstance(tokens, tuple):
            tokens = [tokens]
        best = {}
        for sem, v in tokens:
            best[sem] = max(best.get(sem, 0), v)
        if self.cur == section:
            for sem, v in best.items():
                self.eng.wait_ge(self.sems[sem], v)

    def mm(self, out, lhsT, rhs, start=True, stop=True, inc=None):
        instr = None
        if self.cur == "tensor":
            instr = self.eng.matmul(out, lhsT, rhs, start=start, stop=stop)
        self._fin(instr, inc, 1)

    def tr(self, out, in_, ident, inc=None):
        instr = None
        if self.cur == "tensor":
            instr = self.eng.transpose(out, in_, ident)
        self._fin(instr, inc, 1)

    def act(self, out, in_, func, bias=0.0, scale=1.0, accum_out=None, inc=None):
        instr = None
        if self.cur == "scalar":
            instr = self.eng.activation(out, in_, func, bias=bias, scale=scale,
                                        accum_out=accum_out)
        self._fin(instr, inc, 1)

    def scopy(self, out, in_, inc=None):
        instr = None
        if self.cur == "scalar":
            instr = self.eng.copy(out, in_)
        self._fin(instr, inc, 1)

    def v(self, method, *args, inc=None, **kw):
        instr = None
        if self.cur == "vector":
            instr = getattr(self.eng, method)(*args, **kw)
        self._fin(instr, inc, 1)

    def gp(self, method, *args, inc=None, **kw):
        instr = None
        if self.cur == "gpsimd":
            instr = getattr(self.eng, method)(*args, **kw)
        self._fin(instr, inc, 1)

    def sem_inc(self, section, sem, n=1):
        if self.cur == section:
            self.eng.nop().then_inc(self.sems[sem], n)
        self.count[sem] += n

    def marker(self, section, sem, junk, ones):
        instr = None
        if self.cur == section:
            if section == "vector":
                instr = self.eng.engine_nop()
            elif section == "scalar":
                instr = self.eng.copy(junk[0:1, 0:1], ones[0:1, 0:1])
            elif section == "gpsimd":
                instr = self.eng.memset(junk[0:1, 4:5], 0.0)
        self.count[sem] += 1
        if instr is not None:
            instr.then_inc(self.sems[sem], 1)

    def barrier(self):
        self.wait_dma("sync", list(self.tap_tokens)) if self.tap_tokens else None
        L = self._L
        self.marker("vector", "vec", L["junk"], L["ones"])
        self.marker("scalar", "sc", L["junk"], L["ones"])
        self.marker("gpsimd", "gp", L["junk"], L["ones"])
        self.sem_inc("sync", "sy", 1)
        targets = {"vec": self.count["vec"], "sc": self.count["sc"],
                   "gp": self.count["gp"], "sy": self.count["sy"],
                   "pe": self.count["pe"]}
        own = {"sync": "sy", "gpsimd": "gp", "tensor": "pe",
               "vector": "vec", "scalar": "sc"}
        if self.cur in own:
            for sem, v in targets.items():
                if sem != own[self.cur]:
                    self.eng.wait_ge(self.sems[sem], v)

    def tap(self, name, ap, sem, v):
        if name not in DEBUG_TAPS:
            return
        self.wait("sync", sem, v)
        tok = self.dma("sync", self.tapout[name], ap, inc="wld")
        if not hasattr(self, "tap_tokens"):
            self.tap_tokens = []
        self.tap_tokens.append(tok)


ORG = 16512

MEM = [
    ("xn",     [128, FT * T_ALL], BF16, 0),
    ("big2",   [128, H * T_OWN],  BF16, 0),
    ("hfm",    [128, H * T_OWN],  BF16, 0),
    ("t1r",    [128, NT_OWN * D], BF16, 32768),
    ("yq",     [128, H * T_OWN],  BF16, 65536),
    ("gate8b", [8, D],            BF16, 98304),
    ("gf32",   [128, T_ALL],      F32,  98304),
    ("Af32",   [128, T_ALL],      F32,  106496),
    ("kraw",   [128, T_ALL],      BF16, 114688),
    ("Ktil",   [128, T_ALL],      BF16, 118784),
    ("qraw",   [128, T_OWN],      BF16, 122880),
    ("Qtil",   [128, T_OWN],      BF16, 124928),
    ("Khat",   [128, T_ALL],      BF16, 126976),
    ("t1a",    [128, NT_OWN * D], BF16, 98304),
    ("kv1",    [128, HKV * T_ALL], BF16, 98304),
    ("kv2",    [128, NT_ALL * 512], BF16, 114688),
    ("mb1a",   [128, 16384],      BF16, 98304),
    ("vgla",   [128, NT_ALL * 256], BF16, 131072),
    ("xtile",  [128, D],          F32,  131072),
    ("mb1b",   [128, 4096],       BF16, 131072),
    ("cosb",   [128, T_ALL],      BF16, 139264),
    ("sinb",   [128, T_ALL],      BF16, 143360),
    ("mb1c",   [128, 4096],       BF16, 139264),
    ("wst",    [128, 22528],      BF16, 147456),
    ("scrbf",  [128, 2048],       BF16, 192512),
    ("scrbf2", [128, 1024],       BF16, 196608),
    ("mb1d",   [128, 2048],       BF16, 192512),
    ("mb1e",   [128, 2048],       BF16, 198656),
    ("statsT", [128, 2 * NT_OWN], F32,  198656),
    ("cvals",  [128, NT_OWN * 8], F32,  198720),
    ("gwT",    [128, FT * 8],     BF16, 198976),
    ("r0",     [1, T_ALL],        BF16, 200704),
    ("r1",     [1, T_ALL],        BF16, 204800),
    ("mb1e",   [128, 2048],       BF16, 200704),
    ("Acols",  [128, 6 * NCH],    F32,  208896),
    ("KhatT",  [128, 128],        BF16, 209664),
    ("Qr",     [128, C],          BF16, 209920),
    ("Pt0",    [128, C],          BF16, 210048),
    ("Pt1",    [128, C],          BF16, 210176),
    ("vpad0",  [128, 128],        BF16, 210304),
    ("vpad1",  [128, 128],        BF16, 210560),
    ("Sfp",    [128, 128],        F32,  210816),
    ("denomrow", [1, 512],        F32,  208896),
    ("negM",   [128, 2 * HQ],     F32,  210944),
    ("snorm",  [1, 64],           BF16, 211072),
    ("grow",   [128, 8],          F32,  211200),
    ("gcol",   [128, 16],         F32,  211232),
    ("grow8",  [128, 64],         F32,  208896),
    ("masked8", [128, 64],        F32,  209152),
    ("gsc",    [128, 40],         F32,  209408),
    ("Sbf",    [128, 128],        BF16, 211392),
    ("ones",   [128, 128],        BF16, 211648),
    ("identb", [128, 128],        BF16, 211904),
    ("gammab", [128, FT],         F32,  212160),
    ("betab",  [128, FT],         F32,  212224),
    ("wanorm", [128, FT],         F32,  212288),
    ("wfnorm", [128, FT],         F32,  212352),
    ("wqnc",   [128, 1],          F32,  212416),
    ("wknc",   [128, 1],          F32,  212448),
    ("flagc",  [128, 1],          F32,  212480),
    ("pbiasc", [128, 1],          F32,  212512),
    ("swapm",  [128, 128],        BF16, 212544),
    ("junk",   [128, 8],          F32,  212800),
]
WST_QKG = [0, 6144]
WST_V = [12288, 16384]
WST_ZERO = 20480


def build_nc():
    nc = bass.Bass("TRN2", target_bir_lowering=False, debug=False,
                   detect_race_conditions=False)

    din = {}
    def dinp(name, shape, dtype=F32):
        din[name] = nc.dram_tensor(name, list(shape), dtype, kind="ExternalInput").ap()

    dinp("xT_all", [D, T_ALL]); dinp("x_own", [T_OWN, D])
    dinp("L0_attn_Wq", [D, D]); dinp("L0_attn_Wk", [D, D])
    dinp("L0_attn_Wv", [D, D]); dinp("L0_attn_Wg", [D, D])
    dinp("L0_attn_Wo", [D, D])
    dinp("gamma_t", [128, FT]); dinp("beta_t", [128, FT])
    dinp("w_anorm_t", [128, FT]); dinp("w_fnorm_t", [128, FT])
    dinp("w_qn_t", [128, 1]); dinp("w_kn_t", [128, 1])
    dinp("L0_ffn_q_w", [D, D]); dinp("L0_ffn_k_w", [D, HKV * HD])
    dinp("L0_ffn_v_w", [D, HKV * HD]); dinp("L0_ffn_o_w", [HQ * HD, D])
    dinp("L0_ffn_gate_w", [E, D])
    dinp("L0_ffn_wg", [E, D, I]); dinp("L0_ffn_wu", [E, D, I])
    dinp("L0_ffn_wd", [E, I, D])
    dinp("identf", [128, 128])
    dinp("cosA", [128, T_ALL]); dinp("sinA", [128, T_ALL])
    dinp("swapf", [128, 128])
    dinp("flag_col", [128, 1]); dinp("pbias_col", [128, 1])

    out_own = nc.dram_tensor("out_own", [T_OWN, D], F32, kind="ExternalOutput").ap()

    dint = {}
    def dmk(name, shape):
        dint[name] = nc.dram_tensor(name, list(shape), BF16, kind="Internal").ap()

    dmk("xbf", [T_OWN, D]); dmk("xbfT", [D, T_ALL])
    dmk("t1_sp", [T_OWN, D])

    ctx = ExitStack()
    with ctx:
        sb = {}
        for name, shape, dtype, off in MEM:
            sb[name] = nc.alloc_sbuf_tensor_at(name, list(shape), dtype,
                                               offset=ORG + off)

        def psum(name, shape, dtype=F32):
            sb[name] = ctx.enter_context(nc.psum_tensor(name, list(shape), dtype))
        psum("W4", [128, 2048])
        psum("P0", [128, 512])
        psum("P1", [128, 512])
        psum("TB0", [128, 512], BF16)
        psum("TB1", [128, 512], BF16)

        cast_list = [
            ("ident", sb["identb"][:], din["identf"]),
            ("swap", sb["swapm"][:], din["swapf"]),
            ("gate", sb["gate8b"][:], din["L0_ffn_gate_w"]),
            ("cos", sb["cosb"][:], din["cosA"]),
            ("sin", sb["sinb"][:], din["sinA"]),
            ("x0", dint["xbfT"], din["xT_all"]),
            ("x1", dint["xbf"], din["x_own"]),
        ]
        cast_idx = {name: i for i, (name, _, _) in enumerate(cast_list)}

        sems = {}
        for s in ("pe", "vec", "sc", "gp", "sy",
                  "cq0", "cq1", "cq2", "cq3",
                  "wq0", "wq1", "wq2", "wq3", "wq4", "wq5"):
            sems[s] = ctx.enter_context(nc.semaphore(s))

        tapout = {}
        for name in DEBUG_TAPS:
            shape, dt = TAP_SPECS[name]
            tapout[name] = nc.dram_tensor("tap_" + name, list(shape), dt,
                                          kind="ExternalOutput").ap()

        b = B(nc, sems, tapout)
        b._L = sb
        L0 = dict(sb)
        L0.update(din=din, dint=dint, out_own=out_own,
                  cast_list=cast_list, cast_idx=cast_idx)

        counts = []

        def run_section(name, eng):
            b.begin(name, eng)
            with nc.allow_low_precision(reason="bf16 operand storage by design"):
                program(b, dict(L0))
            counts.append(dict(b.count))
            if len(counts) > 1:
                assert counts[-1] == counts[0], \
                    f"walk divergence in section {name}"

        with nc.Block() as block:
            @block.sync
            def _(eng):
                run_section("sync", eng)

            @block.gpsimd
            def _(eng):
                run_section("gpsimd", eng)

            @block.tensor
            def _(eng):
                run_section("tensor", eng)

            @block.vector
            def _(eng):
                run_section("vector", eng)

            @block.scalar
            def _(eng):
                run_section("scalar", eng)

    return nc


def ROW(L, r, n=T_ALL):
    return L[("r0", "r1")[r]][0:1, 0:n]



def program(b, L):
    din = L["din"]

    cast_tok = {}
    for name, out_ap, in_ap in L["cast_list"]:
        cast_tok[name] = b.dma("gpsimd", out_ap, in_ap, inc="cast")
    L["cd"] = lambda key: cast_tok[key]

    consts_ld = []
    for nm, key in (("gammab", "gamma_t"), ("betab", "beta_t"),
                    ("wanorm", "w_anorm_t"), ("wfnorm", "w_fnorm_t"),
                    ("wqnc", "w_qn_t"), ("wknc", "w_kn_t"),
                    ("flagc", "flag_col"), ("pbiasc", "pbias_col")):
        consts_ld.append(b.dma("sync", L[nm][:], din[key], inc="wld"))

    b.wait_dma("vector", consts_ld)
    b.v("memset", L["ones"][:], 1.0)
    b.v("memset", L["wst"][:, WST_ZERO:WST_ZERO + 2048], 0.0, inc="vec")
    L["consts_v"] = b.val("vec")

    phase1(b, L)
    b.barrier()
    if BUILD_PHASES >= 2:
        phase_gla(b, L)
        b.barrier()
    if BUILD_PHASES >= 3:
        phase_wo(b, L)
        b.barrier()
    if BUILD_PHASES >= 4:
        phase_ffnproj(b, L)
        b.barrier()
        phase_attn(b, L)
        b.barrier()
    if BUILD_PHASES >= 5:
        phase_hidden(b, L)
        b.barrier()
        gate(b, L)
        b.barrier()
    if BUILD_PHASES >= 6:
        phase_moe(b, L)
    finish(b, L)


def ln_stats_fm(b, L, src_getter, tlen, with_mean, eps):
    P0, P1 = L["P0"], L["P1"]
    sq_pe = [0, 0]
    for tb in range(tlen // 512):
        t0 = tb * 512
        for ft in range(FT):
            xt = src_getter(ft, t0)
            scr = [L["scrbf"], L["scrbf2"]][ft % 2]
            b.wait("scalar", "pe", sq_pe[-2])
            b.act(scr[:, 0:512], xt, AF.Square, inc="sc")
            b.wait("tensor", "sc", b.val("sc"))
            if with_mean:
                b.mm(P0[0:1, :], L["ones"][:, 0:1], xt,
                     start=(ft == 0), stop=(ft == FT - 1))
            b.mm(P1[0:1, :], L["ones"][:, 0:1], scr[:, 0:512],
                 start=(ft == 0), stop=(ft == FT - 1), inc="pe")
            sq_pe.append(b.val("pe"))
        b.wait("vector", "pe", b.val("pe"))
        r0s = L["r0"][0:1, t0:t0 + 512]
        r1s = L["r1"][0:1, t0:t0 + 512]
        if with_mean:
            b.v("tensor_scalar", r0s, P0[0:1, :], 1.0 / D, None, ALU.mult)
        b.v("tensor_scalar", r1s, P1[0:1, :], 1.0 / D, None, ALU.mult)
        if with_mean:
            b.v("tensor_mul", P1[0:1, :], r0s, r0s)
            b.v("tensor_sub", r1s, r1s, P1[0:1, :])
        b.v("tensor_scalar", r1s, r1s, eps, None, ALU.add)
        b.v("reciprocal", r1s, r1s, inc="vec")
        b.wait("tensor", "vec", b.val("vec"))
    b.wait("scalar", "vec", b.val("vec"))
    b.act(ROW(L, 1, tlen), ROW(L, 1, tlen), AF.Sqrt, inc="sc")
    b.wait("tensor", "sc", b.val("sc"))
    b.wait("vector", "sc", b.val("sc"))


def bcast_mm(b, L, psum_slice, row_ap):
    b.mm(psum_slice, L["ones"][0:1, :], row_ap, inc="pe")


def phase1(b, L):
    xn, W4, P0, P1 = L["xn"], L["W4"], L["P0"], L["P1"]
    xbf = L["dint"]["xbf"]

    b.wait_dma("tensor", [L["cd"]("ident"), L["cd"]("gate")])
    gtr = [0, 0]
    for dc in range(FT):
        TB = [L["TB0"], L["TB1"]][dc % 2]
        b.wait("tensor", "sc", gtr[-2])
        b.tr(TB[0:128, 0:8], L["gate8b"][0:8, dc * 128:(dc + 1) * 128],
             L["identb"][0:8, 0:8], inc="pe")
        b.wait("scalar", "pe", b.val("pe"))
        b.scopy(L["gwT"][:, dc * 8:(dc + 1) * 8], TB[0:128, 0:8], inc="sc")
        gtr.append(b.val("sc"))

    b.wait_dma("sync", [L["cd"]("x0")])
    xbfT = L["dint"]["xbfT"]
    xld = []
    for ft in range(FT):
        xld.append(b.dma("sync", xn[:, ft * T_ALL:(ft + 1) * T_ALL],
                         xbfT[ft * 128:(ft + 1) * 128, :], inc="wld"))
    b.wait_dma("scalar", xld)
    b.wait_dma("tensor", xld)
    b.wait("tensor", "vec", L["consts_v"])
    ln_stats_fm(b, L, lambda ft, t0: xn[:, ft * T_ALL + t0: ft * T_ALL + t0 + 512],
                T_ALL, True, 1e-5)

    nrm_v = [0, 0]
    for tb in range(4):
        t0 = tb * 512
        mb = W4[:, (tb % 2) * 1024:(tb % 2) * 1024 + 512]
        rb = W4[:, (tb % 2) * 1024 + 512:(tb % 2) * 1024 + 1024]
        b.wait("tensor", "vec", nrm_v[-2])
        bcast_mm(b, L, mb, L["r0"][0:1, t0:t0 + 512])
        bcast_mm(b, L, rb, L["r1"][0:1, t0:t0 + 512])
        b.wait("vector", "pe", b.val("pe"))
        for ft in range(FT):
            xt = xn[:, ft * T_ALL + t0: ft * T_ALL + t0 + 512]
            b.v("tensor_sub", xt, xt, mb)
            b.v("tensor_mul", xt, xt, rb, inc="vec")
        nrm_v.append(b.val("vec"))
    L["xn_v"] = b.val("vec")

    b.wait_dma("tensor", L["cd"]("ident"))
    b.wait("tensor", "vec", L["xn_v"])
    prev = [0, 0]
    for tt in range(NT_OWN):
        t0 = T_PRI + tt * 128
        TB = [L["TB0"], L["TB1"]][tt % 2]
        b.wait("tensor", "sc", prev[-2])
        b.tr(TB[0:128, 0:1], L["r0"][0:1, t0:t0 + 128],
             L["identb"][0:1, 0:1], inc="pe")
        b.tr(TB[0:128, 2:3], L["r1"][0:1, t0:t0 + 128],
             L["identb"][0:1, 0:1], inc="pe")
        b.wait("scalar", "pe", b.val("pe"))
        b.scopy(L["statsT"][:, 2 * tt:2 * tt + 1], TB[0:128, 0:1])
        b.scopy(L["statsT"][:, 2 * tt + 1:2 * tt + 2], TB[0:128, 2:3], inc="sc")
        prev.append(b.val("sc"))
    L["statsT_sc"] = b.val("sc")

    b.tap("xn", xn[:], "vec", L["xn_v"])
    b.tap("rows", L["r0"][:], "vec", L["xn_v"])
    b.tap("statsT", L["statsT"][:], "sc", L["statsT_sc"])


def stream_wT(b, L, dram_f32, ckey, f0, f1, col0):
    w = f1 - f0
    toks = []
    for dc in range(FT):
        toks.append(b.dma("gpsimd",
                          L["wst"][:, col0 + dc * w: col0 + (dc + 1) * w],
                          dram_f32[dc * 128:(dc + 1) * 128, f0:f1], inc="cast"))
    return toks


def phase_gla(b, L):
    xn, Ac, wst = L["xn"], L["Acols"], L["wst"]
    W4, P0, P1 = L["W4"], L["P0"], L["P1"]
    dint = L["dint"]

    b.wait("tensor", "vec", L["xn_v"])
    b.wait_dma("tensor", L["cd"]("ident"))
    b.v("memset", L["vpad0"][:], 0.0)
    b.v("memset", L["vpad1"][:], 0.0)
    b.v("memset", L["Pt0"][:], 0.0)
    b.v("memset", L["Pt1"][:], 0.0, inc="vec")

    trD, evD, dsD, chD = {}, {}, {}, {}
    ptS, ptG, ymD, yeS = [0, 0], [0, 0], [0, 0], [0, 0]
    for h in range(GLA_HEADS):
        f0, f1 = h * 128, (h + 1) * 128
        par = h % 2
        hh2 = h % 2
        qkg0 = WST_QKG[par]

        if h >= 2:
            b.wait("gpsimd", "pe", L[f"gla_wused_{h - 2}"])
        ldq = stream_wT(b, L, L["din"]["L0_attn_Wq"], "wq", f0, f1, qkg0 + 0)
        ldk = stream_wT(b, L, L["din"]["L0_attn_Wk"], "wk", f0, f1, qkg0 + 2048)
        ldg = stream_wT(b, L, L["din"]["L0_attn_Wg"], "wg", f0, f1, qkg0 + 4096)

        if h % 2 == 0:
            grp = h // 2
            v0 = WST_V[par]
            if h >= 2:
                b.wait("gpsimd", "pe", L[f"gla_vused_{h - 2}"])
            ldv = stream_wT(b, L, L["din"]["L0_attn_Wv"], "wv", grp * 256, (grp + 1) * 256,
                            v0)
            b.wait_dma("tensor", ldv)
            if h > 0:
                b.wait("tensor", "vec", L[f"gla_head_done_{h - 1}"])
                b.wait("tensor", "sc", L[f"gla_ysc_{h - 1}"])
            vev = [0, 0]
            for tt in range(NT_ALL):
                pcol = (tt % 2) * 1024
                b.wait("tensor", "vec", vev[-2])
                for dc in range(FT):
                    b.mm(W4[:, pcol:pcol + 256],
                         xn[:, dc * T_ALL + tt * 128: dc * T_ALL + (tt + 1) * 128],
                         wst[:, v0 + dc * 256: v0 + (dc + 1) * 256],
                         start=(dc == 0), stop=(dc == FT - 1),
                         inc=("pe" if dc == FT - 1 else None))
                b.wait("vector", "pe", b.val("pe"))
                b.v("tensor_copy", L["vgla"][:, tt * 256:(tt + 1) * 256],
                    W4[:, pcol:pcol + 256], inc="vec")
                vev.append(b.val("vec"))
            L[f"gla_vused_{h}"] = b.val("pe")
            L[f"gla_vused_{h + 1}"] = b.val("pe")
            L["gla_vgrp_v"] = b.val("vec")

        b.wait_dma("tensor", ldq)
        if h >= 1:
            b.wait("tensor", "vec", L[f"gla_head_done_{h - 1}"])
            b.wait("scalar", "vec", L[f"gla_prep_v_{h - 1}"])
        if h % 2 == 0:
            b.wait("tensor", "vec", L["gla_vgrp_v"])
        for half in range(2):
            ps = [P0, P1][half]
            for dc in range(FT):
                b.mm(ps[:], wst[:, qkg0 + dc * 128: qkg0 + (dc + 1) * 128],
                     xn[:, dc * T_ALL + T_PRI + half * 512:
                        dc * T_ALL + T_PRI + (half + 1) * 512],
                     start=(dc == 0), stop=(dc == FT - 1),
                     inc=("pe" if dc == FT - 1 and half == 1 else None))
        b.wait("scalar", "pe", b.val("pe"))
        b.act(L["qraw"][:, 0:512], P0[:], AF.Copy, scale=QK_SCALE)
        b.act(L["qraw"][:, 512:1024], P1[:], AF.Copy, scale=QK_SCALE, inc="sc")
        qevac = b.val("sc")
        b.wait_dma("tensor", ldk)
        kevacs = []
        for half in range(4):
            ps = [P0, P1][half % 2]
            b.wait("tensor", "sc", qevac if half < 2 else kevacs[half - 2])
            for dc in range(FT):
                b.mm(ps[:], wst[:, qkg0 + 2048 + dc * 128: qkg0 + 2048 + (dc + 1) * 128],
                     xn[:, dc * T_ALL + half * 512: dc * T_ALL + (half + 1) * 512],
                     start=(dc == 0), stop=(dc == FT - 1),
                     inc=("pe" if dc == FT - 1 else None))
            b.wait("scalar", "pe", b.val("pe"))
            b.act(L["kraw"][:, half * 512:(half + 1) * 512], ps[:], AF.Copy, inc="sc")
            kevacs.append(b.val("sc"))
        b.wait_dma("tensor", ldg)
        gevacs = []
        for half in range(4):
            ps = [P0, P1][half % 2]
            b.wait("tensor", "sc", kevacs[2 + half % 2] if half < 2 else gevacs[half - 2])
            for dc in range(FT):
                b.mm(ps[:], wst[:, qkg0 + 4096 + dc * 128: qkg0 + 4096 + (dc + 1) * 128],
                     xn[:, dc * T_ALL + half * 512: dc * T_ALL + (half + 1) * 512],
                     start=(dc == 0), stop=(dc == FT - 1),
                     inc=("pe" if dc == FT - 1 else None))
            b.wait("scalar", "pe", b.val("pe"))
            b.act(L["gf32"][:, half * 512:(half + 1) * 512], ps[:], AF.Sigmoid, inc="sc")
            gevacs.append(b.val("sc"))
        b.act(L["gf32"][:], L["gf32"][:], AF.Ln, inc="sc")
        lsig_sc = b.val("sc")
        L[f"gla_wused_{h}"] = b.val("pe")

        b.wait("vector", "sc", lsig_sc)
        b.v("tensor_tensor_scan", L["Af32"][:], L["gf32"][:],
            wst[:, WST_ZERO:WST_ZERO + 2048], 0.0, ALU.add, ALU.add, inc="vec")
        for c in range(NCH):
            b.v("tensor_copy", Ac[:, 0 * NCH + c:0 * NCH + c + 1],
                L["Af32"][:, c * C + C // 2 - 1:c * C + C // 2])
            b.v("tensor_copy", Ac[:, 1 * NCH + c:1 * NCH + c + 1],
                L["Af32"][:, c * C + C - 1:c * C + C])
        b.v("memset", Ac[:, 2 * NCH:2 * NCH + 1], 0.0)
        b.v("tensor_copy", Ac[:, 2 * NCH + 1:3 * NCH], Ac[:, 1 * NCH:2 * NCH - 1])
        b.v("tensor_sub", Ac[:, 3 * NCH:4 * NCH], Ac[:, 1 * NCH:2 * NCH],
            Ac[:, 2 * NCH:3 * NCH])
        b.v("tensor_sub", Ac[:, 4 * NCH:5 * NCH], Ac[:, 1 * NCH:2 * NCH], Ac[:, 0:NCH])
        b.v("tensor_sub", Ac[:, 5 * NCH:6 * NCH], Ac[:, 0:NCH], Ac[:, 2 * NCH:3 * NCH])
        b.v("tensor_scalar", Ac[:, 2 * NCH:3 * NCH], Ac[:, 0:NCH], -1.0, None,
            ALU.mult, inc="vec")
        cols_v = b.val("vec")
        b.wait("scalar", "vec", cols_v)
        b.act(Ac[:, 3 * NCH:4 * NCH], Ac[:, 3 * NCH:4 * NCH], AF.Exp)
        b.act(Ac[:, 4 * NCH:5 * NCH], Ac[:, 4 * NCH:5 * NCH], AF.Exp)
        b.act(Ac[:, 5 * NCH:6 * NCH], Ac[:, 5 * NCH:6 * NCH], AF.Exp, inc="sc")
        for c in range(NCH):
            b.act(L["scrbf"][:, c * C:(c + 1) * C], L["Af32"][:, c * C:(c + 1) * C],
                  AF.Exp, scale=-1.0, bias=Ac[:, 0 * NCH + c:0 * NCH + c + 1],
                  inc=("sc" if c == NCH - 1 else None))
        ek_sc = b.val("sc")
        for co in range(OWNC):
            c = OWNC + co
            b.act(L["scrbf2"][:, co * C:(co + 1) * C], L["Af32"][:, c * C:(c + 1) * C],
                  AF.Exp, scale=1.0, bias=Ac[:, 2 * NCH + c:2 * NCH + c + 1],
                  inc=("sc" if co == OWNC - 1 else None))
        eq_sc = b.val("sc")
        b.wait("vector", "sc", ek_sc)
        b.v("tensor_mul", L["Ktil"][:], L["kraw"][:], L["scrbf"][:])
        b.wait("vector", "sc", eq_sc)
        b.v("tensor_mul", L["Qtil"][:], L["qraw"][:], L["scrbf2"][:, 0:T_OWN])
        for c in range(NCH):
            b.v("tensor_scalar", L["Khat"][:, c * C:(c + 1) * C],
                L["Ktil"][:, c * C:(c + 1) * C],
                Ac[:, 4 * NCH + c:4 * NCH + c + 1], None, ALU.mult)
        b.v("memset", L["Sfp"][:], 0.0, inc="vec")
        prep_v = b.val("vec")
        L[f"gla_prep_v_{h}"] = prep_v

        b.wait("tensor", "vec", prep_v)
        b.wait("gpsimd", "vec", prep_v)
        for c in range(GLA_CHUNKS):
            cc = h * NCH + c
            sl = cc % 2
            bp = (c % 2) * 64
            tt = c // 2
            TB = [L["TB0"], L["TB1"]][sl]
            PD = [P0, P1][sl]
            vsl = L["vgla"][bp:bp + 64, tt * 256 + hh2 * 128: tt * 256 + (hh2 + 1) * 128]
            b.wait("tensor", "sc", evD.get(cc - 2, 0))
            b.tr(TB[bp:bp + 64, 0:128], L["Khat"][:, c * C:(c + 1) * C],
                 L["identb"][:], inc="pe")
            trD[cc] = b.val("pe")
            b.wait("scalar", "pe", trD[cc])
            b.wait("scalar", "pe", dsD.get(cc - 2, 0))
            b.scopy(L["KhatT"][bp:bp + 64, 0:128], TB[bp:bp + 64, 0:128], inc="sc")
            evD[cc] = b.val("sc")
            b.wait("tensor", "sc", evD[cc])
            b.wait("tensor", "vec", chD.get(cc - 2, 0))
            b.mm(PD[0:128, 0:128], L["KhatT"][bp:bp + 64, 0:128], vsl, inc="pe")
            dsD[cc] = b.val("pe")
            if c >= OWNC:
                co = c - OWNC
                Ptp = [L["Pt0"], L["Pt1"]][c % 2]
                vpad = [L["vpad0"], L["vpad1"]][c % 2]
                b.wait("vector", "pe", ymD[-1])
                b.v("tensor_scalar", L["Qr"][:, 0:C],
                    L["Qtil"][:, co * C:(co + 1) * C],
                    Ac[:, 5 * NCH + c:5 * NCH + c + 1], None, ALU.mult)
                b.v("tensor_copy", L["Sbf"][:], L["Sfp"][:])
                b.v("tensor_copy", vpad[bp:bp + 64, :], vsl, inc="vec")
                qr_v = b.val("vec")
                if GLA_OWN_OPS >= 2:
                    b.wait("tensor", "vec", qr_v)
                    b.wait("tensor", "sc", ptS[-1])
                    b.mm(W4[bp:bp + 64, 1024:1088], L["Ktil"][:, c * C:(c + 1) * C],
                         L["Qtil"][:, co * C:(co + 1) * C], inc="pe")
                    b.wait("scalar", "pe", b.val("pe"))
                    b.wait("scalar", "gp", ptG[-2])
                    b.scopy(Ptp[bp:bp + 64, 0:C], W4[bp:bp + 64, 1024:1088],
                            inc="sc")
                    ptS.append(b.val("sc"))
                if GLA_OWN_OPS >= 3:
                    b.wait("gpsimd", "sc", ptS[-1])
                    b.gp("affine_select", Ptp[bp:bp + 64, 0:C],
                         Ptp[bp:bp + 64, 0:C], [[1, C]], ALU.is_ge, 0.0,
                         base=0, channel_multiplier=-1, inc="gp")
                    ptG.append(b.val("gp"))
                if GLA_OWN_OPS >= 4:
                    b.wait("tensor", "gp", ptG[-1])
                    b.wait("tensor", "sc", yeS[-1])
                    b.mm(W4[0:128, 1536:1600], L["Sbf"][:], L["Qr"][:, 0:C],
                         start=True, stop=False)
                    b.mm(W4[0:128, 1536:1600], vpad[:], Ptp[:, 0:C],
                         start=False, stop=True, inc="pe")
                    ymD.append(b.val("pe"))
                if GLA_OWN_OPS >= 5:
                    b.wait("scalar", "pe", ymD[-1])
                    b.scopy(L["yq"][:, h * T_OWN + co * C: h * T_OWN + (co + 1) * C],
                            W4[0:128, 1536:1600], inc="sc")
                    yeS.append(b.val("sc"))
            b.wait("vector", "pe", dsD[cc])
            if c == OWNC:
                b.v("tensor_scalar", L["Sfp"][:], L["Sfp"][:], L["flagc"][:, 0:1],
                    None, ALU.mult)
            b.v("scalar_tensor_tensor", L["Sfp"][:], L["Sfp"][:],
                Ac[:, 3 * NCH + c:3 * NCH + c + 1], PD[0:128, 0:128],
                ALU.mult, ALU.add, inc="vec")
            chD[cc] = b.val("vec")
        L[f"gla_head_done_{h}"] = b.val("vec")
        L[f"gla_ysc_{h}"] = b.val("sc")
    L["y_sc"] = b.val("sc")
    b.tap("y", L["yq"][:], "sc", L["y_sc"])


def phase_wo(b, L):
    yq, wst, W4 = L["yq"], L["wst"], L["W4"]
    dint = L["dint"]

    b.wait("scalar", "sc", L["y_sc"])
    b.wait("tensor", "sc", L["y_sc"])
    ln_stats_fm(b, L, lambda ft, t0: yq[:, ft * T_OWN + t0: ft * T_OWN + t0 + 512],
                T_OWN, True, 1e-5)
    if WO_STAGE < 2:
        L["t1sp_ld"] = []
        for tt in range(NT_OWN):
            L[f"wo_grp_pe_{tt}"] = 0
        L["t1_gla_v"] = b.val("vec")
        return
    nrm_v = [0, 0]
    for tb in range(2):
        t0 = tb * 512
        mb = W4[:, (tb % 2) * 1024:(tb % 2) * 1024 + 512]
        rb = W4[:, (tb % 2) * 1024 + 512:(tb % 2) * 1024 + 1024]
        b.wait("tensor", "vec", nrm_v[-2])
        bcast_mm(b, L, mb, L["r0"][0:1, t0:t0 + 512])
        bcast_mm(b, L, rb, L["r1"][0:1, t0:t0 + 512])
        b.wait("vector", "pe", b.val("pe"))
        for hh in range(H):
            yt = yq[:, hh * T_OWN + t0: hh * T_OWN + t0 + 512]
            b.v("tensor_sub", yt, yt, mb)
            b.v("tensor_mul", yt, yt, rb)
            b.v("tensor_scalar", yt, yt, L["gammab"][:, hh:hh + 1],
                L["betab"][:, hh:hh + 1], ALU.mult, ALU.add, inc="vec")
        nrm_v.append(b.val("vec"))
    yn_v = b.val("vec")
    if WO_STAGE < 3:
        L["t1sp_ld"] = []
        L["t1_gla_v"] = b.val("vec")
        return

    xl = []
    b.wait_dma("sync", L["cd"]("x1"))
    for tt in range(NT_OWN):
        xl.append(b.dma("sync", L["t1a"][:, tt * D:(tt + 1) * D],
                        dint["xbf"][tt * 128:(tt + 1) * 128, :],
                        inc="wld"))
    acc_v = [0, 0, 0, 0]
    for grp in range(4):
        par = grp % 2
        if grp >= 2:
            b.wait("gpsimd", "pe", L[f"wo_grp_pe_{grp - 2}"])
        ld = stream_wT(b, L, L["din"]["L0_attn_Wo"], "wo", grp * 512, (grp + 1) * 512,
                       par * 8192)
        b.wait_dma("tensor", ld)
        b.wait("tensor", "vec", yn_v)
        for tt in range(NT_OWN):
            bk = tt % 4
            b.wait("tensor", "vec", acc_v[-4])
            if grp == 0:
                b.wait_dma("vector", xl[tt])
            for fc in range(FT):
                b.mm(W4[:, bk * 512:(bk + 1) * 512],
                     yq[:, fc * T_OWN + tt * 128: fc * T_OWN + (tt + 1) * 128],
                     wst[:, par * 8192 + fc * 512: par * 8192 + (fc + 1) * 512],
                     start=(fc == 0), stop=(fc == FT - 1),
                     inc=("pe" if fc == FT - 1 else None))
            b.wait("vector", "pe", b.val("pe"))
            sl1 = L["t1a"][:, tt * D + grp * 512: tt * D + grp * 512 + 512]
            b.v("tensor_add", sl1, sl1, W4[:, bk * 512:(bk + 1) * 512], inc="vec")
            acc_v.append(b.val("vec"))
        L[f"wo_grp_pe_{grp}"] = b.val("pe")
    t1_v = b.val("vec")
    if WO_STAGE >= 4:
        b.wait("sync", "vec", t1_v)
        sp = []
        for tt in range(NT_OWN):
            sp.append(b.dma("sync", dint["t1_sp"][tt * 128:(tt + 1) * 128, :],
                            L["t1a"][:, tt * D:(tt + 1) * D], inc="wld"))
        L["t1sp_ld"] = sp
    else:
        L["t1sp_ld"] = []
    b.tap("t1a", L["t1a"][:], "vec", t1_v)


def phase_ffnproj(b, L):
    xn, wst, W4, P0, P1 = L["xn"], L["wst"], L["W4"], L["P0"], L["P1"]
    dint = L["dint"]

    b.wait("scalar", "vec", L["xn_v"])
    ln_stats_fm(b, L, lambda ft, t0: xn[:, ft * T_ALL + t0: ft * T_ALL + t0 + 512],
                T_ALL, False, 1e-6)
    h_v = [0, 0]
    for tb in range(4):
        t0 = tb * 512
        rb = W4[:, (tb % 2) * 512:(tb % 2) * 512 + 512]
        b.wait("tensor", "vec", h_v[-2])
        bcast_mm(b, L, rb, L["r1"][0:1, t0:t0 + 512])
        b.wait("vector", "pe", b.val("pe"))
        for ft in range(FT):
            xt = xn[:, ft * T_ALL + t0: ft * T_ALL + t0 + 512]
            b.v("tensor_scalar", xt, xt, L["wanorm"][:, ft:ft + 1], None, ALU.mult)
            b.v("tensor_mul", xt, xt, rb, inc="vec")
        h_v.append(b.val("vec"))
    L["h_v"] = b.val("vec")

    ldk = stream_wT(b, L, L["din"]["L0_ffn_k_w"], "fk", 0, 512, 0)
    b.wait("tensor", "vec", L["h_v"])
    b.wait_dma("tensor", ldk)
    b.wait_dma("tensor", L["t1sp_ld"])
    for kh in range(HKV):
        b.wait("scalar", "vec", b.val("vec"))
        kev = []
        for half in range(4):
            ps = [P0, P1][half % 2]
            if half >= 2:
                b.wait("tensor", "sc", kev[half - 2])
            for dc in range(FT):
                b.mm(ps[:], wst[:, dc * 512 + kh * 128: dc * 512 + (kh + 1) * 128],
                     xn[:, dc * T_ALL + half * 512: dc * T_ALL + (half + 1) * 512],
                     start=(dc == 0), stop=(dc == FT - 1),
                     inc=("pe" if dc == FT - 1 else None))
            b.wait("scalar", "pe", b.val("pe"))
            b.act(L["scrbf"][:, half * 512:(half + 1) * 512], ps[:], AF.Copy, inc="sc")
            kev.append(b.val("sc"))
        rope_norm(b, L, src=L["scrbf"], dst=L["kv1"], dst_off=kh * T_ALL,
                  tlen=T_ALL, wcol=L["wknc"], src_sc=kev[-1], tpos_off=0)
    L["khat_v"] = b.val("vec")
    L["kproj_pe"] = b.val("pe")

    ldv = stream_wT(b, L, L["din"]["L0_ffn_v_w"], "fv", 0, 512, 8192)
    b.wait_dma("tensor", ldv)
    b.wait("tensor", "vec", L["khat_v"])
    vev = [0, 0]
    for tt in range(NT_ALL):
        pcol = 1024 + (tt % 2) * 512
        b.wait("tensor", "vec", vev[-2])
        for dc in range(FT):
            b.mm(W4[:, pcol:pcol + 512],
                 xn[:, dc * T_ALL + tt * 128: dc * T_ALL + (tt + 1) * 128],
                 wst[:, 8192 + dc * 512: 8192 + (dc + 1) * 512],
                 start=(dc == 0), stop=(dc == FT - 1),
                 inc=("pe" if dc == FT - 1 else None))
        b.wait("vector", "pe", b.val("pe"))
        b.v("tensor_copy", L["kv2"][:, tt * 512:(tt + 1) * 512],
            W4[:, pcol:pcol + 512], inc="vec")
        vev.append(b.val("vec"))
    L["vffn_v"] = b.val("vec")
    L["vproj_pe"] = b.val("pe")

    for qg in range(4):
        par = qg % 2
        if qg >= 2:
            b.wait("gpsimd", "pe", L[f"fp_qg_pe_{qg - 2}"])
        elif qg == 0:
            b.wait("gpsimd", "pe", L["kproj_pe"])
        else:
            b.wait("gpsimd", "pe", L["vproj_pe"])
        ld = stream_wT(b, L, L["din"]["L0_ffn_q_w"], "fq", qg * 512, (qg + 1) * 512,
                       par * 8192)
        b.wait_dma("tensor", ld)
        for hi in range(4):
            qh = qg * 4 + hi
            b.wait("tensor", "vec", b.val("vec"))
            b.wait("scalar", "vec", b.val("vec"))
            for half in range(2):
                ps = [P0, P1][half]
                for dc in range(FT):
                    b.mm(ps[:], wst[:, par * 8192 + dc * 512 + hi * 128:
                                    par * 8192 + dc * 512 + (hi + 1) * 128],
                         xn[:, dc * T_ALL + T_PRI + half * 512:
                            dc * T_ALL + T_PRI + (half + 1) * 512],
                         start=(dc == 0), stop=(dc == FT - 1),
                         inc=("pe" if dc == FT - 1 and half == 1 else None))
            b.wait("scalar", "pe", b.val("pe"))
            b.act(L["scrbf"][:, 0:512], P0[:], AF.Copy)
            b.act(L["scrbf"][:, 512:1024], P1[:], AF.Copy, inc="sc")
            rope_norm(b, L, src=L["scrbf"], dst=L["yq"], dst_off=qh * T_OWN,
                      tlen=T_OWN, wcol=L["wqnc"], src_sc=b.val("sc"),
                      tpos_off=T_PRI)
        L[f"fp_qg_pe_{qg}"] = b.val("pe")
    L["qhat_v"] = b.val("vec")

    bounds(b, L)
    b.tap("qhat", L["yq"][:], "vec", L["qhat_v"])
    b.tap("khat", L["kv1"][:], "vec", L["khat_v"])
    b.tap("vffn", L["kv2"][:], "vec", L["vffn_v"])
    b.tap("negM", L["negM"][:], "vec", L["negM_v"])


def rope_norm(b, L, src, dst, dst_off, tlen, wcol, src_sc, tpos_off):
    P0, W4 = L["P0"], L["W4"]
    nb = tlen // 512
    b.wait("scalar", "sc", src_sc)
    mm_pe = [0, 0]
    for i in range(nb):
        t0 = i * 512
        b.wait("scalar", "pe", mm_pe[-2])
        b.act(L["scrbf2"][:, 512:1024], src[:, t0:t0 + 512], AF.Square, inc="sc")
        b.wait("tensor", "sc", b.val("sc"))
        b.mm(P0[0:1, :], L["ones"][:, 0:1], L["scrbf2"][:, 512:1024], inc="pe")
        mm_pe.append(b.val("pe"))
        b.wait("vector", "pe", b.val("pe"))
        r1s = L["r1"][0:1, t0:t0 + 512]
        b.v("tensor_scalar", r1s, P0[0:1, :], 1.0 / HD, None, ALU.mult)
        b.v("tensor_scalar", r1s, r1s, 1e-6, None, ALU.add)
        b.v("reciprocal", r1s, r1s, inc="vec")
        b.wait("scalar", "vec", b.val("vec"))
    b.act(ROW(L, 1, tlen), ROW(L, 1, tlen), AF.Sqrt, inc="sc")
    b.wait("tensor", "sc", b.val("sc"))
    rp = [0, 0]
    for i in range(nb):
        t0 = i * 512
        sl = i % 2
        rb = W4[:, sl * 512: sl * 512 + 512]
        sw = W4[:, 1024 + sl * 512: 1024 + sl * 512 + 512]
        b.wait("tensor", "vec", rp[-2])
        bcast_mm(b, L, rb, L["r1"][0:1, t0:t0 + 512])
        b.wait("vector", "pe", b.val("pe"))
        kn = L["scrbf2"][:, 0:512]
        tmp = L["scrbf2"][:, 512:1024]
        b.v("tensor_scalar", kn, src[:, t0:t0 + 512], wcol[:, 0:1], None, ALU.mult)
        b.v("tensor_mul", kn, kn, rb, inc="vec")
        kn_v = b.val("vec")
        b.wait("tensor", "vec", kn_v)
        b.mm(sw, L["swapm"][:], kn, inc="pe")
        b.wait("vector", "pe", b.val("pe"))
        cs = L["cosb"][:, tpos_off + t0: tpos_off + t0 + 512]
        sn = L["sinb"][:, tpos_off + t0: tpos_off + t0 + 512]
        do = dst[:, dst_off + t0: dst_off + t0 + 512]
        b.v("tensor_mul", do, kn, cs)
        b.v("tensor_mul", tmp, sw, sn)
        b.v("tensor_add", do, do, tmp, inc="vec")
        rp.append(b.val("vec"))


def bounds(b, L):
    P0 = L["P0"]
    b.wait("scalar", "vec", max(L["qhat_v"], L["khat_v"]))
    b.wait("tensor", "vec", max(L["qhat_v"], L["khat_v"]))
    mm_pe = [0, 0]

    def max_norm2(src, src_off, tlen, out_cell):
        for i in range(tlen // 512):
            t0 = i * 512
            sl = i % 2
            b.wait("scalar", "pe", mm_pe[-2])
            b.act(L["scrbf2"][:, sl * 512:(sl + 1) * 512],
                  src[:, src_off + t0: src_off + t0 + 512], AF.Square, inc="sc")
            b.wait("tensor", "sc", b.val("sc"))
            b.mm(P0[0:1, :], L["ones"][:, 0:1],
                 L["scrbf2"][:, sl * 512:(sl + 1) * 512], inc="pe")
            mm_pe.append(b.val("pe"))
            b.wait("vector", "pe", b.val("pe"))
            b.v("tensor_copy", L["r1"][0:1, t0:t0 + 512], P0[0:1, :], inc="vec")
            b.wait("tensor", "vec", b.val("vec"))
        b.v("reduce_max", out_cell, ROW(L, 1, tlen), axis=AXX, inc="vec")

    for kh in range(HKV):
        max_norm2(L["kv1"], kh * T_ALL, T_ALL, L["snorm"][0:1, 32 + kh:33 + kh])
    for qh in range(HQ):
        max_norm2(L["yq"], qh * T_OWN, T_OWN, L["snorm"][0:1, qh:qh + 1])
    for qh in range(HQ):
        kh = qh // (HQ // HKV)
        b.v("tensor_mul", L["snorm"][0:1, 48 + qh:49 + qh],
            L["snorm"][0:1, qh:qh + 1], L["snorm"][0:1, 32 + kh:33 + kh], inc="vec")
    b.wait("scalar", "vec", b.val("vec"))
    b.act(L["snorm"][0:1, 48:48 + HQ], L["snorm"][0:1, 48:48 + HQ], AF.Sqrt, inc="sc")
    b.wait("tensor", "sc", b.val("sc"))
    b.mm(L["W4"][:, 0:16], L["ones"][0:1, :], L["snorm"][0:1, 48:48 + HQ], inc="pe")
    b.wait("vector", "pe", b.val("pe"))
    b.v("tensor_scalar", L["negM"][:, 0:HQ], L["W4"][:, 0:16], -ATT_SCALE, None,
        ALU.mult)
    b.v("tensor_scalar", L["negM"][:, HQ:2 * HQ], L["negM"][:, 0:HQ],
        L["pbiasc"][:, 0:1], None, ALU.add, inc="vec")
    b.wait("tensor", "vec", b.val("vec"))
    L["negM_v"] = b.val("vec")


def phase_attn(b, L):
    P0, P1, W4 = L["P0"], L["P1"], L["W4"]
    attn = L["big2"]
    b.wait("tensor", "vec", L["negM_v"])
    b.wait("scalar", "vec", L["negM_v"])
    st_sc, pv_pe, msk_gp = {}, {}, {}
    i = 0
    for qh in range(HQ):
        kh = qh // (HQ // HKV)
        for blk in range(2):
            q0 = blk * 512
            nkt = 8 + (blk + 1) * 4
            diag0 = 8 + blk * 4
            for kt in range(nkt):
                sl = i % 2
                ps = [P0, P1][sl]
                b.wait("tensor", "sc", st_sc.get(i - 2, 0))
                b.mm(ps[:], L["kv1"][:, kh * T_ALL + kt * 128:
                                     kh * T_ALL + (kt + 1) * 128],
                     L["yq"][:, qh * T_OWN + q0: qh * T_OWN + q0 + 512], inc="pe")
                b.wait("scalar", "pe", b.val("pe"))
                b.wait("scalar", "pe", pv_pe.get(i - 2, 0))
                b.wait("scalar", "gp", msk_gp.get(i - 2, 0))
                bias = L["negM"][:, HQ + qh:HQ + qh + 1] if kt < 8 else \
                    L["negM"][:, qh:qh + 1]
                b.act(L["scrbf"][:, sl * 512:(sl + 1) * 512], ps[:], AF.Exp,
                      bias=bias, scale=ATT_SCALE, inc="sc")
                st_sc[i] = b.val("sc")
                if kt >= diag0:
                    b.wait("gpsimd", "sc", st_sc[i])
                    base = (T_PRI + q0) - kt * 128
                    b.gp("affine_select", L["scrbf"][:, sl * 512:(sl + 1) * 512],
                         L["scrbf"][:, sl * 512:(sl + 1) * 512], [[1, 512]],
                         ALU.is_ge, 0.0, base=base, channel_multiplier=-1,
                         inc="gp")
                    msk_gp[i] = b.val("gp")
                    b.wait("tensor", "gp", msk_gp[i])
                else:
                    b.wait("tensor", "sc", st_sc[i])
                b.mm(W4[:, 0:512],
                     L["kv2"][:, kt * 512 + kh * 128: kt * 512 + (kh + 1) * 128],
                     L["scrbf"][:, sl * 512:(sl + 1) * 512],
                     start=(kt == 0), stop=(kt == nkt - 1))
                b.mm(W4[0:1, 512:1024], L["ones"][:, 0:1],
                     L["scrbf"][:, sl * 512:(sl + 1) * 512],
                     start=(kt == 0), stop=(kt == nkt - 1), inc="pe")
                pv_pe[i] = b.val("pe")
                i += 1
            b.wait("scalar", "pe", b.val("pe"))
            b.act(L["denomrow"][0:1, 0:512], W4[0:1, 512:1024], AF.Ln)
            b.act(L["r1"][0:1, 0:512], L["denomrow"][0:1, 0:512], AF.Exp,
                  scale=-1.0, inc="sc")
            b.wait("tensor", "sc", b.val("sc"))
            b.mm(W4[:, 1024:1536], L["ones"][0:1, :], L["r1"][0:1, 0:512],
                 inc="pe")
            b.wait("scalar", "pe", b.val("pe"))
            b.scopy(L["scrbf2"][:, 0:512], W4[:, 1024:1536], inc="sc")
            b.wait("vector", "sc", b.val("sc"))
            b.v("tensor_mul", attn[:, qh * T_OWN + q0: qh * T_OWN + q0 + 512],
                W4[:, 0:512], L["scrbf2"][:, 0:512], inc="vec")
            b.wait("tensor", "vec", b.val("vec"))
    L["attn_v"] = b.val("vec")
    b.tap("attnp", attn[:], "vec", L["attn_v"])


def phase_hidden(b, L):
    wst, W4 = L["wst"], L["W4"]
    attn = L["big2"]
    dint = L["dint"]

    def hid_tile(tt):
        buf = L["kv1"] if tt < 4 else L["kv2"]
        off = (tt % 4) * D
        return buf[:, off:off + D]

    rl = []
    b.wait_dma("sync", L["t1sp_ld"])
    for tt in range(NT_OWN):
        rl.append(b.dma("sync", L["t1r"][:, tt * D:(tt + 1) * D],
                        dint["t1_sp"][tt * 128:(tt + 1) * 128, :], inc="wld"))
    b.wait_dma("vector", rl)
    b.wait("tensor", "vec", L["attn_v"])

    acc_v = [0, 0, 0, 0]
    xn_v = {}
    for grp in range(4):
        par = grp % 2
        if grp >= 2:
            b.wait("gpsimd", "pe", L[f"hid_grp_pe_{grp - 2}"])
        ld = stream_wT(b, L, L["din"]["L0_ffn_o_w"], "fo", grp * 512, (grp + 1) * 512,
                       par * 8192)
        b.wait_dma("tensor", ld)
        for tt in range(NT_OWN):
            bk = tt % 4
            if grp == 0:
                if tt >= 1:
                    b.wait("sync", "vec", xn_v[tt - 1])
                xt_tok = b.dma("sync", L["xtile"][:],
                               L["din"]["x_own"][tt * 128:(tt + 1) * 128, :],
                               inc="wld")
                b.wait_dma("vector", xt_tok)
                b.wait("vector", "sc", L["statsT_sc"])
                b.v("tensor_scalar", hid_tile(tt), L["xtile"][:],
                    L["statsT"][:, 2 * tt:2 * tt + 1],
                    L["statsT"][:, 2 * tt + 1:2 * tt + 2],
                    ALU.subtract, ALU.mult)
                b.v("tensor_add", L["t1r"][:, tt * D:(tt + 1) * D],
                    L["t1r"][:, tt * D:(tt + 1) * D], hid_tile(tt), inc="vec")
                xn_v[tt] = b.val("vec")
            b.wait("tensor", "vec", acc_v[-4])
            if grp == 0:
                b.wait("tensor", "vec", xn_v[tt])
            for fc in range(FT):
                b.mm(W4[:, bk * 512:(bk + 1) * 512],
                     attn[:, fc * T_OWN + tt * 128: fc * T_OWN + (tt + 1) * 128],
                     wst[:, par * 8192 + fc * 512: par * 8192 + (fc + 1) * 512],
                     start=(fc == 0), stop=(fc == FT - 1),
                     inc=("pe" if fc == FT - 1 else None))
            b.wait("vector", "pe", b.val("pe"))
            ps = W4[:, bk * 512:(bk + 1) * 512]
            sl1 = L["t1r"][:, tt * D + grp * 512: tt * D + grp * 512 + 512]
            b.v("tensor_add", sl1, sl1, ps)
            hs = hid_tile(tt)[:, grp * 512:(grp + 1) * 512]
            b.v("tensor_add", hs, hs, ps, inc="vec")
            acc_v.append(b.val("vec"))
        L[f"hid_grp_pe_{grp}"] = b.val("pe")
    L["t1_attn_v"] = b.val("vec")

    b.wait("tensor", "vec", L["t1_attn_v"])
    trs = [0, 0]
    for tt in range(NT_OWN):
        for fc in range(FT):
            sl = (tt * FT + fc) % 2
            TB = [L["TB0"], L["TB1"]][sl]
            b.wait("tensor", "sc", trs[-2])
            b.tr(TB[:, 0:128], hid_tile(tt)[:, fc * 128:(fc + 1) * 128],
                 L["identb"][:], inc="pe")
            b.wait("scalar", "pe", b.val("pe"))
            b.scopy(L["hfm"][:, fc * T_OWN + tt * 128: fc * T_OWN + (tt + 1) * 128],
                    TB[:, 0:128], inc="sc")
            trs.append(b.val("sc"))
    hfm_sc = b.val("sc")
    L["hfm_sc"] = hfm_sc
    b.tap("hfm", L["hfm"][:], "sc", hfm_sc)

    b.wait("scalar", "sc", hfm_sc)
    b.wait("tensor", "sc", hfm_sc)
    ln_stats_fm(b, L, lambda ft, t0: L["hfm"][:, ft * T_OWN + t0:
                                              ft * T_OWN + t0 + 512],
                T_OWN, False, 1e-6)
    h2v = [0, 0]
    for tb in range(2):
        t0 = tb * 512
        rb = W4[:, (tb % 2) * 512:(tb % 2) * 512 + 512]
        b.wait("tensor", "vec", h2v[-2])
        bcast_mm(b, L, rb, L["r1"][0:1, t0:t0 + 512])
        b.wait("vector", "pe", b.val("pe"))
        for ft in range(FT):
            src = L["hfm"][:, ft * T_OWN + t0: ft * T_OWN + t0 + 512]
            dst = L["yq"][:, ft * T_OWN + t0: ft * T_OWN + t0 + 512]
            b.v("tensor_scalar", dst, src, L["wfnorm"][:, ft:ft + 1], None, ALU.mult)
            b.v("tensor_mul", dst, dst, rb, inc="vec")
        h2v.append(b.val("vec"))
    L["h2_v"] = b.val("vec")
    b.tap("h2", L["yq"][:], "vec", L["h2_v"])


def gate(b, L):
    P0 = L["P0"]
    h2 = L["yq"]
    G8, M8, GS = L["grow8"], L["masked8"], L["gsc"]
    b.wait("tensor", "vec", L["h2_v"])
    for tt in range(NT_OWN):
        for dc in range(FT):
            b.mm(P0[0:128, tt * 8:(tt + 1) * 8],
                 h2[:, dc * T_OWN + tt * 128: dc * T_OWN + (tt + 1) * 128],
                 L["gwT"][:, dc * 8:(dc + 1) * 8],
                 start=(dc == 0), stop=(dc == FT - 1),
                 inc=("pe" if dc == FT - 1 else None))
    b.wait("vector", "pe", b.val("pe"))
    for tt in range(NT_OWN):
        b.v("reduce_max", GS[:, 24 + tt:25 + tt], P0[0:128, tt * 8:(tt + 1) * 8],
            axis=AXX, negate=True, inc=("vec" if tt == NT_OWN - 1 else None))
    b.wait("scalar", "vec", b.val("vec"))
    for tt in range(NT_OWN):
        b.act(G8[:, tt * 8:(tt + 1) * 8], P0[0:128, tt * 8:(tt + 1) * 8], AF.Exp,
              bias=GS[:, 24 + tt:25 + tt], accum_out=GS[:, 16 + tt:17 + tt],
              inc=("sc" if tt == NT_OWN - 1 else None))
    b.wait("vector", "sc", b.val("sc"))
    for tt in range(NT_OWN):
        b.v("reduce_max", GS[:, tt:tt + 1], G8[:, tt * 8:(tt + 1) * 8], axis=AXX)
    for tt in range(NT_OWN):
        b.v("tensor_scalar", M8[:, tt * 8:(tt + 1) * 8], G8[:, tt * 8:(tt + 1) * 8],
            GS[:, tt:tt + 1], None, ALU.is_ge)
    for tt in range(NT_OWN):
        b.v("scalar_tensor_tensor", M8[:, tt * 8:(tt + 1) * 8],
            M8[:, tt * 8:(tt + 1) * 8], -1e30, G8[:, tt * 8:(tt + 1) * 8],
            ALU.mult, ALU.add)
    for tt in range(NT_OWN):
        b.v("reduce_max", GS[:, 8 + tt:9 + tt], M8[:, tt * 8:(tt + 1) * 8],
            axis=AXX)
    for tt in range(NT_OWN):
        b.v("tensor_add", GS[:, 24 + tt:25 + tt], GS[:, tt:tt + 1],
            GS[:, 8 + tt:9 + tt])
    for tt in range(NT_OWN):
        b.v("scalar_tensor_tensor", GS[:, 24 + tt:25 + tt],
            GS[:, 16 + tt:17 + tt], 1e-9, GS[:, 24 + tt:25 + tt],
            ALU.mult, ALU.add)
    for tt in range(NT_OWN):
        b.v("reciprocal", GS[:, 32 + tt:33 + tt], GS[:, 24 + tt:25 + tt])
    for tt in range(NT_OWN):
        b.v("scalar_tensor_tensor", L["cvals"][:, tt * 8:(tt + 1) * 8],
            G8[:, tt * 8:(tt + 1) * 8], GS[:, 8 + tt:9 + tt],
            G8[:, tt * 8:(tt + 1) * 8], ALU.is_ge, ALU.mult)
    for tt in range(NT_OWN):
        b.v("tensor_scalar", L["cvals"][:, tt * 8:(tt + 1) * 8],
            L["cvals"][:, tt * 8:(tt + 1) * 8], GS[:, 32 + tt:33 + tt], None,
            ALU.mult, inc=("vec" if tt == NT_OWN - 1 else None))
    L["cvals_v"] = b.val("vec")
    b.tap("cvals", L["cvals"][:], "vec", L["cvals_v"])


def phase_moe(b, L):
    W4, P0, P1 = L["W4"], L["P0"], L["P1"]
    h2 = L["yq"]
    act = L["hfm"]
    groups = [(0, 3), (3, 3), (6, 3), (9, 2)]

    def wtile(buf, dc):
        if buf == 0:
            return L["wst"][:, dc * 1408:(dc + 1) * 1408]
        if dc < 11:
            return L["mb1a"][:, dc * 1408:(dc + 1) * 1408]
        if dc < 13:
            return L["mb1b"][:, (dc - 11) * 1408:(dc - 10) * 1408]
        if dc < 15:
            return L["mb1c"][:, (dc - 13) * 1408:(dc - 12) * 1408]
        return L["mb1d"][:, 0:1408]

    def wdtile(buf, ic):
        if buf == 0:
            return L["wst"][:, ic * 2048:(ic + 1) * 2048]
        if ic < 8:
            return L["mb1a"][:, ic * 2048:(ic + 1) * 2048]
        if ic < 10:
            return L["mb1b"][:, (ic - 8) * 2048:(ic - 7) * 2048]
        return L["mb1c"][:, 0:2048]

    b.wait("vector", "vec", L["cvals_v"])
    b.wait("tensor", "vec", L["cvals_v"])

    pass_pe = {-2: 0, -1: 0}
    ld_done = {}
    prev_act_sc = 0
    prev_eu_v = 0
    out_v = [0] * NT_OWN

    def emit_loads(p):
        e, wi = p // 3, p % 3
        buf = p % 2
        dram = (L["din"]["L0_ffn_wg"], L["din"]["L0_ffn_wu"],
                L["din"]["L0_ffn_wd"])[wi][e]
        b.wait("gpsimd", "pe", pass_pe[p - 2])
        toks = []
        if wi < 2:
            for dc in range(FT):
                toks.append(b.dma("gpsimd", wtile(buf, dc),
                                  dram[dc * 128:(dc + 1) * 128, :],
                                  inc="cast"))
        else:
            for ic in range(IC):
                toks.append(b.dma("gpsimd", wdtile(buf, ic),
                                  dram[ic * 128:(ic + 1) * 128, :],
                                  inc="cast"))
        ld_done[p] = toks

    def emit_compute(p):
        nonlocal prev_act_sc, prev_eu_v
        e, wi = p // 3, p % 3
        buf = p % 2
        b.wait_dma("tensor", ld_done[p])
        if wi == 0 and e > 0:
            b.wait("tensor", "vec", out_v[NT_OWN - 1])
        if wi < 2:
            for g0, gn in groups:
                b.wait("tensor", "sc", prev_act_sc)
                if wi == 1:
                    b.wait("tensor", "vec", prev_eu_v)
                for dc in range(FT):
                    for gi in range(gn):
                        ic = g0 + gi
                        lhs = wtile(buf, dc)[:, ic * 128:(ic + 1) * 128]
                        for th in range(2):
                            if gi < 2:
                                ps = W4[:, gi * 1024 + th * 512:
                                        gi * 1024 + th * 512 + 512]
                            else:
                                ps = [P0, P1][th][:]
                            b.mm(ps, lhs,
                                 h2[:, dc * T_OWN + th * 512:
                                    dc * T_OWN + (th + 1) * 512],
                                 start=(dc == 0), stop=(dc == FT - 1),
                                 inc=("pe" if (dc == FT - 1 and gi == gn - 1
                                               and th == 1) else None))
                grp_pe = b.val("pe")
                for gi in range(gn):
                    ic = g0 + gi
                    for th in range(2):
                        ps = (W4[:, gi * 1024 + th * 512: gi * 1024 + th * 512 + 512]
                              if gi < 2 else [P0, P1][th][:])
                        sl = act[:, ic * T_OWN + th * 512: ic * T_OWN + (th + 1) * 512]
                        if wi == 0:
                            b.wait("scalar", "pe", grp_pe)
                            b.act(sl, ps, AF.Silu, inc="sc")
                        else:
                            b.wait("vector", "pe", grp_pe)
                            b.v("tensor_tensor", sl, sl, ps, ALU.mult, inc="vec")
                if wi == 0:
                    prev_act_sc = b.val("sc")
                else:
                    prev_eu_v = b.val("vec")
        else:
            b.wait("tensor", "vec", prev_eu_v)
            b.wait("tensor", "sc", prev_act_sc)
            for tt in range(NT_OWN):
                b.wait("tensor", "vec", out_v[tt - 1] if tt > 0 else
                       out_v[NT_OWN - 1])
                for ic in range(IC):
                    for nbk in range(4):
                        b.mm(W4[:, nbk * 512:(nbk + 1) * 512],
                             act[:, ic * T_OWN + tt * 128:
                                 ic * T_OWN + (tt + 1) * 128],
                             wdtile(buf, ic)[:, nbk * 512:(nbk + 1) * 512],
                             start=(ic == 0), stop=(ic == IC - 1),
                             inc=("pe" if ic == IC - 1 and nbk == 3 else None))
                b.wait("vector", "pe", b.val("pe"))
                b.v("scalar_tensor_tensor", L["t1r"][:, tt * D:(tt + 1) * D],
                    W4[:], L["cvals"][:, tt * 8 + e: tt * 8 + e + 1],
                    L["t1r"][:, tt * D:(tt + 1) * D], ALU.mult, ALU.add, inc="vec")
                out_v[tt] = b.val("vec")
        pass_pe[p] = b.val("pe")

    NP = 3 * E
    emit_loads(0)
    for p in range(1, NP):
        emit_loads(p)
        emit_compute(p - 1)
    emit_compute(NP - 1)
    for tt in range(NT_OWN):
        L[f"out_tile_v_{tt}"] = out_v[tt]


def finish(b, L):
    toks = list(getattr(b, "tap_tokens", []))
    if BUILD_PHASES >= 6:
        for tt in range(NT_OWN):
            b.wait("gpsimd", "vec", L[f"out_tile_v_{tt}"])
            toks.append(b.dma("gpsimd", L["out_own"][tt * 128:(tt + 1) * 128, :],
                              L["t1r"][:, tt * D:(tt + 1) * D], inc="cast"))
    else:
        b.barrier()
        src_t = L["yq"] if (BUILD_PHASES < 3 or WO_STAGE < 3) else (
            L["t1a"] if BUILD_PHASES < 4 else L["t1r"])
        for tt in range(NT_OWN):
            toks.append(b.dma("gpsimd", L["out_own"][tt * 128:(tt + 1) * 128, :],
                              src_t[:, tt * D:(tt + 1) * D], inc="cast"))
    b.wait_dma("gpsimd", toks)
    b.wait_dma("sync", toks)



_NC_CACHE = {}

def kernel(**inputs):
    key = (BUILD_PHASES, tuple(DEBUG_TAPS))
    if key not in _NC_CACHE:
        _NC_CACHE[key] = build_nc()
    nc = _NC_CACHE[key]

    x = np.ascontiguousarray(np.asarray(inputs["L0_attn_x"]), dtype=np.float32)
    freqs = 1.0 / ROPE_THETA ** (np.arange(0, HD, 2, dtype=np.float32) / HD)
    ident = np.eye(128, dtype=np.float32)

    def tcols(w):
        return np.ascontiguousarray(np.asarray(w, np.float32).reshape(FT, 128).T)

    base = {
        "gamma_t": tcols(inputs["L0_attn_gamma"]),
        "beta_t": tcols(inputs["L0_attn_beta"]),
        "w_anorm_t": tcols(inputs["L0_ffn_attn_norm_w"]),
        "w_fnorm_t": tcols(inputs["L0_ffn_ffn_norm_w"]),
        "w_qn_t": np.ascontiguousarray(
            np.asarray(inputs["L0_ffn_q_norm_w"], np.float32).reshape(128, 1)),
        "w_kn_t": np.ascontiguousarray(
            np.asarray(inputs["L0_ffn_k_norm_w"], np.float32).reshape(128, 1)),
        "identf": ident,
        "swapf": np.ascontiguousarray(
            np.eye(128, dtype=np.float32)[:, list(range(64, 128)) + list(range(64))]),
    }
    for k in ("L0_attn_Wq", "L0_attn_Wk", "L0_attn_Wv", "L0_attn_Wg", "L0_attn_Wo",
              "L0_ffn_q_w", "L0_ffn_k_w", "L0_ffn_v_w", "L0_ffn_o_w"):
        base[k] = np.ascontiguousarray(np.asarray(inputs[k], np.float32).T)
    base["L0_ffn_gate_w"] = np.ascontiguousarray(
        np.asarray(inputs["L0_ffn_gate_w"], np.float32))
    for k in ("L0_ffn_wg", "L0_ffn_wu", "L0_ffn_wd"):
        base[k] = np.ascontiguousarray(
            np.asarray(inputs[k], np.float32).transpose(0, 2, 1))

    in_maps = []
    for core in range(8):
        bb, half = core // 2, core % 2
        pos = np.concatenate([np.arange(0, T_PRI, dtype=np.float32),
                              np.arange(half * T_OWN, half * T_OWN + T_OWN,
                                        dtype=np.float32)])
        f = np.outer(pos, freqs)
        m = dict(base)
        xa = np.concatenate([x[bb, 0:T_PRI],
                             x[bb, half * T_OWN:(half + 1) * T_OWN]], 0)
        m["xT_all"] = np.ascontiguousarray(xa.T)
        m["x_own"] = np.ascontiguousarray(x[bb, half * T_OWN:(half + 1) * T_OWN])
        cosT = np.cos(f).T.astype(np.float32)
        sinT = np.sin(f).T.astype(np.float32)
        m["cosA"] = np.ascontiguousarray(np.concatenate([cosT, cosT], 0))
        m["sinA"] = np.ascontiguousarray(np.concatenate([-sinT, sinT], 0))
        m["flag_col"] = np.full((128, 1), 1.0 if half == 1 else 0.0, np.float32)
        m["pbias_col"] = np.full((128, 1), 0.0 if half == 1 else -1e30, np.float32)
        in_maps.append(m)

    res = run_bass_kernel_spmd(nc, in_maps, core_ids=list(range(8)))
    out = np.empty((4, T_ALL, D), dtype=np.float32)
    for core in range(8):
        bb, half = core // 2, core % 2
        out[bb, half * T_OWN:(half + 1) * T_OWN] = res.results[core]["out_own"]
    kernel._last_results = res
    return out


# version 08 (final)
# speedup vs baseline: 1.1036x; 1.0368x over previous
import numpy as np
import concourse.bass as bass
import concourse.mybir as mybir
from concourse.bass_utils import run_bass_kernel_spmd
from contextlib import ExitStack

F32 = mybir.dt.float32
BF16 = mybir.dt.bfloat16
AF = mybir.ActivationFunctionType
ALU = mybir.AluOpType
AXX = mybir.AxisListType.X

D = 2048
FT = 16
H = 16
KD = 128
HQ, HKV, HD = 16, 4, 128
E, I = 8, 1408
IC = 11
C = 64
T_OWN = 1024
T_PRI = 1024
T_ALL = 2048
NT_OWN = 8
NT_ALL = 16
NCH = T_ALL // C
OWNC = T_OWN // C
ROPE_THETA = 1e6
QK_SCALE = float(KD) ** (-0.5)
ATT_SCALE = float(HD) ** (-0.5)

BUILD_PHASES = 6
DEBUG_TAPS = []
GLA_HEADS = H
GLA_CHUNKS = NCH
GLA_OWN_OPS = 5
WO_STAGE = 4

TAP_SPECS = {
    "xn": ([128, FT * T_ALL], BF16),
    "rows": ([1, T_ALL], BF16),
    "statsT": ([128, 2 * NT_OWN], F32),
    "y": ([128, H * T_OWN], BF16),
    "t1a": ([128, NT_OWN * D], BF16),
    "qhat": ([128, HQ * T_OWN], BF16),
    "khat": ([128, HKV * T_ALL], BF16),
    "vffn": ([128, NT_ALL * 512], BF16),
    "attnp": ([128, HQ * T_OWN], BF16),
    "h2": ([128, H * T_OWN], BF16),
    "cvals": ([128, NT_OWN * 8], F32),
    "negM": ([128, 2 * HQ], F32),
    "hfm": ([128, H * T_OWN], BF16),
}


class B:

    def __init__(self, nc, sems, tapout):
        self.nc = nc
        self.sems = sems
        self.tapout = tapout
        self.cur = None
        self.eng = None
        self.count = None

    def begin(self, section, eng):
        self.cur = section
        self.eng = eng
        self.count = {k: 0 for k in self.sems}
        self.dq = {
            "cast": {"sems": [f"cq{i}" for i in range(4)], "n": 4, "next": 0},
            "wld": {"sems": [f"wq{i}" for i in range(6)], "n": 6, "next": 0},
        }
        self.tap_tokens = []

    def val(self, sem):
        return self.count[sem]

    def _fin(self, instr, inc, n):
        if inc is not None:
            self.count[inc] += n
            if instr is not None:
                instr.then_inc(self.sems[inc], n)

    def wait(self, section, sem, v):
        if v <= 0:
            return
        if self.cur == section:
            self.eng.wait_ge(self.sems[sem], v)

    def dma(self, section, out, in_, inc, transpose=False):
        q = self.dq[inc]
        lane = q["next"] % q["n"]
        q["next"] += 1
        sem = q["sems"][lane]
        prev = self.count[sem]
        if prev > 0 and self.cur == section:
            self.eng.wait_ge(self.sems[sem], prev)
        instr = None
        if self.cur == section:
            instr = self.eng.dma_start(out=out, in_=in_, transpose=transpose)
        self.count[sem] += 16
        if instr is not None:
            instr.then_inc(self.sems[sem], 16)
        return (sem, self.count[sem])

    def wait_dma(self, section, tokens):
        if isinstance(tokens, tuple):
            tokens = [tokens]
        best = {}
        for sem, v in tokens:
            best[sem] = max(best.get(sem, 0), v)
        if self.cur == section:
            for sem, v in best.items():
                self.eng.wait_ge(self.sems[sem], v)

    def mm(self, out, lhsT, rhs, start=True, stop=True, inc=None):
        instr = None
        if self.cur == "tensor":
            instr = self.eng.matmul(out, lhsT, rhs, start=start, stop=stop)
        self._fin(instr, inc, 1)

    def tr(self, out, in_, ident, inc=None):
        instr = None
        if self.cur == "tensor":
            instr = self.eng.transpose(out, in_, ident)
        self._fin(instr, inc, 1)

    def act(self, out, in_, func, bias=0.0, scale=1.0, accum_out=None, inc=None):
        instr = None
        if self.cur == "scalar":
            instr = self.eng.activation(out, in_, func, bias=bias, scale=scale,
                                        accum_out=accum_out)
        self._fin(instr, inc, 1)

    def scopy(self, out, in_, inc=None):
        instr = None
        if self.cur == "scalar":
            instr = self.eng.copy(out, in_)
        self._fin(instr, inc, 1)

    def v(self, method, *args, inc=None, **kw):
        instr = None
        if self.cur == "vector":
            instr = getattr(self.eng, method)(*args, **kw)
        self._fin(instr, inc, 1)

    def gp(self, method, *args, inc=None, **kw):
        instr = None
        if self.cur == "gpsimd":
            instr = getattr(self.eng, method)(*args, **kw)
        self._fin(instr, inc, 1)

    def sem_inc(self, section, sem, n=1):
        if self.cur == section:
            self.eng.nop().then_inc(self.sems[sem], n)
        self.count[sem] += n

    def marker(self, section, sem, junk, ones):
        instr = None
        if self.cur == section:
            if section == "vector":
                instr = self.eng.engine_nop()
            elif section == "scalar":
                instr = self.eng.copy(junk[0:1, 0:1], ones[0:1, 0:1])
            elif section == "gpsimd":
                instr = self.eng.memset(junk[0:1, 4:5], 0.0)
        self.count[sem] += 1
        if instr is not None:
            instr.then_inc(self.sems[sem], 1)

    def barrier(self):
        self.wait_dma("sync", list(self.tap_tokens)) if self.tap_tokens else None
        L = self._L
        self.marker("vector", "vec", L["junk"], L["ones"])
        self.marker("scalar", "sc", L["junk"], L["ones"])
        self.marker("gpsimd", "gp", L["junk"], L["ones"])
        self.sem_inc("sync", "sy", 1)
        targets = {"vec": self.count["vec"], "sc": self.count["sc"],
                   "gp": self.count["gp"], "sy": self.count["sy"],
                   "pe": self.count["pe"]}
        own = {"sync": "sy", "gpsimd": "gp", "tensor": "pe",
               "vector": "vec", "scalar": "sc"}
        if self.cur in own:
            for sem, v in targets.items():
                if sem != own[self.cur]:
                    self.eng.wait_ge(self.sems[sem], v)

    def tap(self, name, ap, sem, v):
        if name not in DEBUG_TAPS:
            return
        self.wait("sync", sem, v)
        tok = self.dma("sync", self.tapout[name], ap, inc="wld")
        if not hasattr(self, "tap_tokens"):
            self.tap_tokens = []
        self.tap_tokens.append(tok)


ORG = 16512

MEM = [
    ("xn",     [128, FT * T_ALL], BF16, 0),
    ("big2",   [128, H * T_OWN],  BF16, 0),
    ("hfm",    [128, H * T_OWN],  BF16, 0),
    ("t1r",    [128, NT_OWN * D], BF16, 32768),
    ("yq",     [128, H * T_OWN],  BF16, 65536),
    ("gate8b", [8, D],            BF16, 98304),
    ("gf32",   [128, T_ALL],      F32,  98304),
    ("Af32",   [128, T_ALL],      F32,  106496),
    ("kraw",   [128, T_ALL],      BF16, 114688),
    ("Ktil",   [128, T_ALL],      BF16, 118784),
    ("qraw",   [128, T_OWN],      BF16, 122880),
    ("Qtil",   [128, T_OWN],      BF16, 124928),
    ("Khat",   [128, T_ALL],      BF16, 126976),
    ("t1a",    [128, NT_OWN * D], BF16, 98304),
    ("kv1",    [128, HKV * T_ALL], BF16, 98304),
    ("kv2",    [128, NT_ALL * 512], BF16, 114688),
    ("mb1a",   [128, 16384],      BF16, 98304),
    ("vgla",   [128, NT_ALL * 256], BF16, 131072),
    ("xtile",  [128, D],          F32,  131072),
    ("mb1b",   [128, 4096],       BF16, 131072),
    ("cosb",   [128, T_ALL],      BF16, 139264),
    ("KhatT32", [128, T_ALL],     BF16, 139264),
    ("sinb",   [128, T_ALL],      BF16, 143360),
    ("mb1c",   [128, 4096],       BF16, 139264),
    ("wst",    [128, 22528],      BF16, 147456),
    ("scrbf",  [128, 2048],       BF16, 192512),
    ("scrbf2", [128, 1024],       BF16, 196608),
    ("mb1d",   [128, 2048],       BF16, 192512),
    ("mb1e",   [128, 2048],       BF16, 198656),
    ("statsT", [128, 2 * NT_OWN], F32,  198656),
    ("cvals",  [128, NT_OWN * 8], F32,  198720),
    ("gwT",    [128, FT * 8],     BF16, 198976),
    ("r0",     [1, T_ALL],        BF16, 200704),
    ("r1",     [1, T_ALL],        BF16, 204800),
    ("mb1e",   [128, 2048],       BF16, 200704),
    ("Acols",  [128, 6 * NCH],    F32,  208896),
    ("KhatT",  [128, 128],        BF16, 209664),
    ("Qr",     [128, C],          BF16, 209920),
    ("Pt0",    [128, C],          BF16, 210048),
    ("Pt1",    [128, C],          BF16, 210176),
    ("vpad0",  [128, 128],        BF16, 210304),
    ("vpad1",  [128, 128],        BF16, 210560),
    ("Sfp",    [128, 128],        F32,  210816),
    ("denomrow", [1, 512],        F32,  208896),
    ("negM",   [128, 2 * HQ],     F32,  210944),
    ("snorm",  [1, 64],           BF16, 211072),
    ("grow",   [128, 8],          F32,  211200),
    ("gcol",   [128, 16],         F32,  211232),
    ("grow8",  [128, 64],         F32,  208896),
    ("masked8", [128, 64],        F32,  209152),
    ("gsc",    [128, 40],         F32,  209408),
    ("Sbf",    [128, 128],        BF16, 211392),
    ("ones",   [128, 128],        BF16, 211648),
    ("identb", [128, 128],        BF16, 211904),
    ("gammab", [128, FT],         F32,  212160),
    ("betab",  [128, FT],         F32,  212224),
    ("wanorm", [128, FT],         F32,  212288),
    ("wfnorm", [128, FT],         F32,  212352),
    ("wqnc",   [128, 1],          F32,  212416),
    ("wknc",   [128, 1],          F32,  212448),
    ("flagc",  [128, 1],          F32,  212480),
    ("pbiasc", [128, 1],          F32,  212512),
    ("swapm",  [128, 128],        BF16, 212544),
    ("junk",   [128, 8],          F32,  212800),
]
WST_QKG = [0, 6144]
WST_V = [12288, 16384]
WST_ZERO = 20480


def build_nc():
    nc = bass.Bass("TRN2", target_bir_lowering=False, debug=False,
                   detect_race_conditions=False)

    din = {}
    def dinp(name, shape, dtype=F32):
        din[name] = nc.dram_tensor(name, list(shape), dtype, kind="ExternalInput").ap()

    dinp("xT_all", [D, T_ALL]); dinp("x_own", [T_OWN, D])
    dinp("L0_attn_Wq", [D, D]); dinp("L0_attn_Wk", [D, D])
    dinp("L0_attn_Wv", [D, D]); dinp("L0_attn_Wg", [D, D])
    dinp("L0_attn_Wo", [D, D])
    dinp("gamma_t", [128, FT]); dinp("beta_t", [128, FT])
    dinp("w_anorm_t", [128, FT]); dinp("w_fnorm_t", [128, FT])
    dinp("w_qn_t", [128, 1]); dinp("w_kn_t", [128, 1])
    dinp("L0_ffn_q_w", [D, D]); dinp("L0_ffn_k_w", [D, HKV * HD])
    dinp("L0_ffn_v_w", [D, HKV * HD]); dinp("L0_ffn_o_w", [HQ * HD, D])
    dinp("L0_ffn_gate_w", [E, D])
    dinp("L0_ffn_wg", [E, D, I]); dinp("L0_ffn_wu", [E, D, I])
    dinp("L0_ffn_wd", [E, I, D])
    dinp("identf", [128, 128])
    dinp("cosA", [128, T_ALL]); dinp("sinA", [128, T_ALL])
    dinp("swapf", [128, 128])
    dinp("flag_col", [128, 1]); dinp("pbias_col", [128, 1])

    out_own = nc.dram_tensor("out_own", [T_OWN, D], F32, kind="ExternalOutput").ap()

    dint = {}
    def dmk(name, shape):
        dint[name] = nc.dram_tensor(name, list(shape), BF16, kind="Internal").ap()

    dmk("xbf", [T_OWN, D]); dmk("xbfT", [D, T_ALL])
    dmk("t1_sp", [T_OWN, D])

    ctx = ExitStack()
    with ctx:
        sb = {}
        for name, shape, dtype, off in MEM:
            sb[name] = nc.alloc_sbuf_tensor_at(name, list(shape), dtype,
                                               offset=ORG + off)

        def psum(name, shape, dtype=F32):
            sb[name] = ctx.enter_context(nc.psum_tensor(name, list(shape), dtype))
        psum("W4", [128, 2048])
        psum("P0", [128, 512])
        psum("P1", [128, 512])
        psum("TB0", [128, 512], BF16)
        psum("TB1", [128, 512], BF16)

        cast_list = [
            ("ident", sb["identb"][:], din["identf"]),
            ("swap", sb["swapm"][:], din["swapf"]),
            ("gate", sb["gate8b"][:], din["L0_ffn_gate_w"]),
            ("sin", sb["sinb"][:], din["sinA"]),
            ("x0", dint["xbfT"], din["xT_all"]),
            ("x1", dint["xbf"], din["x_own"]),
        ]
        cast_idx = {name: i for i, (name, _, _) in enumerate(cast_list)}

        sems = {}
        for s in ("pe", "vec", "sc", "gp", "sy",
                  "cq0", "cq1", "cq2", "cq3",
                  "wq0", "wq1", "wq2", "wq3", "wq4", "wq5"):
            sems[s] = ctx.enter_context(nc.semaphore(s))

        tapout = {}
        for name in DEBUG_TAPS:
            shape, dt = TAP_SPECS[name]
            tapout[name] = nc.dram_tensor("tap_" + name, list(shape), dt,
                                          kind="ExternalOutput").ap()

        b = B(nc, sems, tapout)
        b._L = sb
        L0 = dict(sb)
        L0.update(din=din, dint=dint, out_own=out_own,
                  cast_list=cast_list, cast_idx=cast_idx)

        counts = []

        def run_section(name, eng):
            b.begin(name, eng)
            with nc.allow_low_precision(reason="bf16 operand storage by design"):
                program(b, dict(L0))
            counts.append(dict(b.count))
            if len(counts) > 1:
                assert counts[-1] == counts[0], \
                    f"walk divergence in section {name}"

        with nc.Block() as block:
            @block.sync
            def _(eng):
                run_section("sync", eng)

            @block.gpsimd
            def _(eng):
                run_section("gpsimd", eng)

            @block.tensor
            def _(eng):
                run_section("tensor", eng)

            @block.vector
            def _(eng):
                run_section("vector", eng)

            @block.scalar
            def _(eng):
                run_section("scalar", eng)

    return nc


def ROW(L, r, n=T_ALL):
    return L[("r0", "r1")[r]][0:1, 0:n]



def program(b, L):
    din = L["din"]

    cast_tok = {}
    for name, out_ap, in_ap in L["cast_list"]:
        cast_tok[name] = b.dma("gpsimd", out_ap, in_ap, inc="cast")
    L["cd"] = lambda key: cast_tok[key]

    consts_ld = []
    for nm, key in (("gammab", "gamma_t"), ("betab", "beta_t"),
                    ("wanorm", "w_anorm_t"), ("wfnorm", "w_fnorm_t"),
                    ("wqnc", "w_qn_t"), ("wknc", "w_kn_t"),
                    ("flagc", "flag_col"), ("pbiasc", "pbias_col")):
        consts_ld.append(b.dma("sync", L[nm][:], din[key], inc="wld"))

    b.wait_dma("vector", consts_ld)
    b.v("memset", L["ones"][:], 1.0)
    b.v("memset", L["wst"][:, WST_ZERO:WST_ZERO + 2048], 0.0, inc="vec")
    L["consts_v"] = b.val("vec")

    phase1(b, L)
    b.barrier()
    if BUILD_PHASES >= 2:
        phase_gla(b, L)
        b.barrier()
    if BUILD_PHASES >= 3:
        phase_wo(b, L)
        b.barrier()
    if BUILD_PHASES >= 4:
        phase_ffnproj(b, L)
        b.barrier()
        phase_attn(b, L)
        b.barrier()
    if BUILD_PHASES >= 5:
        phase_hidden(b, L)
        b.barrier()
        gate(b, L)
        b.barrier()
    if BUILD_PHASES >= 6:
        phase_moe(b, L)
    finish(b, L)


def ln_stats_fm(b, L, src_getter, tlen, with_mean, eps):
    P0, P1 = L["P0"], L["P1"]
    sq_pe = [0, 0]
    for tb in range(tlen // 512):
        t0 = tb * 512
        for ft in range(FT):
            xt = src_getter(ft, t0)
            scr = [L["scrbf"], L["scrbf2"]][ft % 2]
            b.wait("scalar", "pe", sq_pe[-2])
            b.act(scr[:, 0:512], xt, AF.Square, inc="sc")
            b.wait("tensor", "sc", b.val("sc"))
            if with_mean:
                b.mm(P0[0:1, :], L["ones"][:, 0:1], xt,
                     start=(ft == 0), stop=(ft == FT - 1))
            b.mm(P1[0:1, :], L["ones"][:, 0:1], scr[:, 0:512],
                 start=(ft == 0), stop=(ft == FT - 1), inc="pe")
            sq_pe.append(b.val("pe"))
        b.wait("vector", "pe", b.val("pe"))
        r0s = L["r0"][0:1, t0:t0 + 512]
        r1s = L["r1"][0:1, t0:t0 + 512]
        if with_mean:
            b.v("tensor_scalar", r0s, P0[0:1, :], 1.0 / D, None, ALU.mult)
        b.v("tensor_scalar", r1s, P1[0:1, :], 1.0 / D, None, ALU.mult)
        if with_mean:
            b.v("tensor_mul", P1[0:1, :], r0s, r0s)
            b.v("tensor_sub", r1s, r1s, P1[0:1, :])
        b.v("tensor_scalar", r1s, r1s, eps, None, ALU.add)
        b.v("reciprocal", r1s, r1s, inc="vec")
        b.wait("tensor", "vec", b.val("vec"))
    b.wait("scalar", "vec", b.val("vec"))
    b.act(ROW(L, 1, tlen), ROW(L, 1, tlen), AF.Sqrt, inc="sc")
    b.wait("tensor", "sc", b.val("sc"))
    b.wait("vector", "sc", b.val("sc"))


def bcast_mm(b, L, psum_slice, row_ap):
    b.mm(psum_slice, L["ones"][0:1, :], row_ap, inc="pe")


def phase1(b, L):
    xn, W4, P0, P1 = L["xn"], L["W4"], L["P0"], L["P1"]
    xbf = L["dint"]["xbf"]

    b.wait_dma("tensor", [L["cd"]("ident"), L["cd"]("gate")])
    gtr = [0, 0]
    for dc in range(FT):
        TB = [L["TB0"], L["TB1"]][dc % 2]
        b.wait("tensor", "sc", gtr[-2])
        b.tr(TB[0:128, 0:8], L["gate8b"][0:8, dc * 128:(dc + 1) * 128],
             L["identb"][0:8, 0:8], inc="pe")
        b.wait("scalar", "pe", b.val("pe"))
        b.scopy(L["gwT"][:, dc * 8:(dc + 1) * 8], TB[0:128, 0:8], inc="sc")
        gtr.append(b.val("sc"))

    b.wait_dma("sync", [L["cd"]("x0")])
    xbfT = L["dint"]["xbfT"]
    xld = []
    for ft in range(FT):
        xld.append(b.dma("sync", xn[:, ft * T_ALL:(ft + 1) * T_ALL],
                         xbfT[ft * 128:(ft + 1) * 128, :], inc="wld"))
    b.wait_dma("scalar", xld)
    b.wait_dma("tensor", xld)
    b.wait("tensor", "vec", L["consts_v"])
    ln_stats_fm(b, L, lambda ft, t0: xn[:, ft * T_ALL + t0: ft * T_ALL + t0 + 512],
                T_ALL, True, 1e-5)

    nrm_v = [0, 0]
    for tb in range(4):
        t0 = tb * 512
        mb = W4[:, (tb % 2) * 1024:(tb % 2) * 1024 + 512]
        rb = W4[:, (tb % 2) * 1024 + 512:(tb % 2) * 1024 + 1024]
        b.wait("tensor", "vec", nrm_v[-2])
        bcast_mm(b, L, mb, L["r0"][0:1, t0:t0 + 512])
        bcast_mm(b, L, rb, L["r1"][0:1, t0:t0 + 512])
        b.wait("vector", "pe", b.val("pe"))
        for ft in range(FT):
            xt = xn[:, ft * T_ALL + t0: ft * T_ALL + t0 + 512]
            b.v("tensor_sub", xt, xt, mb)
            b.v("tensor_mul", xt, xt, rb, inc="vec")
        nrm_v.append(b.val("vec"))
    L["xn_v"] = b.val("vec")

    b.wait_dma("tensor", L["cd"]("ident"))
    b.wait("tensor", "vec", L["xn_v"])
    prev = [0, 0]
    for tt in range(NT_OWN):
        t0 = T_PRI + tt * 128
        TB = [L["TB0"], L["TB1"]][tt % 2]
        b.wait("tensor", "sc", prev[-2])
        b.tr(TB[0:128, 0:1], L["r0"][0:1, t0:t0 + 128],
             L["identb"][0:1, 0:1], inc="pe")
        b.tr(TB[0:128, 2:3], L["r1"][0:1, t0:t0 + 128],
             L["identb"][0:1, 0:1], inc="pe")
        b.wait("scalar", "pe", b.val("pe"))
        b.scopy(L["statsT"][:, 2 * tt:2 * tt + 1], TB[0:128, 0:1])
        b.scopy(L["statsT"][:, 2 * tt + 1:2 * tt + 2], TB[0:128, 2:3], inc="sc")
        prev.append(b.val("sc"))
    L["statsT_sc"] = b.val("sc")

    b.tap("xn", xn[:], "vec", L["xn_v"])
    b.tap("rows", L["r0"][:], "vec", L["xn_v"])
    b.tap("statsT", L["statsT"][:], "sc", L["statsT_sc"])


def stream_wT(b, L, dram_f32, ckey, f0, f1, col0):
    w = f1 - f0
    toks = []
    for dc in range(FT):
        toks.append(b.dma("gpsimd",
                          L["wst"][:, col0 + dc * w: col0 + (dc + 1) * w],
                          dram_f32[dc * 128:(dc + 1) * 128, f0:f1], inc="cast"))
    return toks


def phase_gla(b, L):
    xn, Ac, wst = L["xn"], L["Acols"], L["wst"]
    W4, P0, P1 = L["W4"], L["P0"], L["P1"]
    dint = L["dint"]

    b.wait("tensor", "vec", L["xn_v"])
    b.wait_dma("tensor", L["cd"]("ident"))
    b.v("memset", L["vpad0"][:], 0.0)
    b.v("memset", L["vpad1"][:], 0.0)
    b.v("memset", L["Pt0"][:], 0.0)
    b.v("memset", L["Pt1"][:], 0.0, inc="vec")

    trD, evD, dsD, chD = {}, {}, {}, {}
    ptS, ptG, ymD, yeS = [0, 0], [0, 0], [0, 0], [0, 0]
    for h in range(GLA_HEADS):
        f0, f1 = h * 128, (h + 1) * 128
        par = h % 2
        hh2 = h % 2
        qkg0 = WST_QKG[par]

        if h >= 2:
            b.wait("gpsimd", "pe", L[f"gla_wused_{h - 2}"])
        ldq = stream_wT(b, L, L["din"]["L0_attn_Wq"], "wq", f0, f1, qkg0 + 0)
        ldk = stream_wT(b, L, L["din"]["L0_attn_Wk"], "wk", f0, f1, qkg0 + 2048)
        ldg = stream_wT(b, L, L["din"]["L0_attn_Wg"], "wg", f0, f1, qkg0 + 4096)

        if h % 2 == 0:
            grp = h // 2
            v0 = WST_V[par]
            if h >= 2:
                b.wait("gpsimd", "pe", L[f"gla_vused_{h - 2}"])
            ldv = stream_wT(b, L, L["din"]["L0_attn_Wv"], "wv", grp * 256, (grp + 1) * 256,
                            v0)
            b.wait_dma("tensor", ldv)
            if h > 0:
                b.wait("tensor", "vec", L[f"gla_head_done_{h - 1}"])
                b.wait("tensor", "sc", L[f"gla_ysc_{h - 1}"])
            vev = [0, 0]
            for tt in range(NT_ALL):
                pcol = (tt % 2) * 1024
                b.wait("tensor", "vec", vev[-2])
                for dc in range(FT):
                    b.mm(W4[:, pcol:pcol + 256],
                         xn[:, dc * T_ALL + tt * 128: dc * T_ALL + (tt + 1) * 128],
                         wst[:, v0 + dc * 256: v0 + (dc + 1) * 256],
                         start=(dc == 0), stop=(dc == FT - 1),
                         inc=("pe" if dc == FT - 1 else None))
                b.wait("vector", "pe", b.val("pe"))
                b.v("tensor_copy", L["vgla"][:, tt * 256:(tt + 1) * 256],
                    W4[:, pcol:pcol + 256], inc="vec")
                vev.append(b.val("vec"))
            L[f"gla_vused_{h}"] = b.val("pe")
            L[f"gla_vused_{h + 1}"] = b.val("pe")
            L["gla_vgrp_v"] = b.val("vec")

        b.wait_dma("tensor", ldq)
        if h >= 1:
            b.wait("tensor", "vec", L[f"gla_head_done_{h - 1}"])
            b.wait("scalar", "vec", L[f"gla_prep_v_{h - 1}"])
        if h % 2 == 0:
            b.wait("tensor", "vec", L["gla_vgrp_v"])
        for half in range(2):
            ps = [P0, P1][half]
            for dc in range(FT):
                b.mm(ps[:], wst[:, qkg0 + dc * 128: qkg0 + (dc + 1) * 128],
                     xn[:, dc * T_ALL + T_PRI + half * 512:
                        dc * T_ALL + T_PRI + (half + 1) * 512],
                     start=(dc == 0), stop=(dc == FT - 1),
                     inc=("pe" if dc == FT - 1 and half == 1 else None))
        b.wait("scalar", "pe", b.val("pe"))
        b.act(L["qraw"][:, 0:512], P0[:], AF.Copy, scale=QK_SCALE)
        b.act(L["qraw"][:, 512:1024], P1[:], AF.Copy, scale=QK_SCALE, inc="sc")
        qevac = b.val("sc")
        b.wait_dma("tensor", ldk)
        kevacs = []
        for half in range(4):
            ps = [P0, P1][half % 2]
            b.wait("tensor", "sc", qevac if half < 2 else kevacs[half - 2])
            for dc in range(FT):
                b.mm(ps[:], wst[:, qkg0 + 2048 + dc * 128: qkg0 + 2048 + (dc + 1) * 128],
                     xn[:, dc * T_ALL + half * 512: dc * T_ALL + (half + 1) * 512],
                     start=(dc == 0), stop=(dc == FT - 1),
                     inc=("pe" if dc == FT - 1 else None))
            b.wait("scalar", "pe", b.val("pe"))
            b.act(L["kraw"][:, half * 512:(half + 1) * 512], ps[:], AF.Copy, inc="sc")
            kevacs.append(b.val("sc"))
        b.wait_dma("tensor", ldg)
        gevacs = []
        for half in range(4):
            ps = [P0, P1][half % 2]
            b.wait("tensor", "sc", kevacs[2 + half % 2] if half < 2 else gevacs[half - 2])
            for dc in range(FT):
                b.mm(ps[:], wst[:, qkg0 + 4096 + dc * 128: qkg0 + 4096 + (dc + 1) * 128],
                     xn[:, dc * T_ALL + half * 512: dc * T_ALL + (half + 1) * 512],
                     start=(dc == 0), stop=(dc == FT - 1),
                     inc=("pe" if dc == FT - 1 else None))
            b.wait("scalar", "pe", b.val("pe"))
            b.act(L["gf32"][:, half * 512:(half + 1) * 512], ps[:], AF.Sigmoid, inc="sc")
            gevacs.append(b.val("sc"))
        b.act(L["gf32"][:], L["gf32"][:], AF.Ln, inc="sc")
        lsig_sc = b.val("sc")
        L[f"gla_wused_{h}"] = b.val("pe")

        b.wait("vector", "sc", lsig_sc)
        b.v("tensor_tensor_scan", L["Af32"][:], L["gf32"][:],
            wst[:, WST_ZERO:WST_ZERO + 2048], 0.0, ALU.add, ALU.add, inc="vec")
        for c in range(NCH):
            b.v("tensor_copy", Ac[:, 0 * NCH + c:0 * NCH + c + 1],
                L["Af32"][:, c * C + C // 2 - 1:c * C + C // 2])
            b.v("tensor_copy", Ac[:, 1 * NCH + c:1 * NCH + c + 1],
                L["Af32"][:, c * C + C - 1:c * C + C])
        b.v("memset", Ac[:, 2 * NCH:2 * NCH + 1], 0.0)
        b.v("tensor_copy", Ac[:, 2 * NCH + 1:3 * NCH], Ac[:, 1 * NCH:2 * NCH - 1])
        b.v("tensor_sub", Ac[:, 3 * NCH:4 * NCH], Ac[:, 1 * NCH:2 * NCH],
            Ac[:, 2 * NCH:3 * NCH])
        b.v("tensor_sub", Ac[:, 4 * NCH:5 * NCH], Ac[:, 1 * NCH:2 * NCH], Ac[:, 0:NCH])
        b.v("tensor_sub", Ac[:, 5 * NCH:6 * NCH], Ac[:, 0:NCH], Ac[:, 2 * NCH:3 * NCH])
        b.v("tensor_scalar", Ac[:, 2 * NCH:3 * NCH], Ac[:, 0:NCH], -1.0, None,
            ALU.mult, inc="vec")
        cols_v = b.val("vec")
        b.wait("scalar", "vec", cols_v)
        b.act(Ac[:, 3 * NCH:4 * NCH], Ac[:, 3 * NCH:4 * NCH], AF.Exp)
        b.act(Ac[:, 4 * NCH:5 * NCH], Ac[:, 4 * NCH:5 * NCH], AF.Exp)
        b.act(Ac[:, 5 * NCH:6 * NCH], Ac[:, 5 * NCH:6 * NCH], AF.Exp, inc="sc")
        for c in range(NCH):
            b.act(L["scrbf"][:, c * C:(c + 1) * C], L["Af32"][:, c * C:(c + 1) * C],
                  AF.Exp, scale=-1.0, bias=Ac[:, 0 * NCH + c:0 * NCH + c + 1],
                  inc=("sc" if c == NCH - 1 else None))
        ek_sc = b.val("sc")
        for co in range(OWNC):
            c = OWNC + co
            b.act(L["scrbf2"][:, co * C:(co + 1) * C], L["Af32"][:, c * C:(c + 1) * C],
                  AF.Exp, scale=1.0, bias=Ac[:, 2 * NCH + c:2 * NCH + c + 1],
                  inc=("sc" if co == OWNC - 1 else None))
        eq_sc = b.val("sc")
        b.wait("vector", "sc", ek_sc)
        b.v("tensor_mul", L["Ktil"][:], L["kraw"][:], L["scrbf"][:])
        b.wait("vector", "sc", eq_sc)
        b.v("tensor_mul", L["Qtil"][:], L["qraw"][:], L["scrbf2"][:, 0:T_OWN])
        for c in range(NCH):
            b.v("tensor_scalar", L["Khat"][:, c * C:(c + 1) * C],
                L["Ktil"][:, c * C:(c + 1) * C],
                Ac[:, 4 * NCH + c:4 * NCH + c + 1], None, ALU.mult)
        b.v("memset", L["Sfp"][:], 0.0, inc="vec")
        prep_v = b.val("vec")
        L[f"gla_prep_v_{h}"] = prep_v

        b.wait("tensor", "vec", prep_v)
        b.wait("gpsimd", "vec", prep_v)
        trS = [0, 0]
        for j in range(GLA_CHUNKS // 2):
            TB = [L["TB0"], L["TB1"]][j % 2]
            b.wait("tensor", "sc", trS[-2])
            b.tr(TB[0:128, 0:128], L["Khat"][:, j * 128:(j + 1) * 128],
                 L["identb"][:], inc="pe")
            b.wait("scalar", "pe", b.val("pe"))
            b.scopy(L["KhatT32"][:, j * 128:(j + 1) * 128], TB[0:128, 0:128],
                    inc="sc")
            trS.append(b.val("sc"))
        b.wait("tensor", "sc", b.val("sc"))
        for c in range(GLA_CHUNKS):
            cc = h * NCH + c
            sl = cc % 2
            bp = (c % 2) * 64
            tt = c // 2
            PD = [P0, P1][sl]
            vsl = L["vgla"][bp:bp + 64, tt * 256 + hh2 * 128: tt * 256 + (hh2 + 1) * 128]
            b.wait("tensor", "vec", chD.get(cc - 2, 0))
            b.mm(PD[0:128, 0:128],
                 L["KhatT32"][bp:bp + 64, (c // 2) * 128:(c // 2 + 1) * 128],
                 vsl, inc="pe")
            dsD[cc] = b.val("pe")
            if c >= OWNC:
                co = c - OWNC
                Ptp = [L["Pt0"], L["Pt1"]][c % 2]
                vpad = [L["vpad0"], L["vpad1"]][c % 2]
                b.wait("vector", "pe", ymD[-1])
                b.v("tensor_scalar", L["Qr"][:, 0:C],
                    L["Qtil"][:, co * C:(co + 1) * C],
                    Ac[:, 5 * NCH + c:5 * NCH + c + 1], None, ALU.mult)
                b.v("tensor_copy", L["Sbf"][:], L["Sfp"][:])
                b.v("tensor_copy", vpad[bp:bp + 64, :], vsl, inc="vec")
                qr_v = b.val("vec")
                if GLA_OWN_OPS >= 2:
                    b.wait("tensor", "vec", qr_v)
                    b.wait("tensor", "sc", ptS[-1])
                    b.mm(W4[bp:bp + 64, 1024:1088], L["Ktil"][:, c * C:(c + 1) * C],
                         L["Qtil"][:, co * C:(co + 1) * C], inc="pe")
                    b.wait("scalar", "pe", b.val("pe"))
                    b.wait("scalar", "gp", ptG[-2])
                    b.scopy(Ptp[bp:bp + 64, 0:C], W4[bp:bp + 64, 1024:1088],
                            inc="sc")
                    ptS.append(b.val("sc"))
                if GLA_OWN_OPS >= 3:
                    b.wait("gpsimd", "sc", ptS[-1])
                    b.gp("affine_select", Ptp[bp:bp + 64, 0:C],
                         Ptp[bp:bp + 64, 0:C], [[1, C]], ALU.is_ge, 0.0,
                         base=0, channel_multiplier=-1, inc="gp")
                    ptG.append(b.val("gp"))
                if GLA_OWN_OPS >= 4:
                    b.wait("tensor", "gp", ptG[-1])
                    b.wait("tensor", "sc", yeS[-1])
                    b.mm(W4[0:128, 1536:1600], L["Sbf"][:], L["Qr"][:, 0:C],
                         start=True, stop=False)
                    b.mm(W4[0:128, 1536:1600], vpad[:], Ptp[:, 0:C],
                         start=False, stop=True, inc="pe")
                    ymD.append(b.val("pe"))
                if GLA_OWN_OPS >= 5:
                    b.wait("scalar", "pe", ymD[-1])
                    b.scopy(L["yq"][:, h * T_OWN + co * C: h * T_OWN + (co + 1) * C],
                            W4[0:128, 1536:1600], inc="sc")
                    yeS.append(b.val("sc"))
            b.wait("vector", "pe", dsD[cc])
            if c == OWNC:
                b.v("tensor_scalar", L["Sfp"][:], L["Sfp"][:], L["flagc"][:, 0:1],
                    None, ALU.mult)
            b.v("scalar_tensor_tensor", L["Sfp"][:], L["Sfp"][:],
                Ac[:, 3 * NCH + c:3 * NCH + c + 1], PD[0:128, 0:128],
                ALU.mult, ALU.add, inc="vec")
            chD[cc] = b.val("vec")
        L[f"gla_head_done_{h}"] = b.val("vec")
        L[f"gla_ysc_{h}"] = b.val("sc")
    L["y_sc"] = b.val("sc")
    b.tap("y", L["yq"][:], "sc", L["y_sc"])


def phase_wo(b, L):
    yq, wst, W4 = L["yq"], L["wst"], L["W4"]
    dint = L["dint"]

    b.wait("scalar", "sc", L["y_sc"])
    b.wait("tensor", "sc", L["y_sc"])
    b.wait("gpsimd", "sc", L["y_sc"])
    L["cos_tok"] = b.dma("gpsimd", L["cosb"][:], L["din"]["cosA"], inc="cast")
    ln_stats_fm(b, L, lambda ft, t0: yq[:, ft * T_OWN + t0: ft * T_OWN + t0 + 512],
                T_OWN, True, 1e-5)
    if WO_STAGE < 2:
        L["t1sp_ld"] = []
        for tt in range(NT_OWN):
            L[f"wo_grp_pe_{tt}"] = 0
        L["t1_gla_v"] = b.val("vec")
        return
    nrm_v = [0, 0]
    for tb in range(2):
        t0 = tb * 512
        mb = W4[:, (tb % 2) * 1024:(tb % 2) * 1024 + 512]
        rb = W4[:, (tb % 2) * 1024 + 512:(tb % 2) * 1024 + 1024]
        b.wait("tensor", "vec", nrm_v[-2])
        bcast_mm(b, L, mb, L["r0"][0:1, t0:t0 + 512])
        bcast_mm(b, L, rb, L["r1"][0:1, t0:t0 + 512])
        b.wait("vector", "pe", b.val("pe"))
        for hh in range(H):
            yt = yq[:, hh * T_OWN + t0: hh * T_OWN + t0 + 512]
            b.v("tensor_sub", yt, yt, mb)
            b.v("tensor_mul", yt, yt, rb)
            b.v("tensor_scalar", yt, yt, L["gammab"][:, hh:hh + 1],
                L["betab"][:, hh:hh + 1], ALU.mult, ALU.add, inc="vec")
        nrm_v.append(b.val("vec"))
    yn_v = b.val("vec")
    if WO_STAGE < 3:
        L["t1sp_ld"] = []
        L["t1_gla_v"] = b.val("vec")
        return

    xl = []
    b.wait_dma("sync", L["cd"]("x1"))
    for tt in range(NT_OWN):
        xl.append(b.dma("sync", L["t1a"][:, tt * D:(tt + 1) * D],
                        dint["xbf"][tt * 128:(tt + 1) * 128, :],
                        inc="wld"))
    acc_v = [0, 0, 0, 0]
    for grp in range(4):
        par = grp % 2
        if grp >= 2:
            b.wait("gpsimd", "pe", L[f"wo_grp_pe_{grp - 2}"])
        ld = stream_wT(b, L, L["din"]["L0_attn_Wo"], "wo", grp * 512, (grp + 1) * 512,
                       par * 8192)
        b.wait_dma("tensor", ld)
        b.wait("tensor", "vec", yn_v)
        for tt in range(NT_OWN):
            bk = tt % 4
            b.wait("tensor", "vec", acc_v[-4])
            if grp == 0:
                b.wait_dma("vector", xl[tt])
            for fc in range(FT):
                b.mm(W4[:, bk * 512:(bk + 1) * 512],
                     yq[:, fc * T_OWN + tt * 128: fc * T_OWN + (tt + 1) * 128],
                     wst[:, par * 8192 + fc * 512: par * 8192 + (fc + 1) * 512],
                     start=(fc == 0), stop=(fc == FT - 1),
                     inc=("pe" if fc == FT - 1 else None))
            b.wait("vector", "pe", b.val("pe"))
            sl1 = L["t1a"][:, tt * D + grp * 512: tt * D + grp * 512 + 512]
            b.v("tensor_add", sl1, sl1, W4[:, bk * 512:(bk + 1) * 512], inc="vec")
            acc_v.append(b.val("vec"))
        L[f"wo_grp_pe_{grp}"] = b.val("pe")
    t1_v = b.val("vec")
    if WO_STAGE >= 4:
        b.wait("sync", "vec", t1_v)
        sp = []
        for tt in range(NT_OWN):
            sp.append(b.dma("sync", dint["t1_sp"][tt * 128:(tt + 1) * 128, :],
                            L["t1a"][:, tt * D:(tt + 1) * D], inc="wld"))
        L["t1sp_ld"] = sp
    else:
        L["t1sp_ld"] = []
    b.tap("t1a", L["t1a"][:], "vec", t1_v)


def phase_ffnproj(b, L):
    xn, wst, W4, P0, P1 = L["xn"], L["wst"], L["W4"], L["P0"], L["P1"]
    dint = L["dint"]

    b.wait("scalar", "vec", L["xn_v"])
    ln_stats_fm(b, L, lambda ft, t0: xn[:, ft * T_ALL + t0: ft * T_ALL + t0 + 512],
                T_ALL, False, 1e-6)
    h_v = [0, 0]
    for tb in range(4):
        t0 = tb * 512
        rb = W4[:, (tb % 2) * 512:(tb % 2) * 512 + 512]
        b.wait("tensor", "vec", h_v[-2])
        bcast_mm(b, L, rb, L["r1"][0:1, t0:t0 + 512])
        b.wait("vector", "pe", b.val("pe"))
        for ft in range(FT):
            xt = xn[:, ft * T_ALL + t0: ft * T_ALL + t0 + 512]
            b.v("tensor_scalar", xt, xt, L["wanorm"][:, ft:ft + 1], None, ALU.mult)
            b.v("tensor_mul", xt, xt, rb, inc="vec")
        h_v.append(b.val("vec"))
    L["h_v"] = b.val("vec")

    ldk = stream_wT(b, L, L["din"]["L0_ffn_k_w"], "fk", 0, 512, 0)
    b.wait_dma("vector", L["cos_tok"])
    b.wait("tensor", "vec", L["h_v"])
    b.wait_dma("tensor", ldk)
    b.wait_dma("tensor", L["t1sp_ld"])
    for kh in range(HKV):
        b.wait("scalar", "vec", b.val("vec"))
        kev = []
        for half in range(4):
            ps = [P0, P1][half % 2]
            if half >= 2:
                b.wait("tensor", "sc", kev[half - 2])
            for dc in range(FT):
                b.mm(ps[:], wst[:, dc * 512 + kh * 128: dc * 512 + (kh + 1) * 128],
                     xn[:, dc * T_ALL + half * 512: dc * T_ALL + (half + 1) * 512],
                     start=(dc == 0), stop=(dc == FT - 1),
                     inc=("pe" if dc == FT - 1 else None))
            b.wait("scalar", "pe", b.val("pe"))
            b.act(L["scrbf"][:, half * 512:(half + 1) * 512], ps[:], AF.Copy, inc="sc")
            kev.append(b.val("sc"))
        rope_norm(b, L, src=L["scrbf"], dst=L["kv1"], dst_off=kh * T_ALL,
                  tlen=T_ALL, wcol=L["wknc"], src_sc=kev[-1], tpos_off=0)
    L["khat_v"] = b.val("vec")
    L["kproj_pe"] = b.val("pe")

    ldv = stream_wT(b, L, L["din"]["L0_ffn_v_w"], "fv", 0, 512, 8192)
    b.wait_dma("tensor", ldv)
    b.wait("tensor", "vec", L["khat_v"])
    vev = [0, 0]
    for tt in range(NT_ALL):
        pcol = 1024 + (tt % 2) * 512
        b.wait("tensor", "vec", vev[-2])
        for dc in range(FT):
            b.mm(W4[:, pcol:pcol + 512],
                 xn[:, dc * T_ALL + tt * 128: dc * T_ALL + (tt + 1) * 128],
                 wst[:, 8192 + dc * 512: 8192 + (dc + 1) * 512],
                 start=(dc == 0), stop=(dc == FT - 1),
                 inc=("pe" if dc == FT - 1 else None))
        b.wait("vector", "pe", b.val("pe"))
        b.v("tensor_copy", L["kv2"][:, tt * 512:(tt + 1) * 512],
            W4[:, pcol:pcol + 512], inc="vec")
        vev.append(b.val("vec"))
    L["vffn_v"] = b.val("vec")
    L["vproj_pe"] = b.val("pe")

    for qg in range(4):
        par = qg % 2
        if qg >= 2:
            b.wait("gpsimd", "pe", L[f"fp_qg_pe_{qg - 2}"])
        elif qg == 0:
            b.wait("gpsimd", "pe", L["kproj_pe"])
        else:
            b.wait("gpsimd", "pe", L["vproj_pe"])
        ld = stream_wT(b, L, L["din"]["L0_ffn_q_w"], "fq", qg * 512, (qg + 1) * 512,
                       par * 8192)
        b.wait_dma("tensor", ld)
        for hi in range(4):
            qh = qg * 4 + hi
            b.wait("tensor", "vec", b.val("vec"))
            b.wait("scalar", "vec", b.val("vec"))
            for half in range(2):
                ps = [P0, P1][half]
                for dc in range(FT):
                    b.mm(ps[:], wst[:, par * 8192 + dc * 512 + hi * 128:
                                    par * 8192 + dc * 512 + (hi + 1) * 128],
                         xn[:, dc * T_ALL + T_PRI + half * 512:
                            dc * T_ALL + T_PRI + (half + 1) * 512],
                         start=(dc == 0), stop=(dc == FT - 1),
                         inc=("pe" if dc == FT - 1 and half == 1 else None))
            b.wait("scalar", "pe", b.val("pe"))
            b.act(L["scrbf"][:, 0:512], P0[:], AF.Copy)
            b.act(L["scrbf"][:, 512:1024], P1[:], AF.Copy, inc="sc")
            rope_norm(b, L, src=L["scrbf"], dst=L["yq"], dst_off=qh * T_OWN,
                      tlen=T_OWN, wcol=L["wqnc"], src_sc=b.val("sc"),
                      tpos_off=T_PRI)
        L[f"fp_qg_pe_{qg}"] = b.val("pe")
    L["qhat_v"] = b.val("vec")

    bounds(b, L)
    b.tap("qhat", L["yq"][:], "vec", L["qhat_v"])
    b.tap("khat", L["kv1"][:], "vec", L["khat_v"])
    b.tap("vffn", L["kv2"][:], "vec", L["vffn_v"])
    b.tap("negM", L["negM"][:], "vec", L["negM_v"])


def rope_norm(b, L, src, dst, dst_off, tlen, wcol, src_sc, tpos_off):
    P0, W4 = L["P0"], L["W4"]
    nb = tlen // 512
    b.wait("scalar", "sc", src_sc)
    mm_pe = [0, 0]
    for i in range(nb):
        t0 = i * 512
        b.wait("scalar", "pe", mm_pe[-2])
        b.act(L["scrbf2"][:, 512:1024], src[:, t0:t0 + 512], AF.Square, inc="sc")
        b.wait("tensor", "sc", b.val("sc"))
        b.mm(P0[0:1, :], L["ones"][:, 0:1], L["scrbf2"][:, 512:1024], inc="pe")
        mm_pe.append(b.val("pe"))
        b.wait("vector", "pe", b.val("pe"))
        r1s = L["r1"][0:1, t0:t0 + 512]
        b.v("tensor_scalar", r1s, P0[0:1, :], 1.0 / HD, None, ALU.mult)
        b.v("tensor_scalar", r1s, r1s, 1e-6, None, ALU.add)
        b.v("reciprocal", r1s, r1s, inc="vec")
        b.wait("scalar", "vec", b.val("vec"))
    b.act(ROW(L, 1, tlen), ROW(L, 1, tlen), AF.Sqrt, inc="sc")
    b.wait("tensor", "sc", b.val("sc"))
    rp = [0, 0]
    for i in range(nb):
        t0 = i * 512
        sl = i % 2
        rb = W4[:, sl * 512: sl * 512 + 512]
        sw = W4[:, 1024 + sl * 512: 1024 + sl * 512 + 512]
        b.wait("tensor", "vec", rp[-2])
        bcast_mm(b, L, rb, L["r1"][0:1, t0:t0 + 512])
        b.wait("vector", "pe", b.val("pe"))
        kn = L["scrbf2"][:, 0:512]
        tmp = L["scrbf2"][:, 512:1024]
        b.v("tensor_scalar", kn, src[:, t0:t0 + 512], wcol[:, 0:1], None, ALU.mult)
        b.v("tensor_mul", kn, kn, rb, inc="vec")
        kn_v = b.val("vec")
        b.wait("tensor", "vec", kn_v)
        b.mm(sw, L["swapm"][:], kn, inc="pe")
        b.wait("vector", "pe", b.val("pe"))
        cs = L["cosb"][:, tpos_off + t0: tpos_off + t0 + 512]
        sn = L["sinb"][:, tpos_off + t0: tpos_off + t0 + 512]
        do = dst[:, dst_off + t0: dst_off + t0 + 512]
        b.v("tensor_mul", do, kn, cs)
        b.v("tensor_mul", tmp, sw, sn)
        b.v("tensor_add", do, do, tmp, inc="vec")
        rp.append(b.val("vec"))


def bounds(b, L):
    P0 = L["P0"]
    b.wait("scalar", "vec", max(L["qhat_v"], L["khat_v"]))
    b.wait("tensor", "vec", max(L["qhat_v"], L["khat_v"]))
    mm_pe = [0, 0]

    def max_norm2(src, src_off, tlen, out_cell):
        for i in range(tlen // 512):
            t0 = i * 512
            sl = i % 2
            b.wait("scalar", "pe", mm_pe[-2])
            b.act(L["scrbf2"][:, sl * 512:(sl + 1) * 512],
                  src[:, src_off + t0: src_off + t0 + 512], AF.Square, inc="sc")
            b.wait("tensor", "sc", b.val("sc"))
            b.mm(P0[0:1, :], L["ones"][:, 0:1],
                 L["scrbf2"][:, sl * 512:(sl + 1) * 512], inc="pe")
            mm_pe.append(b.val("pe"))
            b.wait("vector", "pe", b.val("pe"))
            b.v("tensor_copy", L["r1"][0:1, t0:t0 + 512], P0[0:1, :], inc="vec")
            b.wait("tensor", "vec", b.val("vec"))
        b.v("reduce_max", out_cell, ROW(L, 1, tlen), axis=AXX, inc="vec")

    for kh in range(HKV):
        max_norm2(L["kv1"], kh * T_ALL, T_ALL, L["snorm"][0:1, 32 + kh:33 + kh])
    for qh in range(HQ):
        max_norm2(L["yq"], qh * T_OWN, T_OWN, L["snorm"][0:1, qh:qh + 1])
    for qh in range(HQ):
        kh = qh // (HQ // HKV)
        b.v("tensor_mul", L["snorm"][0:1, 48 + qh:49 + qh],
            L["snorm"][0:1, qh:qh + 1], L["snorm"][0:1, 32 + kh:33 + kh], inc="vec")
    b.wait("scalar", "vec", b.val("vec"))
    b.act(L["snorm"][0:1, 48:48 + HQ], L["snorm"][0:1, 48:48 + HQ], AF.Sqrt, inc="sc")
    b.wait("tensor", "sc", b.val("sc"))
    b.mm(L["W4"][:, 0:16], L["ones"][0:1, :], L["snorm"][0:1, 48:48 + HQ], inc="pe")
    b.wait("vector", "pe", b.val("pe"))
    b.v("tensor_scalar", L["negM"][:, 0:HQ], L["W4"][:, 0:16], -ATT_SCALE, None,
        ALU.mult)
    b.v("tensor_scalar", L["negM"][:, HQ:2 * HQ], L["negM"][:, 0:HQ],
        L["pbiasc"][:, 0:1], None, ALU.add, inc="vec")
    b.wait("tensor", "vec", b.val("vec"))
    L["negM_v"] = b.val("vec")


def phase_attn(b, L):
    P0, P1, W4 = L["P0"], L["P1"], L["W4"]
    attn = L["big2"]
    b.wait("tensor", "vec", L["negM_v"])
    b.wait("scalar", "vec", L["negM_v"])
    st_sc, pv_pe, msk_gp = {}, {}, {}
    i = 0
    for qh in range(HQ):
        kh = qh // (HQ // HKV)
        for blk in range(2):
            q0 = blk * 512
            nkt = 8 + (blk + 1) * 4
            diag0 = 8 + blk * 4
            for kt in range(nkt):
                sl = i % 2
                ps = [P0, P1][sl]
                b.wait("tensor", "sc", st_sc.get(i - 2, 0))
                b.mm(ps[:], L["kv1"][:, kh * T_ALL + kt * 128:
                                     kh * T_ALL + (kt + 1) * 128],
                     L["yq"][:, qh * T_OWN + q0: qh * T_OWN + q0 + 512], inc="pe")
                b.wait("scalar", "pe", b.val("pe"))
                b.wait("scalar", "pe", pv_pe.get(i - 2, 0))
                b.wait("scalar", "gp", msk_gp.get(i - 2, 0))
                bias = L["negM"][:, HQ + qh:HQ + qh + 1] if kt < 8 else \
                    L["negM"][:, qh:qh + 1]
                b.act(L["scrbf"][:, sl * 512:(sl + 1) * 512], ps[:], AF.Exp,
                      bias=bias, scale=ATT_SCALE, inc="sc")
                st_sc[i] = b.val("sc")
                if kt >= diag0:
                    b.wait("gpsimd", "sc", st_sc[i])
                    base = (T_PRI + q0) - kt * 128
                    b.gp("affine_select", L["scrbf"][:, sl * 512:(sl + 1) * 512],
                         L["scrbf"][:, sl * 512:(sl + 1) * 512], [[1, 512]],
                         ALU.is_ge, 0.0, base=base, channel_multiplier=-1,
                         inc="gp")
                    msk_gp[i] = b.val("gp")
                    b.wait("tensor", "gp", msk_gp[i])
                else:
                    b.wait("tensor", "sc", st_sc[i])
                b.mm(W4[:, 0:512],
                     L["kv2"][:, kt * 512 + kh * 128: kt * 512 + (kh + 1) * 128],
                     L["scrbf"][:, sl * 512:(sl + 1) * 512],
                     start=(kt == 0), stop=(kt == nkt - 1))
                b.mm(W4[0:1, 512:1024], L["ones"][:, 0:1],
                     L["scrbf"][:, sl * 512:(sl + 1) * 512],
                     start=(kt == 0), stop=(kt == nkt - 1), inc="pe")
                pv_pe[i] = b.val("pe")
                i += 1
            b.wait("scalar", "pe", b.val("pe"))
            b.act(L["denomrow"][0:1, 0:512], W4[0:1, 512:1024], AF.Ln)
            b.act(L["r1"][0:1, 0:512], L["denomrow"][0:1, 0:512], AF.Exp,
                  scale=-1.0, inc="sc")
            b.wait("tensor", "sc", b.val("sc"))
            b.mm(W4[:, 1024:1536], L["ones"][0:1, :], L["r1"][0:1, 0:512],
                 inc="pe")
            b.wait("scalar", "pe", b.val("pe"))
            b.scopy(L["scrbf2"][:, 0:512], W4[:, 1024:1536], inc="sc")
            b.wait("vector", "sc", b.val("sc"))
            b.v("tensor_mul", attn[:, qh * T_OWN + q0: qh * T_OWN + q0 + 512],
                W4[:, 0:512], L["scrbf2"][:, 0:512], inc="vec")
            b.wait("tensor", "vec", b.val("vec"))
    L["attn_v"] = b.val("vec")
    b.tap("attnp", attn[:], "vec", L["attn_v"])


def phase_hidden(b, L):
    wst, W4 = L["wst"], L["W4"]
    attn = L["big2"]
    dint = L["dint"]

    def hid_tile(tt):
        buf = L["kv1"] if tt < 4 else L["kv2"]
        off = (tt % 4) * D
        return buf[:, off:off + D]

    rl = []
    b.wait_dma("sync", L["t1sp_ld"])
    for tt in range(NT_OWN):
        rl.append(b.dma("sync", L["t1r"][:, tt * D:(tt + 1) * D],
                        dint["t1_sp"][tt * 128:(tt + 1) * 128, :], inc="wld"))
    b.wait_dma("vector", rl)
    b.wait("tensor", "vec", L["attn_v"])

    acc_v = [0, 0, 0, 0]
    xn_v = {}
    for grp in range(4):
        par = grp % 2
        if grp >= 2:
            b.wait("gpsimd", "pe", L[f"hid_grp_pe_{grp - 2}"])
        ld = stream_wT(b, L, L["din"]["L0_ffn_o_w"], "fo", grp * 512, (grp + 1) * 512,
                       par * 8192)
        b.wait_dma("tensor", ld)
        for tt in range(NT_OWN):
            bk = tt % 4
            if grp == 0:
                if tt >= 1:
                    b.wait("sync", "vec", xn_v[tt - 1])
                xt_tok = b.dma("sync", L["xtile"][:],
                               L["din"]["x_own"][tt * 128:(tt + 1) * 128, :],
                               inc="wld")
                b.wait_dma("vector", xt_tok)
                b.wait("vector", "sc", L["statsT_sc"])
                b.v("tensor_scalar", hid_tile(tt), L["xtile"][:],
                    L["statsT"][:, 2 * tt:2 * tt + 1],
                    L["statsT"][:, 2 * tt + 1:2 * tt + 2],
                    ALU.subtract, ALU.mult)
                b.v("tensor_add", L["t1r"][:, tt * D:(tt + 1) * D],
                    L["t1r"][:, tt * D:(tt + 1) * D], hid_tile(tt), inc="vec")
                xn_v[tt] = b.val("vec")
            b.wait("tensor", "vec", acc_v[-4])
            if grp == 0:
                b.wait("tensor", "vec", xn_v[tt])
            for fc in range(FT):
                b.mm(W4[:, bk * 512:(bk + 1) * 512],
                     attn[:, fc * T_OWN + tt * 128: fc * T_OWN + (tt + 1) * 128],
                     wst[:, par * 8192 + fc * 512: par * 8192 + (fc + 1) * 512],
                     start=(fc == 0), stop=(fc == FT - 1),
                     inc=("pe" if fc == FT - 1 else None))
            b.wait("vector", "pe", b.val("pe"))
            ps = W4[:, bk * 512:(bk + 1) * 512]
            sl1 = L["t1r"][:, tt * D + grp * 512: tt * D + grp * 512 + 512]
            b.v("tensor_add", sl1, sl1, ps)
            hs = hid_tile(tt)[:, grp * 512:(grp + 1) * 512]
            b.v("tensor_add", hs, hs, ps, inc="vec")
            acc_v.append(b.val("vec"))
        L[f"hid_grp_pe_{grp}"] = b.val("pe")
    L["t1_attn_v"] = b.val("vec")

    b.wait("tensor", "vec", L["t1_attn_v"])
    trs = [0, 0]
    for tt in range(NT_OWN):
        for fc in range(FT):
            sl = (tt * FT + fc) % 2
            TB = [L["TB0"], L["TB1"]][sl]
            b.wait("tensor", "sc", trs[-2])
            b.tr(TB[:, 0:128], hid_tile(tt)[:, fc * 128:(fc + 1) * 128],
                 L["identb"][:], inc="pe")
            b.wait("scalar", "pe", b.val("pe"))
            b.scopy(L["hfm"][:, fc * T_OWN + tt * 128: fc * T_OWN + (tt + 1) * 128],
                    TB[:, 0:128], inc="sc")
            trs.append(b.val("sc"))
    hfm_sc = b.val("sc")
    L["hfm_sc"] = hfm_sc
    b.tap("hfm", L["hfm"][:], "sc", hfm_sc)

    b.wait("scalar", "sc", hfm_sc)
    b.wait("tensor", "sc", hfm_sc)
    ln_stats_fm(b, L, lambda ft, t0: L["hfm"][:, ft * T_OWN + t0:
                                              ft * T_OWN + t0 + 512],
                T_OWN, False, 1e-6)
    h2v = [0, 0]
    for tb in range(2):
        t0 = tb * 512
        rb = W4[:, (tb % 2) * 512:(tb % 2) * 512 + 512]
        b.wait("tensor", "vec", h2v[-2])
        bcast_mm(b, L, rb, L["r1"][0:1, t0:t0 + 512])
        b.wait("vector", "pe", b.val("pe"))
        for ft in range(FT):
            src = L["hfm"][:, ft * T_OWN + t0: ft * T_OWN + t0 + 512]
            dst = L["yq"][:, ft * T_OWN + t0: ft * T_OWN + t0 + 512]
            b.v("tensor_scalar", dst, src, L["wfnorm"][:, ft:ft + 1], None, ALU.mult)
            b.v("tensor_mul", dst, dst, rb, inc="vec")
        h2v.append(b.val("vec"))
    L["h2_v"] = b.val("vec")
    b.tap("h2", L["yq"][:], "vec", L["h2_v"])


def gate(b, L):
    P0 = L["P0"]
    h2 = L["yq"]
    G8, M8, GS = L["grow8"], L["masked8"], L["gsc"]
    b.wait("tensor", "vec", L["h2_v"])
    for tt in range(NT_OWN):
        for dc in range(FT):
            b.mm(P0[0:128, tt * 8:(tt + 1) * 8],
                 h2[:, dc * T_OWN + tt * 128: dc * T_OWN + (tt + 1) * 128],
                 L["gwT"][:, dc * 8:(dc + 1) * 8],
                 start=(dc == 0), stop=(dc == FT - 1),
                 inc=("pe" if dc == FT - 1 else None))
    b.wait("vector", "pe", b.val("pe"))
    for tt in range(NT_OWN):
        b.v("reduce_max", GS[:, 24 + tt:25 + tt], P0[0:128, tt * 8:(tt + 1) * 8],
            axis=AXX, negate=True, inc=("vec" if tt == NT_OWN - 1 else None))
    b.wait("scalar", "vec", b.val("vec"))
    for tt in range(NT_OWN):
        b.act(G8[:, tt * 8:(tt + 1) * 8], P0[0:128, tt * 8:(tt + 1) * 8], AF.Exp,
              bias=GS[:, 24 + tt:25 + tt], accum_out=GS[:, 16 + tt:17 + tt],
              inc=("sc" if tt == NT_OWN - 1 else None))
    b.wait("vector", "sc", b.val("sc"))
    for tt in range(NT_OWN):
        b.v("reduce_max", GS[:, tt:tt + 1], G8[:, tt * 8:(tt + 1) * 8], axis=AXX)
    for tt in range(NT_OWN):
        b.v("tensor_scalar", M8[:, tt * 8:(tt + 1) * 8], G8[:, tt * 8:(tt + 1) * 8],
            GS[:, tt:tt + 1], None, ALU.is_ge)
    for tt in range(NT_OWN):
        b.v("scalar_tensor_tensor", M8[:, tt * 8:(tt + 1) * 8],
            M8[:, tt * 8:(tt + 1) * 8], -1e30, G8[:, tt * 8:(tt + 1) * 8],
            ALU.mult, ALU.add)
    for tt in range(NT_OWN):
        b.v("reduce_max", GS[:, 8 + tt:9 + tt], M8[:, tt * 8:(tt + 1) * 8],
            axis=AXX)
    for tt in range(NT_OWN):
        b.v("tensor_add", GS[:, 24 + tt:25 + tt], GS[:, tt:tt + 1],
            GS[:, 8 + tt:9 + tt])
    for tt in range(NT_OWN):
        b.v("scalar_tensor_tensor", GS[:, 24 + tt:25 + tt],
            GS[:, 16 + tt:17 + tt], 1e-9, GS[:, 24 + tt:25 + tt],
            ALU.mult, ALU.add)
    for tt in range(NT_OWN):
        b.v("reciprocal", GS[:, 32 + tt:33 + tt], GS[:, 24 + tt:25 + tt])
    for tt in range(NT_OWN):
        b.v("scalar_tensor_tensor", L["cvals"][:, tt * 8:(tt + 1) * 8],
            G8[:, tt * 8:(tt + 1) * 8], GS[:, 8 + tt:9 + tt],
            G8[:, tt * 8:(tt + 1) * 8], ALU.is_ge, ALU.mult)
    for tt in range(NT_OWN):
        b.v("tensor_scalar", L["cvals"][:, tt * 8:(tt + 1) * 8],
            L["cvals"][:, tt * 8:(tt + 1) * 8], GS[:, 32 + tt:33 + tt], None,
            ALU.mult, inc=("vec" if tt == NT_OWN - 1 else None))
    L["cvals_v"] = b.val("vec")
    b.tap("cvals", L["cvals"][:], "vec", L["cvals_v"])


def phase_moe(b, L):
    W4, P0, P1 = L["W4"], L["P0"], L["P1"]
    h2 = L["yq"]
    act = L["hfm"]
    groups = [(0, 3), (3, 3), (6, 3), (9, 2)]

    def wtile(buf, dc):
        if buf == 0:
            return L["wst"][:, dc * 1408:(dc + 1) * 1408]
        if dc < 11:
            return L["mb1a"][:, dc * 1408:(dc + 1) * 1408]
        if dc < 13:
            return L["mb1b"][:, (dc - 11) * 1408:(dc - 10) * 1408]
        if dc < 15:
            return L["mb1c"][:, (dc - 13) * 1408:(dc - 12) * 1408]
        return L["mb1d"][:, 0:1408]

    def wdtile(buf, ic):
        if buf == 0:
            return L["wst"][:, ic * 2048:(ic + 1) * 2048]
        if ic < 8:
            return L["mb1a"][:, ic * 2048:(ic + 1) * 2048]
        if ic < 10:
            return L["mb1b"][:, (ic - 8) * 2048:(ic - 7) * 2048]
        return L["mb1c"][:, 0:2048]

    b.wait("vector", "vec", L["cvals_v"])
    b.wait("tensor", "vec", L["cvals_v"])

    pass_pe = {-2: 0, -1: 0}
    ld_done = {}
    prev_act_sc = 0
    prev_eu_v = 0
    out_v = [0] * NT_OWN

    def emit_loads(p):
        e, wi = p // 3, p % 3
        buf = p % 2
        dram = (L["din"]["L0_ffn_wg"], L["din"]["L0_ffn_wu"],
                L["din"]["L0_ffn_wd"])[wi][e]
        b.wait("gpsimd", "pe", pass_pe[p - 2])
        toks = []
        if wi < 2:
            for dc in range(FT):
                toks.append(b.dma("gpsimd", wtile(buf, dc),
                                  dram[dc * 128:(dc + 1) * 128, :],
                                  inc="cast"))
        else:
            for ic in range(IC):
                toks.append(b.dma("gpsimd", wdtile(buf, ic),
                                  dram[ic * 128:(ic + 1) * 128, :],
                                  inc="cast"))
        ld_done[p] = toks

    def emit_compute(p):
        nonlocal prev_act_sc, prev_eu_v
        e, wi = p // 3, p % 3
        buf = p % 2
        b.wait_dma("tensor", ld_done[p])
        if wi == 0 and e > 0:
            b.wait("tensor", "vec", out_v[NT_OWN - 1])
        if wi < 2:
            for g0, gn in groups:
                b.wait("tensor", "sc", prev_act_sc)
                if wi == 1:
                    b.wait("tensor", "vec", prev_eu_v)
                for dc in range(FT):
                    for gi in range(gn):
                        ic = g0 + gi
                        lhs = wtile(buf, dc)[:, ic * 128:(ic + 1) * 128]
                        for th in range(2):
                            if gi < 2:
                                ps = W4[:, gi * 1024 + th * 512:
                                        gi * 1024 + th * 512 + 512]
                            else:
                                ps = [P0, P1][th][:]
                            b.mm(ps, lhs,
                                 h2[:, dc * T_OWN + th * 512:
                                    dc * T_OWN + (th + 1) * 512],
                                 start=(dc == 0), stop=(dc == FT - 1),
                                 inc=("pe" if (dc == FT - 1 and gi == gn - 1
                                               and th == 1) else None))
                grp_pe = b.val("pe")
                for gi in range(gn):
                    ic = g0 + gi
                    for th in range(2):
                        ps = (W4[:, gi * 1024 + th * 512: gi * 1024 + th * 512 + 512]
                              if gi < 2 else [P0, P1][th][:])
                        sl = act[:, ic * T_OWN + th * 512: ic * T_OWN + (th + 1) * 512]
                        if wi == 0:
                            b.wait("scalar", "pe", grp_pe)
                            b.act(sl, ps, AF.Silu, inc="sc")
                        else:
                            b.wait("vector", "pe", grp_pe)
                            b.v("tensor_tensor", sl, sl, ps, ALU.mult, inc="vec")
                if wi == 0:
                    prev_act_sc = b.val("sc")
                else:
                    prev_eu_v = b.val("vec")
        else:
            b.wait("tensor", "vec", prev_eu_v)
            b.wait("tensor", "sc", prev_act_sc)
            for tt in range(NT_OWN):
                b.wait("tensor", "vec", out_v[tt - 1] if tt > 0 else
                       out_v[NT_OWN - 1])
                for ic in range(IC):
                    for nbk in range(4):
                        b.mm(W4[:, nbk * 512:(nbk + 1) * 512],
                             act[:, ic * T_OWN + tt * 128:
                                 ic * T_OWN + (tt + 1) * 128],
                             wdtile(buf, ic)[:, nbk * 512:(nbk + 1) * 512],
                             start=(ic == 0), stop=(ic == IC - 1),
                             inc=("pe" if ic == IC - 1 and nbk == 3 else None))
                b.wait("vector", "pe", b.val("pe"))
                b.v("scalar_tensor_tensor", L["t1r"][:, tt * D:(tt + 1) * D],
                    W4[:], L["cvals"][:, tt * 8 + e: tt * 8 + e + 1],
                    L["t1r"][:, tt * D:(tt + 1) * D], ALU.mult, ALU.add, inc="vec")
                out_v[tt] = b.val("vec")
        pass_pe[p] = b.val("pe")

    NP = 3 * E
    emit_loads(0)
    for p in range(1, NP):
        emit_loads(p)
        emit_compute(p - 1)
    emit_compute(NP - 1)
    for tt in range(NT_OWN):
        L[f"out_tile_v_{tt}"] = out_v[tt]


def finish(b, L):
    toks = list(getattr(b, "tap_tokens", []))
    if BUILD_PHASES >= 6:
        for tt in range(NT_OWN):
            b.wait("gpsimd", "vec", L[f"out_tile_v_{tt}"])
            toks.append(b.dma("gpsimd", L["out_own"][tt * 128:(tt + 1) * 128, :],
                              L["t1r"][:, tt * D:(tt + 1) * D], inc="cast"))
    else:
        b.barrier()
        src_t = L["yq"] if (BUILD_PHASES < 3 or WO_STAGE < 3) else (
            L["t1a"] if BUILD_PHASES < 4 else L["t1r"])
        for tt in range(NT_OWN):
            toks.append(b.dma("gpsimd", L["out_own"][tt * 128:(tt + 1) * 128, :],
                              src_t[:, tt * D:(tt + 1) * D], inc="cast"))
    b.wait_dma("gpsimd", toks)
    b.wait_dma("sync", toks)



_NC_CACHE = {}

def kernel(**inputs):
    key = (BUILD_PHASES, tuple(DEBUG_TAPS))
    if key not in _NC_CACHE:
        _NC_CACHE[key] = build_nc()
    nc = _NC_CACHE[key]

    x = np.ascontiguousarray(np.asarray(inputs["L0_attn_x"]), dtype=np.float32)
    freqs = 1.0 / ROPE_THETA ** (np.arange(0, HD, 2, dtype=np.float32) / HD)
    ident = np.eye(128, dtype=np.float32)

    def tcols(w):
        return np.ascontiguousarray(np.asarray(w, np.float32).reshape(FT, 128).T)

    base = {
        "gamma_t": tcols(inputs["L0_attn_gamma"]),
        "beta_t": tcols(inputs["L0_attn_beta"]),
        "w_anorm_t": tcols(inputs["L0_ffn_attn_norm_w"]),
        "w_fnorm_t": tcols(inputs["L0_ffn_ffn_norm_w"]),
        "w_qn_t": np.ascontiguousarray(
            np.asarray(inputs["L0_ffn_q_norm_w"], np.float32).reshape(128, 1)),
        "w_kn_t": np.ascontiguousarray(
            np.asarray(inputs["L0_ffn_k_norm_w"], np.float32).reshape(128, 1)),
        "identf": ident,
        "swapf": np.ascontiguousarray(
            np.eye(128, dtype=np.float32)[:, list(range(64, 128)) + list(range(64))]),
    }
    for k in ("L0_attn_Wq", "L0_attn_Wk", "L0_attn_Wv", "L0_attn_Wg", "L0_attn_Wo",
              "L0_ffn_q_w", "L0_ffn_k_w", "L0_ffn_v_w", "L0_ffn_o_w"):
        base[k] = np.ascontiguousarray(np.asarray(inputs[k], np.float32).T)
    base["L0_ffn_gate_w"] = np.ascontiguousarray(
        np.asarray(inputs["L0_ffn_gate_w"], np.float32))
    for k in ("L0_ffn_wg", "L0_ffn_wu", "L0_ffn_wd"):
        base[k] = np.ascontiguousarray(
            np.asarray(inputs[k], np.float32).transpose(0, 2, 1))

    in_maps = []
    for core in range(8):
        bb, half = core // 2, core % 2
        pos = np.concatenate([np.arange(0, T_PRI, dtype=np.float32),
                              np.arange(half * T_OWN, half * T_OWN + T_OWN,
                                        dtype=np.float32)])
        f = np.outer(pos, freqs)
        m = dict(base)
        xa = np.concatenate([x[bb, 0:T_PRI],
                             x[bb, half * T_OWN:(half + 1) * T_OWN]], 0)
        m["xT_all"] = np.ascontiguousarray(xa.T)
        m["x_own"] = np.ascontiguousarray(x[bb, half * T_OWN:(half + 1) * T_OWN])
        cosT = np.cos(f).T.astype(np.float32)
        sinT = np.sin(f).T.astype(np.float32)
        m["cosA"] = np.ascontiguousarray(np.concatenate([cosT, cosT], 0))
        m["sinA"] = np.ascontiguousarray(np.concatenate([-sinT, sinT], 0))
        m["flag_col"] = np.full((128, 1), 1.0 if half == 1 else 0.0, np.float32)
        m["pbias_col"] = np.full((128, 1), 0.0 if half == 1 else -1e30, np.float32)
        in_maps.append(m)

    res = run_bass_kernel_spmd(nc, in_maps, core_ids=list(range(8)))
    out = np.empty((4, T_ALL, D), dtype=np.float32)
    for core in range(8):
        bb, half = core // 2, core % 2
        out[bb, half * T_OWN:(half + 1) * T_OWN] = res.results[core]["out_own"]
    kernel._last_results = res
    return out
